# version 07
# baseline (speedup 1.0000x reference)
import math
import contextlib
import numpy as np
import ml_dtypes
import concourse.bass as bass
import concourse.mybir as mybir
from concourse.bass_utils import run_bass_kernel_spmd

F32 = mybir.dt.float32
BF16 = mybir.dt.bfloat16
I32 = mybir.dt.int32
AF = mybir.ActivationFunctionType
ALU = mybir.AluOpType
AX = mybir.AxisListType
BF = ml_dtypes.bfloat16

NC = 8
S, D = 8192, 2048
NBLK, BLK = 16, 512
QL = 1024
H0, DN, DR, DV = 16, 128, 64, 128
QSZ = H0 * (DN + DR)
H1, QB1, VB1 = 32, 8, 8
HID1 = 5632
HID0 = 8192

SHARED_AG = True


def blocks_of(c):
    return (c, 15 - c)


class Ctx:
    def __init__(self, name):
        self.nc = bass.Bass("TRN2", target_bir_lowering=False, debug=False)
        self.es = contextlib.ExitStack()
        self.cnt = {}
        self.sems = {}
        self.block = None
        self.name = name
        self.prefix = ""
        self.phase_es = self.es
        self.slots = []
        self.slot_of = {}
        self.stable_slot = {}
        self.reserved = set()
        self.used_now = set()
        self.base = {}
        self._slot_snap = {}
        self._barh = None
        self._nbar = 0
        self.phase_marks = []
        self.cls_seen = set()
        self.viol = set()

    def set_phase(self, prefix):
        if self.phase_es is not self.es:
            snap = self._slot_snap
            slots = self.slots
            blk = self.block
            if self._barh is None:
                self._barh = self.es.enter_context(self.nc.semaphore("barx"))
            barh = self._barh
            self._nbar += 1
            tgt = 5 * self._nbar

            @blk.sync
            def _(sy):
                for i, s in enumerate(slots):
                    if s["cls"] != "C" and s["cnt"] != snap.get(i, 0):
                        sy.wait_ge(s["h"], s["cnt"])
            for secname in ("sync", "scalar", "vector", "tensor", "gpsimd"):
                @getattr(blk, secname)
                def _(eng):
                    eng.wait_ge(barh, 0).then_inc(barh, 1)
                    eng.wait_ge(barh, tgt)
            self.phase_es.close()
        self.prefix = prefix + ":"
        self.used_now = set(self.reserved)
        self._slot_snap = {i: s["cnt"] for i, s in enumerate(self.slots)}
        self.phase_marks.append((prefix, self.nc.next_id()))
        self.phase_es = contextlib.ExitStack()

    def end_phases(self):
        if self.phase_es is not self.es:
            self.phase_es.close()
            self.phase_es = self.es
        self.prefix = ""

    def start_block(self):
        self.block = self.es.enter_context(self.nc.Block())
        return self.block

    SWDGE_SEMS = frozenset(["ast", "dq_ld", "tern_ld", "x_ld", "xn4_ld", "zst",
                            "cast_ld0", "cast_ld1", "sl0", "sl1",
                            "x1l0", "x1l1", "x2l0", "x2l1", "x3l0", "x3l1",
                            "kvv0", "kvv1", "x1_st", "xn2_ld", "xn2_st",
                            "xn3_st", "xn4_st"])
    STABLE_SEMS = frozenset(["aga", "agk", "agw"])

    def sem(self, name):
        full = self.prefix + name
        idx = self.slot_of.get(full)
        if idx is None:
            if name in self.STABLE_SEMS:
                cls = "C"
            elif name in self.SWDGE_SEMS:
                cls = "S"
            else:
                cls = "H"
            if cls == "C" and name in self.stable_slot:
                idx = self.stable_slot[name]
            else:
                idx = next((i for i, s in enumerate(self.slots)
                            if s["cls"] == cls and i not in self.used_now), None)
                if idx is None:
                    idx = len(self.slots)
                    self.slots.append({"h": self.es.enter_context(self.nc.semaphore(f"sem{idx}")),
                                       "cnt": 0, "cls": cls})
                if cls == "C":
                    self.stable_slot[name] = idx
                    self.reserved.add(idx)
            self.slot_of[full] = idx
            self.used_now.add(idx)
            self.base[full] = 0 if name in self.STABLE_SEMS else self.slots[idx]["cnt"]
            self.cnt[full] = 0
        return self.slots[idx]["h"]

    def val(self, name):
        self.sem(name)
        return self.cnt[self.prefix + name]

    def inc(self, instr, name, n=1):
        h = self.sem(name)
        full = self.prefix + name
        if full not in self.cls_seen and name not in self.STABLE_SEMS:
            c = instr.concise()
            swdge = c.lstrip().startswith("PL") and ("DMACopy" in c or "Dma" in c)
            want = "S" if swdge else "H"
            if self.slots[self.slot_of[full]]["cls"] != want:
                raise RuntimeError(f"sem {name} misclassified: should be {want}")
            self.cls_seen.add(full)
        instr.then_inc(h, n)
        self.cnt[full] += n
        self.slots[self.slot_of[full]]["cnt"] += n
        return self.cnt[full]

    def wait(self, eng, name, v=None):
        h = self.sem(name)
        full = self.prefix + name
        eng.wait_ge(h, self.base[full] + (self.cnt[full] if v is None else v))

    def waitv(self, eng, name, v):
        h = self.sem(name)
        full = self.prefix + name
        eng.wait_ge(h, self.base[full] + v)

    def agwait(self, eng, name, v):
        if getattr(self, "ags_on", False):
            self.waitv(eng, name, v)

    def sb(self, name, shape, dtype):
        return self.phase_es.enter_context(self.nc.sbuf_tensor(self.prefix.replace(":", "_") + name, list(shape), dtype))

    def ps(self, name, shape, dtype=F32):
        return self.phase_es.enter_context(self.nc.psum_tensor(self.prefix.replace(":", "_") + name, list(shape), dtype))

    def din(self, name, shape, dtype):
        return self.nc.dram_tensor(name, list(shape), dtype, kind="ExternalInput").ap()

    def dout(self, name, shape, dtype):
        return self.nc.dram_tensor(name, list(shape), dtype, kind="ExternalOutput").ap()

    def dint(self, name, shape, dtype):
        return self.nc.dram_tensor(self.prefix.replace(":", "_") + name, list(shape), dtype, kind="Internal").ap()


def emit_pre(ctx, io, CM=None):
    castflat = io["castflat"]
    wgu_sl = io["wgu_sl"]
    wproj_sl = io["wproj_sl"]
    wo1_sl = io["wo1_sl"]
    ve0_sl = io["ve0_sl"]
    ve1_sl = io["ve1_sl"]
    wqg_sl = io["wqg_sl"]
    sg_sl = io["sg_sl"]
    wqu_sl = io["wqu_sl"]
    su_sl = io["su_sl"]
    wqd_sl = io["wqd_sl"]
    sd_sl = io["sd_sl"]

    castflat_o = io["castflat_o"]
    wgu_o = io["wgu_o"]
    wproj_o = io["wproj_o"]
    wo1_o = io["wo1_o"]
    wqg_o = io["wqg_o"]
    wqu_o = io["wqu_o"]
    wqd_o = io["wqd_o"]

    FRMAX, GMAX = 8192, 128
    CCH = 8192
    ncast = (CM + CCH - 1) // CCH

    if True:
        cbuf = [ctx.sb(f"cbuf{i}", [128, CCH], BF16) for i in range(2)]
        wb = ctx.sb("wb", [128, FRMAX], BF16)
        ob = ctx.sb("ob", [128, FRMAX], BF16)
        absb = ctx.sb("absb", [128, FRMAX], BF16)
        scl = ctx.sb("scl", [128, GMAX], F32)
        ssum = ctx.sb("ssum", [128, GMAX], F32)
        smb = ctx.sb("smb", [128, GMAX], BF16)
        rec = ctx.sb("rec", [128, GMAX], F32)
        sgn = ctx.sb("sgn", [128, FRMAX], BF16)
        rbf = ctx.sb("rbf", [128, FRMAX], BF16)
        stp = ctx.sb("stp", [128, FRMAX], BF16)
        mhalfb = ctx.sb("mhalfb", [128, 1], F32)
        dqint = ctx.sb("dqint", [128, 5632], I32)
        dq = ctx.sb("dq", [128, 5632], BF16)
        dqs = ctx.sb("dqs", [128, 1], F32)
        dqb = ctx.sb("dqb", [128, 1], F32)
        wo1b = ctx.sb("wo1b", [32, 2048], F32)
        ved = ctx.sb("ved", [32, 1], F32)
        ve1b = ctx.sb("ve1b", [32, 1], F32)
        wo1ob = ctx.sb("wo1ob", [32, 2048], BF16)
        blk = ctx.block if ctx.block is not None else ctx.start_block()

        @blk.gpsimd
        def _(g):
            for i in range(ncast):
                c0, c1 = i * CCH, min(CM, (i + 1) * CCH)
                if i >= 2:
                    ctx.waitv(g, f"cast_st{i % 2}", (i // 2) * 16)
                ctx.inc(g.dma_start(out=cbuf[i % 2][:, 0:c1 - c0], in_=castflat[:, c0:c1]), f"cast_ld{i % 2}", 16)

        @blk.sync
        def _(sy):
            for i in range(ncast):
                c0, c1 = i * CCH, min(CM, (i + 1) * CCH)
                ctx.waitv(sy, f"cast_ld{i % 2}", (i // 2 + 1) * 16)
                ctx.inc(sy.dma_start(out=castflat_o[:, c0:c1], in_=cbuf[i % 2][:, 0:c1 - c0]), f"cast_st{i % 2}", 16)

        if getattr(ctx, "ag_cast_issue", None):
            @blk.gpsimd
            def _(g):
                ctx.wait(g, "cast_st0")
                ctx.wait(g, "cast_st1")
                ctx.ag_cast_issue(g)

        @blk.scalar
        def _(sc):
            ctx.inc(sc.dma_start(out=wo1b[:, :], in_=wo1_sl[:, :]), "wo1_ld", 16)
            ctx.inc(sc.dma_start(out=ved[:, :], in_=ve0_sl[:, :]), "wo1_ld", 16)
            ctx.inc(sc.dma_start(out=ve1b[:, :], in_=ve1_sl[:, :]), "wo1_ld", 16)

        @blk.vector
        def _(v):
            ctx.inc(v.memset(mhalfb[:, :], -0.5), "mhalf", 1)

        @blk.vector
        def _(v):
            ctx.wait(v, "wo1_ld")
            ctx.inc(v.tensor_sub(out=ved[:, :], in0=ve1b[:, :], in1=ved[:, :]), "vv", 1)
            ctx.wait(v, "vv")
            ctx.inc(v.tensor_scalar(out=wo1ob[:, :], in0=wo1b[:, :], scalar1=ved[:, 0:1],
                                    scalar2=None, op0=ALU.mult), "wo1_done", 1)

        @blk.scalar
        def _(sc):
            ctx.wait(sc, "wo1_done")
            ctx.inc(sc.dma_start(out=wo1_o[:, :], in_=wo1ob[:, :]), "wo1_st", 16)

        def dequant(qin, sin, qout, nrows, ncols):
            r = 0
            while r < nrows:
                rr = min(128, nrows - r)

                @blk.gpsimd
                def _(g, r=r, rr=rr):
                    ctx.wait(g, "dq_st")
                    ctx.inc(g.dma_start(out=dqint[0:rr, 0:ncols], in_=qin[r:r + rr, 0:ncols]), "dq_ld", 16)
                    ctx.inc(g.dma_start(out=dqs[0:rr, :], in_=sin[r:r + rr, :]), "dq_ld", 16)

                @blk.vector
                def _(v, r=r, rr=rr):
                    ctx.wait(v, "dq_ld")
                    ctx.inc(v.tensor_scalar(out=dqb[0:rr, :], in0=dqs[0:rr, :], scalar1=-128.0, scalar2=None, op0=ALU.mult), "vv", 1)
                    ctx.wait(v, "vv")
                    ctx.inc(v.tensor_scalar(out=dq[0:rr, 0:ncols], in0=dqint[0:rr, 0:ncols],
                                            scalar1=dqs[0:rr, 0:1], scalar2=dqb[0:rr, 0:1],
                                            op0=ALU.mult, op1=ALU.add), "dq_done", 1)

                @blk.sync
                def _(sy, r=r, rr=rr):
                    ctx.wait(sy, "dq_done")
                    ctx.inc(sy.dma_start(out=qout[r:r + rr, 0:ncols], in_=dq[0:rr, 0:ncols]), "dq_st", 16)
                r += 128

        dequant(wqg_sl, sg_sl, wqg_o, 704, 2048)
        dequant(wqu_sl, su_sl, wqu_o, 704, 2048)
        dequant(wqd_sl, sd_sl, wqd_o, 256, 5632)

        wgu_v = wgu_sl.rearrange("(t p) d -> t p d", p=128)
        wgu_ov = wgu_o.rearrange("(t p) d -> t p d", p=128)
        jobs = [("wgu", t) for t in range(16)] + [("wproj", t) for t in range(2)]
        for kind, t in jobs:
            if kind == "wgu":
                src, dst, FRt, G = wgu_v[t, :, :], wgu_ov[t, :, :], 2048, 32
            else:
                src, dst = wproj_sl[128 * t:128 * (t + 1), :], wproj_o[128 * t:128 * (t + 1), :]
                FRt, G = 8192, 128

            @blk.gpsimd
            def _(g, src=src, FRt=FRt):
                ctx.wait(g, "tern_st")
                ctx.inc(g.dma_start(out=wb[:, 0:FRt], in_=src), "tern_ld", 16)

            @blk.scalar
            def _(sc, FRt=FRt):
                ctx.wait(sc, "tern_ld")
                ctx.wait(sc, "mhalf")
                ctx.wait(sc, "tern_v")
                ctx.inc(sc.activation(out=absb[:, 0:FRt], in_=wb[:, 0:FRt], func=AF.Abs), "tern_a", 1)

            @blk.vector
            def _(v, FRt=FRt, G=G):
                ctx.wait(v, "tern_a")
                ctx.inc(v.tensor_reduce(out=ssum[:, 0:G], in_=absb[:, 0:FRt].rearrange("p (g r) -> p g r", r=64),
                                axis=AX.X, op=ALU.add), "vv", 1)
                ctx.wait(v, "vv")
                ctx.inc(v.tensor_scalar(out=smb[:, 0:G], in0=ssum[:, 0:G], scalar1=1.0 / 64.0, scalar2=None, op0=ALU.mult), "vv", 1)
                ctx.wait(v, "vv")
                ctx.inc(v.tensor_copy(out=scl[:, 0:G], in_=smb[:, 0:G]), "vv", 1)
                ctx.wait(v, "vv")
                ctx.inc(v.tensor_scalar(out=scl[:, 0:G], in0=scl[:, 0:G], scalar1=1e-8, scalar2=None, op0=ALU.max), "vv", 1)
                ctx.wait(v, "vv")
                ctx.inc(v.reciprocal(out=rec[:, 0:G], in_=scl[:, 0:G]), "vv", 1)
                ctx.wait(v, "vv")
                for g in range(G):
                    ins = v.tensor_scalar(out=rbf[:, 64 * g:64 * (g + 1)], in0=wb[:, 64 * g:64 * (g + 1)],
                                    scalar1=rec[:, g:g + 1], scalar2=None, op0=ALU.mult)
                ctx.inc(ins, "tern_rb", 1)

            @blk.scalar
            def _(sc, FRt=FRt):
                ctx.wait(sc, "tern_rb")
                ctx.inc(sc.activation(out=absb[:, 0:FRt], in_=rbf[:, 0:FRt], func=AF.Abs), "ss", 1)
                ctx.wait(sc, "ss")
                ctx.inc(sc.activation(out=stp[:, 0:FRt], in_=absb[:, 0:FRt], func=AF.Sign, bias=mhalfb[:, 0:1], scale=1.0), "ss", 1)
                ctx.wait(sc, "ss")
                sc.activation(out=stp[:, 0:FRt], in_=stp[:, 0:FRt], func=AF.Relu)
                ctx.inc(sc.activation(out=sgn[:, 0:FRt], in_=rbf[:, 0:FRt], func=AF.Sign), "tern_s", 1)

            @blk.vector
            def _(v, FRt=FRt, G=G):
                ctx.wait(v, "tern_s")
                ctx.inc(v.tensor_mul(out=sgn[:, 0:FRt], in0=sgn[:, 0:FRt], in1=stp[:, 0:FRt]), "vv", 1)
                ctx.wait(v, "vv")
                for g in range(G):
                    ins = v.tensor_scalar(out=ob[:, 64 * g:64 * (g + 1)], in0=sgn[:, 64 * g:64 * (g + 1)],
                                    scalar1=scl[:, g:g + 1], scalar2=None, op0=ALU.mult)
                ctx.inc(ins, "tern_v", 1)

            @blk.sync
            def _(sy, dst=dst, FRt=FRt):
                ctx.wait(sy, "tern_v")
                ctx.inc(sy.dma_start(out=dst, in_=ob[:, 0:FRt]), "tern_st", 16)

        @blk.gpsimd
        def _(g):
            ctx.wait(g, "cast_st0")
            ctx.wait(g, "cast_st1")
            ctx.wait(g, "wo1_st")
            ctx.wait(g, "dq_st")
            ctx.wait(g, "tern_st")

    return


CAST_KEYS = ["L0_attn_Wq", "L0_attn_Wk", "L0_attn_Wv", "L0_attn_Wo",
             "L1_attn_Wq", "L1_attn_Wk", "L1_attn_Wv", "L1_attn_Wg", "L1_attn_Wo"]

def pre_inmaps(inputs):
    flat = np.concatenate([np.ascontiguousarray(inputs[k]).reshape(-1) for k in CAST_KEYS])
    tot = flat.size
    percore = tot // NC
    assert tot % NC == 0
    CM = percore // 128
    assert percore % 128 == 0
    wgu = inputs["L0_ffn_w_gate_up"]
    wproj = inputs["L0_ffn_w_proj"]
    wo1 = inputs["L1_attn_Wo"]
    ve0 = inputs["L1_attn_v_emb0"].reshape(256, 1)
    ve1 = inputs["L1_attn_v_emb1"].reshape(256, 1)
    maps = []
    for c in range(NC):
        maps.append({
            "castflat": np.ascontiguousarray(flat[c * percore:(c + 1) * percore].reshape(128, CM)),
            "wgu_sl": np.ascontiguousarray(wgu[2048 * c:2048 * (c + 1)]),
            "wproj_sl": np.ascontiguousarray(wproj[256 * c:256 * (c + 1)]),
            "wo1_sl": np.ascontiguousarray(wo1[32 * c:32 * (c + 1)]),
            "ve0_sl": np.ascontiguousarray(ve0[32 * c:32 * (c + 1)]),
            "ve1_sl": np.ascontiguousarray(ve1[32 * c:32 * (c + 1)]),
            "wqg_sl": np.ascontiguousarray(inputs["L1_ffn_wq_gate"][704 * c:704 * (c + 1)]),
            "sg_sl": np.ascontiguousarray(inputs["L1_ffn_s_gate"].reshape(-1, 1)[704 * c:704 * (c + 1)]),
            "wqu_sl": np.ascontiguousarray(inputs["L1_ffn_wq_up"][704 * c:704 * (c + 1)]),
            "su_sl": np.ascontiguousarray(inputs["L1_ffn_s_up"].reshape(-1, 1)[704 * c:704 * (c + 1)]),
            "wqd_sl": np.ascontiguousarray(inputs["L1_ffn_wq_down"][:, :].reshape(2048, 5632)[256 * c:256 * (c + 1)]),
            "sd_sl": np.ascontiguousarray(inputs["L1_ffn_s_down"].reshape(-1, 1)[256 * c:256 * (c + 1)]),
        })
    return maps, CM

def pre_assemble(results):
    flat = np.concatenate([r["castflat_o"].reshape(-1) for r in results])
    out = {}
    off = 0
    shapes = {"L0_attn_Wq": (3072, 2048), "L0_attn_Wk": (3072, 2048),
              "L0_attn_Wv": (2048, 2048), "L0_attn_Wo": (2048, 2048),
              "L1_attn_Wq": (2048, 256), "L1_attn_Wk": (2048, 256),
              "L1_attn_Wv": (2048, 256), "L1_attn_Wg": (2048, 2048),
              "L1_attn_Wo": (256, 2048)}
    for k in CAST_KEYS:
        n = shapes[k][0] * shapes[k][1]
        out[k + "_bf"] = flat[off:off + n].reshape(shapes[k])
        off += n
    out["wgu_hat"] = np.concatenate([r["wgu_o"] for r in results], 0)
    out["wproj_hat"] = np.concatenate([r["wproj_o"] for r in results], 0)
    out["wo1_scaled"] = np.concatenate([r["wo1_o"] for r in results], 0)
    out["wqg_bf"] = np.concatenate([r["wqg_o"] for r in results], 0)
    out["wqu_bf"] = np.concatenate([r["wqu_o"] for r in results], 0)
    out["wqd_bf"] = np.concatenate([r["wqd_o"] for r in results], 0)
    return out


def run_pre(inputs):
    maps, CM = pre_inmaps(inputs)
    ctx = build_pre(CM)
    res = run_bass_kernel_spmd(ctx.nc, maps, core_ids=list(range(NC)))
    return pre_assemble(res.results)




RSN = 1.0 / math.sqrt(DN)
RSR = 1.0 / math.sqrt(DR)

def emit_ln_featmajor(ctx, blk, xT, xn, psum4, sqbuf, nsc, mbS, ibS, stats_sb, consts, pfx, eps=1e-5):
    m, msq, tv = stats_sb["m"], stats_sb["msq"], stats_sb["t"]
    ocf, ocb, orf = consts["ones_col_f"], consts["ones_col_b"], consts["ones_row_f"]

    for d in range(16):
        @blk.vector
        def _(v, d=d):
            if d == 0:
                ctx.wait(v, pfx + "_start")
            if d >= 2:
                ctx.waitv(v, pfx + "_mmprog", d - 1)
            ctx.inc(v.tensor_mul(out=sqbuf[d % 2][:, :], in0=xT[:, 1024*d:1024*(d+1)],
                                 in1=xT[:, 1024*d:1024*(d+1)]), pfx + "_sq", 1)

        @blk.tensor
        def _(t, d=d):
            ctx.wait(t, pfx + "_sq")
            st, sp = (d == 0), (d == 15)
            for ch in range(2):
                t.matmul(psum4[ch][0:1, :], ocf[:, 0:1], xT[:, 1024*d + 512*ch:1024*d + 512*(ch+1)],
                         start=st, stop=sp)
            for ch in range(2):
                ins = t.matmul(psum4[2 + ch][0:1, :], ocb[:, 0:1], sqbuf[d % 2][:, 512*ch:512*(ch+1)],
                               start=st, stop=sp)
            ctx.inc(ins, pfx + "_mmprog", 1)

    @blk.vector
    def _(v):
        ctx.wait(v, pfx + "_mmprog")
        for ch in range(2):
            v.tensor_scalar(out=m[0:1, 512*ch:512*(ch+1)], in0=psum4[ch][0:1, :], scalar1=1.0/2048.0, scalar2=None, op0=ALU.mult)
            ctx.inc(v.tensor_scalar(out=msq[0:1, 512*ch:512*(ch+1)], in0=psum4[2+ch][0:1, :], scalar1=1.0/2048.0, scalar2=None, op0=ALU.mult), "vv", 1)
        ctx.wait(v, "vv")
        ctx.inc(v.tensor_mul(out=tv[0:1, :], in0=m[0:1, :], in1=m[0:1, :]), "vv", 1)
        ctx.wait(v, "vv")
        ctx.inc(v.tensor_sub(out=tv[0:1, :], in0=msq[0:1, :], in1=tv[0:1, :]), "vv", 1)
        ctx.wait(v, "vv")
        ctx.inc(v.tensor_scalar(out=tv[0:1, :], in0=tv[0:1, :], scalar1=eps, scalar2=None, op0=ALU.add), pfx + "_var", 1)

    @blk.scalar
    def _(sc):
        ctx.wait(sc, pfx + "_var")
        ctx.inc(sc.activation(out=tv[0:1, :], in_=tv[0:1, :], func=AF.Sqrt), pfx + "_sd", 1)

    @blk.vector
    def _(v):
        ctx.wait(v, pfx + "_sd")
        ctx.inc(v.reciprocal(out=msq[0:1, :], in_=tv[0:1, :]), pfx + "_inv", 1)

    @blk.tensor
    def _(t):
        ctx.wait(t, pfx + "_inv")
        for ch in range(2):
            t.matmul(psum4[ch][:, :], orf[0:1, :], m[0:1, 512*ch:512*(ch+1)], start=True, stop=True)
        for ch in range(2):
            ins = t.matmul(psum4[2+ch][:, :], orf[0:1, :], msq[0:1, 512*ch:512*(ch+1)], start=True, stop=True)
        ctx.inc(ins, pfx + "_bc", 1)

    @blk.vector
    def _(v):
        ctx.wait(v, pfx + "_bc")
        for ch in range(2):
            v.tensor_copy(out=mbS[:, 512*ch:512*(ch+1)], in_=psum4[ch][:, :])
            ins = v.tensor_copy(out=ibS[:, 512*ch:512*(ch+1)], in_=psum4[2+ch][:, :])
        ctx.inc(ins, pfx + "_bcev", 1)


    for d in range(16):
        @blk.vector
        def _(v, d=d):
            if d == 0:
                ctx.wait(v, pfx + "_bcev")
            ctx.inc(v.tensor_sub(out=nsc[d % 2][:, :], in0=xT[:, 1024*d:1024*(d+1)], in1=mbS[:, :]), "vv", 1)
            ctx.wait(v, "vv")
            ins = v.tensor_mul(out=xn[:, 1024*d:1024*(d+1)], in0=nsc[d % 2][:, :], in1=ibS[:, :])
            if d == 15:
                ctx.inc(ins, pfx + "_done", 1)


def emit_la(ctx, io):
    xrows = io["xrows"]
    wqbf = io["wqbf"]
    wkbf = io["wkbf"]
    wvbf = io["wvbf"]
    cosrep = io["cosrep"]
    sinrep = io["sinrep"]
    identf = io["identf"]
    identb = io["identb"]
    permf = io["permf"]

    qT_o = io["qT_o"]
    kT_o = io["kT_o"]
    vnat_o = io["vnat_o"]
    xT_o = io["xT_o"]

    if True:
        xin2 = [ctx.sb(f"xin{i}", [128, D], F32) for i in range(2)]
        xnat = [ctx.sb(f"xnat{i}", [128, D], BF16) for i in range(2)]
        sqsc = ctx.sb("sqsc", [128, D], F32)
        stat2 = [ctx.sb(f"stat{i}", [128, 8], F32) for i in range(2)]
        xn = ctx.sb("xn", [128, 16 * QL], BF16)
        xTp = [ctx.sb(f"xTp{i}", [128, QL], F32) for i in range(2)]
        ones_col_f = ctx.sb("ones_col_f", [128, 1], F32)
        sid = ctx.sb("sid", [128, 128], F32)
        sidb = ctx.sb("sidb", [128, 128], BF16)
        sperm = ctx.sb("sperm", [128, 128], F32)
        scos = ctx.sb("scos", [128, QL], F32)
        ssin = ctx.sb("ssin", [128, QL], F32)
        slab2 = [ctx.sb(f"slab{i}", [128, 256], BF16) for i in range(2)]
        qkstage = ctx.sb("qkstage", [128, 2 * QL], BF16)
        vTstage = ctx.sb("vTstage", [128, 2 * QL], BF16)
        vnstage = ctx.sb("vnstage", [128, 8 * 256], BF16)
        ropesc = ctx.sb("ropesc", [128, 2 * QL], F32)
        ropet1 = ctx.sb("ropet1", [128, 2 * QL], F32)
        pA = ctx.ps("pA", [128, QL], F32)
        pC = [ctx.ps(f"pC{i}", [128, 512], F32) for i in range(4)]
        pVT = ctx.ps("pVT", [128, QL], BF16)
        blk = ctx.block if ctx.block is not None else ctx.start_block()

        @blk.sync
        def _(sy):
            ctx.inc(sy.dma_start(out=sid[:, :], in_=identf[:, :]), "c_ld", 16)
            ctx.inc(sy.dma_start(out=sidb[:, :], in_=identb[:, :]), "c_ld", 16)
            ctx.inc(sy.dma_start(out=sperm[:, :], in_=permf[:, :]), "c_ld", 16)
            ctx.inc(sy.dma_start(out=scos[:, :], in_=cosrep[:, :]), "c_ld", 16)
            ctx.inc(sy.dma_start(out=ssin[:, :], in_=sinrep[:, :]), "c_ld", 16)
            for s in range(8):
                if s >= 2:
                    ctx.waitv(sy, "xntr_ev", (s - 1) * 2)
                ctx.inc(sy.dma_start(out=xin2[s % 2][:, :], in_=xrows[128*s:128*(s+1), :]), f"x_ld{s}", 16)

        @blk.vector
        def _(v):
            ctx.inc(v.memset(ones_col_f[:, :], 1.0), "ones", 1)

        for s in range(8):
            @blk.scalar
            def _(sc, s=s):
                ctx.waitv(sc, f"x_ld{s}", 16)
                if s >= 2:
                    ctx.waitv(sc, "ln_xn", s - 1)
                ctx.inc(sc.activation(out=sqsc[:, :], in_=xin2[s % 2][:, :], func=AF.Square,
                                      accum_out=stat2[s % 2][:, 1:2]), "ln_sq", 1)

            @blk.vector
            def _(v, s=s):
                ctx.waitv(v, f"x_ld{s}", 16)
                st = stat2[s % 2]
                ctx.inc(v.reduce_sum(out=st[:, 0:1], in_=xin2[s % 2][:, :], axis=AX.X), "vv", 1)
                ctx.wait(v, "ln_sq")
                ctx.wait(v, "vv")
                ctx.inc(v.tensor_scalar(out=st[:, 2:3], in0=st[:, 0:1], scalar1=1.0/2048.0, scalar2=None, op0=ALU.mult), "vv", 1)
                ctx.inc(v.tensor_scalar(out=st[:, 3:4], in0=st[:, 1:2], scalar1=1.0/2048.0, scalar2=None, op0=ALU.mult), "vv", 1)
                ctx.wait(v, "vv")
                ctx.inc(v.tensor_mul(out=st[:, 4:5], in0=st[:, 2:3], in1=st[:, 2:3]), "vv", 1)
                ctx.wait(v, "vv")
                ctx.inc(v.tensor_sub(out=st[:, 4:5], in0=st[:, 3:4], in1=st[:, 4:5]), "vv", 1)
                ctx.wait(v, "vv")
                ctx.inc(v.tensor_scalar(out=st[:, 4:5], in0=st[:, 4:5], scalar1=1e-5, scalar2=None, op0=ALU.add), "ln_var", 1)

            @blk.scalar
            def _(sc, s=s):
                ctx.wait(sc, "ln_var")
                ctx.inc(sc.activation(out=stat2[s % 2][:, 5:6], in_=stat2[s % 2][:, 4:5], func=AF.Sqrt), "ln_sd", 1)

            @blk.vector
            def _(v, s=s):
                ctx.wait(v, "ln_sd")
                st = stat2[s % 2]
                ctx.inc(v.reciprocal(out=st[:, 6:7], in_=st[:, 5:6]), "vv", 1)
                ctx.wait(v, "vv")
                ctx.inc(v.tensor_scalar(out=xnat[s % 2][:, :], in0=xin2[s % 2][:, :],
                                        scalar1=st[:, 2:3], scalar2=st[:, 6:7],
                                        op0=ALU.subtract, op1=ALU.mult), "ln_xn", 1)

            for b in range(2):
                @blk.tensor
                def _(t, s=s, b=b):
                    if s == 0 and b == 0:
                        ctx.wait(t, "c_ld")
                    ctx.waitv(t, f"x_ld{s}", 16)
                    ctx.wait(t, "xtr_ev")
                    for j in range(8):
                        dt = b * 8 + j
                        ins = t.transpose(pA[:, 128*j:128*(j+1)], xin2[s % 2][:, 128*dt:128*(dt+1)], sid[:, :])
                    ctx.inc(ins, "xtr_mm", 1)

                @blk.vector
                def _(v, s=s, b=b):
                    ctx.wait(v, "xtr_mm")
                    ctx.waitv(v, f"xT_st{b}", max(0, s) * 16)
                    ctx.inc(v.tensor_copy(out=xTp[b][:, :], in_=pA[:, :]), "xtr_ev", 1)

                @blk.scalar
                def _(sc, s=s, b=b):
                    ctx.wait(sc, "xtr_ev")
                    ins = sc.dma_start(
                        out=xT_o.rearrange("(t p) q -> p t q", p=128)[:, 8*b:8*b+8, 128*s:128*(s+1)],
                        in_=xTp[b][:].rearrange("p (j c) -> p j c", c=128))
                    ctx.inc(ins, f"xT_st{b}", 16)

            for b in range(2):
                @blk.tensor
                def _(t, s=s, b=b):
                    ctx.wait(t, "ln_xn")
                    ctx.wait(t, "xntr_ev")
                    for j in range(8):
                        dt = b * 8 + j
                        ins = t.transpose(pVT[:, 128*j:128*(j+1)], xnat[s % 2][:, 128*dt:128*(dt+1)], sidb[:, :])
                    ctx.inc(ins, "xntr_mm", 1)

                @blk.vector
                def _(v, s=s, b=b):
                    ctx.wait(v, "xntr_mm")
                    ins = v.tensor_copy(
                        out=xn[:].rearrange("p (t q) -> p t q", q=QL)[:, 8*b:8*b+8, 128*s:128*(s+1)],
                        in_=pVT[:].rearrange("p (j c) -> p j c", c=128))
                    ctx.inc(ins, "xntr_ev", 1)



        def slab_src(fg, d):
            if fg < 12:
                return wqbf[256*fg:256*(fg+1), 128*d:128*(d+1)]
            if fg < 24:
                return wkbf[256*(fg-12):256*(fg-11), 128*d:128*(d+1)]
            return wvbf[256*(fg-24):256*(fg-23), 128*d:128*(d+1)]

        @blk.sync
        def _(sy):
            ctx.agwait(sy, "aga", 1)
            for fg in range(32):
                for d in range(16):
                    i = fg * 16 + d
                    if i >= 2:
                        ctx.waitv(sy, "qkv_mmprog", i - 1)
                    ctx.inc(sy.dma_start(out=slab2[i % 2][:, :], in_=slab_src(fg, d), transpose=True), f"slab_ld{i % 2}", 16)

        for fg in range(32):
            @blk.tensor
            def _(t, fg=fg):
                if fg == 0:
                    ctx.wait(t, "xntr_ev")
                for d in range(16):
                    i = fg * 16 + d
                    ctx.waitv(t, f"slab_ld{i % 2}", (i // 2 + 1) * 16)
                    if d == 0:
                        ctx.wait(t, "qkv_ev")
                    for j in range(2):
                        for ch in range(2):
                            ins = t.matmul(pC[2*j + ch][:, :], slab2[i % 2][:, 128*j:128*(j+1)],
                                           xn[:, 1024*d + 512*ch:1024*d + 512*(ch+1)],
                                           start=(d == 0), stop=(d == 15))
                    ctx.inc(ins, "qkv_mmprog", 1)

            gf = 2 * fg
            if fg < 8 or (12 <= fg < 20):
                isq = fg < 8
                dsto = qT_o if isq else kT_o
                fwithin = gf if isq else gf - 24

                @blk.vector
                def _(v, isq=isq):
                    ctx.wait(v, "qkv_mmprog")
                    ctx.wait(v, "qk_st")
                    for j in range(2):
                        for ch in range(2):
                            col = j * QL + 512 * ch
                            if isq:
                                ins = v.tensor_scalar(out=qkstage[:, col:col+512], in0=pC[2*j+ch][:, :],
                                                      scalar1=RSN, scalar2=None, op0=ALU.mult)
                            else:
                                ins = v.tensor_copy(out=qkstage[:, col:col+512], in_=pC[2*j+ch][:, :])
                    ctx.inc(ins, "qkv_ev", 1)

                @blk.scalar
                def _(sc, dsto=dsto, fwithin=fwithin):
                    ctx.wait(sc, "qkv_ev")
                    ins = sc.dma_start(
                        out=dsto.rearrange("(t p) q -> p t q", p=128)[:, fwithin:fwithin+2, :],
                        in_=qkstage[:].rearrange("p (t q) -> p t q", q=QL))
                    ctx.inc(ins, "qk_st", 16)
            elif fg < 12 or (20 <= fg < 24):
                isq = fg < 12
                dsto = qT_o if isq else kT_o
                fwithin = gf if isq else gf - 24
                scl = RSR if isq else 1.0

                @blk.vector
                def _(v):
                    ctx.wait(v, "qkv_mmprog")
                    ctx.wait(v, "rope_out")
                    for j in range(2):
                        for ch in range(2):
                            ins = v.tensor_copy(out=ropesc[:, 1024*j + 512*ch:1024*j + 512*(ch+1)],
                                                in_=pC[2*j+ch][:, :])
                    ctx.inc(ins, "qkv_ev", 1)

                for j in range(2):
                    @blk.tensor
                    def _(t, j=j):
                        ctx.wait(t, "qkv_ev")
                        ctx.wait(t, "rope_yev")
                        for ch in range(2):
                            ins = t.matmul(pA[:, 512*ch:512*(ch+1)], sperm[:, :],
                                           ropesc[:, 1024*j + 512*ch:1024*j + 512*(ch+1)],
                                           start=True, stop=True)
                        ctx.inc(ins, "rope_y", 1)

                    @blk.vector
                    def _(v, j=j, scl=scl):
                        ctx.wait(v, "rope_y")
                        ctx.wait(v, "qk_st")
                        ctx.wait(v, "rope_out")
                        v.scalar_tensor_tensor(out=ropet1[:, 0:QL], in0=ropesc[:, 1024*j:1024*(j+1)],
                                               scalar=scl, in1=scos[:, :], op0=ALU.mult, op1=ALU.mult)
                        ins = v.scalar_tensor_tensor(out=ropet1[:, QL:2*QL], in0=pA[:, :],
                                                     scalar=scl, in1=ssin[:, :], op0=ALU.mult, op1=ALU.mult)
                        ctx.inc(ins, "rope_yev", 1)
                        ctx.wait(v, "rope_yev")
                        ins = v.tensor_add(out=qkstage[:, j*QL:(j+1)*QL], in0=ropet1[:, 0:QL], in1=ropet1[:, QL:2*QL])
                        ctx.inc(ins, "rope_out", 1)

                @blk.scalar
                def _(sc, dsto=dsto, fwithin=fwithin):
                    ctx.wait(sc, "rope_out")
                    ins = sc.dma_start(
                        out=dsto.rearrange("(t p) q -> p t q", p=128)[:, fwithin:fwithin+2, :],
                        in_=qkstage[:].rearrange("p (t q) -> p t q", q=QL))
                    ctx.inc(ins, "qk_st", 16)
            else:
                fv = 2 * (fg - 24)

                @blk.vector
                def _(v):
                    ctx.wait(v, "qkv_mmprog")
                    ctx.wait(v, "vT_ev")
                    for j in range(2):
                        for ch in range(2):
                            ins = v.tensor_copy(out=vTstage[:, 1024*j + 512*ch:1024*j + 512*(ch+1)],
                                                in_=pC[2*j+ch][:, :])
                    ctx.inc(ins, "qkv_ev", 1)

                for j in range(2):
                    @blk.tensor
                    def _(t, j=j):
                        ctx.wait(t, "qkv_ev")
                        ctx.wait(t, "vT_ev")
                        for s in range(8):
                            ins = t.transpose(pVT[:, 128*s:128*(s+1)],
                                              vTstage[:, 1024*j + 128*s:1024*j + 128*(s+1)], sidb[:, :])
                        ctx.inc(ins, "vT_mm", 1)

                    @blk.vector
                    def _(v, j=j):
                        ctx.wait(v, "vT_mm")
                        if j == 0:
                            ctx.wait(v, "vn_st")
                        ins = v.tensor_copy(
                            out=vnstage[:].rearrange("p (s c) -> p s c", c=256)[:, :, 128*j:128*(j+1)],
                            in_=pVT[:].rearrange("p (s c) -> p s c", c=128))
                        ctx.inc(ins, "vT_ev", 1)

                @blk.scalar
                def _(sc, fv=fv):
                    ctx.wait(sc, "vT_ev")
                    ins = sc.dma_start(
                        out=vnat_o.rearrange("(s p) d2 -> p s d2", p=128)[:, :, 128*fv:128*fv+256],
                        in_=vnstage[:].rearrange("p (s c) -> p s c", c=256))
                    ctx.inc(ins, "vn_st", 16)

        @blk.gpsimd
        def _(g):
            ctx.wait(g, "qk_st")
            ctx.wait(g, "vn_st")
            ctx.wait(g, "xT_st0")
            ctx.wait(g, "xT_st1")
    return


def rope_tables():
    inv_freq = 1.0 / (10000.0 ** (np.arange(0, DR, 2, dtype=np.float32) / DR))
    t = np.arange(S, dtype=np.float32)
    freqs = t[:, None] * inv_freq[None, :]
    return np.cos(freqs).astype(np.float32), np.sin(freqs).astype(np.float32)

def core_cos_sin(c):
    cos, sin = rope_tables()
    lo, hi = blocks_of(c)
    rows = np.concatenate([np.arange(lo*BLK, (lo+1)*BLK), np.arange(hi*BLK, (hi+1)*BLK)])
    cosT = cos[rows].T
    sinT = sin[rows].T
    crep = np.concatenate([cosT, cosT, cosT, cosT], 0)
    srep = np.concatenate([sinT, sinT, sinT, sinT], 0)
    return np.ascontiguousarray(crep), np.ascontiguousarray(srep)

def perm_matrix():
    P = np.zeros((128, 128), np.float32)
    for b in range(2):
        o = 64 * b
        for i in range(32):
            P[o + 32 + i, o + i] = -1.0
            P[o + i, o + 32 + i] = 1.0
    return P

def la_inmaps(inputs, prep):
    x = np.asarray(inputs["L0_attn_hidden_states"]).reshape(S, D)
    maps = []
    ident = np.eye(128, dtype=np.float32)
    identb = np.eye(128, dtype=np.float32).astype(BF)
    perm = perm_matrix()
    for c in range(NC):
        lo, hi = blocks_of(c)
        xr = np.concatenate([x[lo*BLK:(lo+1)*BLK], x[hi*BLK:(hi+1)*BLK]], 0)
        crep, srep = core_cos_sin(c)
        maps.append({
            "xrows": np.ascontiguousarray(xr),
            "wqbf": prep["L0_attn_Wq_bf"], "wkbf": prep["L0_attn_Wk_bf"], "wvbf": prep["L0_attn_Wv_bf"],
            "cosrep": crep, "sinrep": srep, "identf": ident, "identb": identb, "permf": perm,
        })
    return maps

def emit_lb(ctx, io):
    qT_i = io["qT_i"]
    kTfull = io["kTfull"]
    vfull = io["vfull"]
    xT_i = io["xT_i"]
    wobf = io["wobf"]

    xT1_o = io["xT1_o"]
    xn2_o = io["xn2_o"]

    NKT = 64

    if True:
        oT = ctx.sb("oT", [128, H0 * QL], BF16)
        onesb = ctx.sb("onesb", [128, 1], BF16)
        onesrow = ctx.sb("onesrow", [1, 128], F32)
        ones_col_f = ctx.sb("ones_col_f", [128, 1], F32)
        es2 = contextlib.ExitStack()
        sb2 = lambda n, s, d: es2.enter_context(ctx.nc.sbuf_tensor(ctx.prefix.replace(':', '_') + n, list(s), d))
        qn = sb2("qn", [128, H0 * QL], BF16)
        qr = sb2("qr", [64, H0 * QL], BF16)
        kn2 = [sb2(f"kn{i}", [128, NC * QL], BF16) for i in range(2)]
        kr2 = [sb2(f"kr{i}", [64, NC * QL], BF16) for i in range(2)]
        vh2 = [sb2(f"vh{i}", [128, NKT * 128], BF16) for i in range(2)]
        pp = [sb2(f"pp{i}", [128, QL], BF16) for i in range(2)]
        denr = sb2("denr", [1, QL], F32)
        dbc = sb2("dbc", [128, QL], F32)

        psA = [ctx.ps(f"psA{i}", [128, QL], F32) for i in range(2)]
        pav = ctx.ps("pav", [128, QL], F32)
        pden = [ctx.ps(f"pden{i}", [1, 512], F32) for i in range(2)]
        blk = ctx.block if ctx.block is not None else ctx.start_block()

        @blk.scalar
        def _(sc):
            ctx.inc(sc.dma_start(out=qn[:, :], in_=qT_i[0:H0*128, :].rearrange("(h p) q -> p h q", p=128)), "q_ld", 16)
            ctx.inc(sc.dma_start(out=qr[:, :], in_=qT_i[H0*128:QSZ, :].rearrange("(h r) q -> r h q", r=64)), "q_ld", 16)

        @blk.vector
        def _(v):
            v.memset(onesb[:, :], 1.0)
            v.memset(onesrow[:, :], 1.0)
            ctx.inc(v.memset(ones_col_f[:, :], 1.0), "ones", 1)

        for h in range(H0):
            @blk.sync
            def _(sy, h=h):
                if h == 0:
                    ctx.agwait(sy, "agk", 1)
                if h >= 2:
                    ctx.waitv(sy, "hdone", h - 1)
                for r in range(NC):
                    ctx.inc(sy.dma_start(out=kn2[h % 2][:, QL*r:QL*(r+1)],
                                         in_=kTfull[QSZ*r + 128*h:QSZ*r + 128*(h+1), :]), f"kvk{h % 2}", 16)
                    ctx.inc(sy.dma_start(out=kr2[h % 2][0:64, QL*r:QL*(r+1)],
                                         in_=kTfull[QSZ*r + 2048 + 64*h:QSZ*r + 2048 + 64*(h+1), :]), f"kvk{h % 2}", 16)

            @blk.gpsimd
            def _(g, h=h):
                if h == 0:
                    ctx.agwait(g, "agk", 2)
                if h >= 2:
                    ctx.waitv(g, "hdone", h - 1)
                for r in range(NC):
                    ctx.inc(g.dma_start(out=vh2[h % 2][:].rearrange("p (kt dv) -> p kt dv", dv=128)[:, 8*r:8*(r+1), :],
                                        in_=vfull[QL*r:QL*(r+1), 128*h:128*(h+1)].rearrange("(kt p) dv -> p kt dv", p=128)),
                            f"kvv{h % 2}", 16)

        for h in range(H0):
            @blk.tensor
            def _(t, h=h):
                if h == 0:
                    ctx.wait(t, "q_ld")
                    ctx.wait(t, "ones")
                ctx.waitv(t, f"kvk{h % 2}", (h // 2 + 1) * 16 * 16)
                ctx.waitv(t, f"kvv{h % 2}", (h // 2 + 1) * 8 * 16)
                if h > 0:
                    ctx.wait(t, "dbc_ev")
                    ctx.wait(t, "hdone")
                    ctx.wait(t, "dnr")
                kn, kr, vh = kn2[h % 2], kr2[h % 2], vh2[h % 2]
                xpbase = NKT * h

                def avden(t, kt):
                    st, sp = (kt == 0), (kt == NKT - 1)
                    for ch in range(2):
                        t.matmul(pav[:, 512*ch:512*(ch+1)], vh[:, 128*kt:128*(kt+1)],
                                 pp[kt % 2][:, 512*ch:512*(ch+1)], start=st, stop=sp)
                    for ch in range(2):
                        ins = t.matmul(pden[ch][0:1, :], onesb[:, 0:1],
                                       pp[kt % 2][:, 512*ch:512*(ch+1)], start=st, stop=sp)
                    ctx.inc(ins, "avd", 1)

                for kt in range(NKT):
                    if not (h == 0 and kt < 2):
                        ctx.waitv(t, "xp", xpbase + kt - 1)
                    for ch in range(2):
                        t.matmul(psA[kt % 2][:, 512*ch:512*(ch+1)], kn[:, 128*kt:128*(kt+1)],
                                 qn[:, QL*h + 512*ch:QL*h + 512*(ch+1)], start=True, stop=False)
                    for ch in range(2):
                        ins = t.matmul(psA[kt % 2][:, 512*ch:512*(ch+1)], kr[0:64, 128*kt:128*(kt+1)],
                                       qr[0:64, QL*h + 512*ch:QL*h + 512*(ch+1)], start=False, stop=True)
                    ctx.inc(ins, "scmm", 1)
                    if kt >= 1:
                        ctx.waitv(t, "xp", xpbase + kt)
                        avden(t, kt - 1)
                ctx.waitv(t, "xp", xpbase + NKT)
                avden(t, NKT - 1)

            @blk.scalar
            def _(sc, h=h):
                for kt in range(NKT):
                    xpbase = NKT * h
                    ctx.waitv(sc, "scmm", xpbase + kt + 1)
                    if not (h == 0 and kt < 2):
                        ctx.waitv(sc, "avd", xpbase + kt - 1)
                    ctx.inc(sc.activation(out=pp[kt % 2][:, :], in_=psA[kt % 2][:, :], func=AF.Exp), "xp", 1)

            @blk.vector
            def _(v, h=h):
                ctx.wait(v, "avd")
                for ch in range(2):
                    ins = v.reciprocal(out=denr[0:1, 512*ch:512*(ch+1)], in_=pden[ch][0:1, :])
                ctx.inc(ins, "dnr", 1)

            @blk.tensor
            def _(t, h=h):
                ctx.wait(t, "dnr")
                for ch in range(2):
                    ins = t.matmul(psA[0][:, 512*ch:512*(ch+1)], onesrow[0:1, :],
                                   denr[0:1, 512*ch:512*(ch+1)], start=True, stop=True)
                ctx.inc(ins, "bcmm", 1)

            @blk.vector
            def _(v, h=h):
                ctx.wait(v, "bcmm")
                ctx.inc(v.tensor_copy(out=dbc[:, :], in_=psA[0][:, :]), "dbc_ev", 1)
                ctx.wait(v, "dbc_ev")
                ctx.inc(v.tensor_mul(out=oT[:, QL*h:QL*(h+1)], in0=pav[:, :], in1=dbc[:, :]), "hdone", 1)

        es2.close()
        woslab2 = [ctx.sb(f"woslab{i}", [128, 256], BF16) for i in range(2)]
        xTd = ctx.sb("xTd", [128, 16 * QL], F32)
        xn2 = ctx.sb("xn2", [128, 16 * QL], BF16)
        sqbuf = [ctx.sb(f"sqb{i}", [128, QL], BF16) for i in range(2)]
        nsc = [ctx.sb(f"nsc{i}", [128, QL], F32) for i in range(2)]
        mbS = ctx.sb("mbS", [128, QL], F32)
        ibS = ctx.sb("ibS", [128, QL], F32)
        m_sb = ctx.sb("m_sb", [1, QL], F32)
        msq_sb = ctx.sb("msq_sb", [1, QL], F32)
        t_sb = ctx.sb("t_sb", [1, QL], F32)

        @blk.gpsimd
        def _(g):
            ctx.wait(g, "hdone")
            for dt in range(16):
                ctx.inc(g.dma_start(out=xTd[:, QL*dt:QL*(dt+1)], in_=xT_i[128*dt:128*(dt+1), :]), "x_ld", 16)

        @blk.scalar
        def _(sc):
            ctx.agwait(sc, "aga", 1)
            ctx.wait(sc, "hdone")
            for hg in range(8):
                for vt in range(16):
                    i = hg * 16 + vt
                    if i >= 2:
                        ctx.waitv(sc, "proj_mm", i - 1)
                    ctx.inc(sc.dma_start(out=woslab2[i % 2][:, :],
                                         in_=wobf[256*hg:256*(hg+1), 128*vt:128*(vt+1)], transpose=True),
                            f"woslab{i % 2}", 16)

        for hg in range(8):
            @blk.tensor
            def _(t, hg=hg):
                if hg == 0:
                    ctx.wait(t, "hdone")
                for vt in range(16):
                    i = hg * 16 + vt
                    ctx.waitv(t, f"woslab{i % 2}", (i // 2 + 1) * 16)
                    if vt == 0:
                        ctx.wait(t, "proj_ev")
                    for j in range(2):
                        for ch in range(2):
                            ins = t.matmul(psA[j][:, 512*ch:512*(ch+1)], woslab2[i % 2][:, 128*j:128*(j+1)],
                                           oT[:, QL*vt + 512*ch:QL*vt + 512*(ch+1)],
                                           start=(vt == 0), stop=(vt == 15))
                    ctx.inc(ins, "proj_mm", 1)

            @blk.vector
            def _(v, hg=hg):
                ctx.wait(v, "proj_mm")
                ctx.waitv(v, "x_ld", 256)
                for j in range(2):
                    dt = 2 * hg + j
                    ins = v.tensor_add(out=xTd[:, QL*dt:QL*(dt+1)], in0=xTd[:, QL*dt:QL*(dt+1)], in1=psA[j][:, :])
                ctx.inc(ins, "proj_ev", 1)

        @blk.gpsimd
        def _(g):
            ctx.wait(g, "proj_ev")
            ctx.inc(g.dma_start(out=xT1_o.rearrange("(t p) q -> p t q", p=128),
                                in_=xTd[:].rearrange("p (t q) -> p t q", q=QL)), "x1_st", 16)

        @blk.vector
        def _(v):
            ctx.wait(v, "proj_ev")
            ctx.inc(v.memset(nsc[0][:, 0:4], 0.0), "ln2_start", 1)

        psum4 = [psA[0][:, 0:512], psA[0][:, 512:QL], psA[1][:, 0:512], psA[1][:, 512:QL]]
        emit_ln_featmajor(ctx, blk, xTd, xn2, psum4, sqbuf, nsc, mbS, ibS,
                          dict(m=m_sb, msq=msq_sb, t=t_sb),
                          dict(ones_col_f=ones_col_f, ones_col_b=onesb, ones_row_f=onesrow),
                          "ln2")

        @blk.gpsimd
        def _(g):
            ctx.wait(g, "ln2_done")
            ctx.inc(g.dma_start(out=xn2_o.rearrange("(t p) q -> p t q", p=128),
                                in_=xn2[:].rearrange("p (t q) -> p t q", q=QL)), "xn2_st", 16)
            ctx.wait(g, "xn2_st")
            ctx.wait(g, "x1_st")
    return


def lb_inmaps(la_results, prep):
    kTfull = np.ascontiguousarray(np.concatenate([r["kT_o"].view(BF) for r in la_results], 0))
    vfull = np.ascontiguousarray(np.concatenate([r["vnat_o"].view(BF) for r in la_results], 0))
    maps = []
    for c in range(NC):
        maps.append({
            "qT_i": la_results[c]["qT_o"].view(BF),
            "kTfull": kTfull, "vfull": vfull,
            "xT_i": la_results[c]["xT_o"],
            "wobf": prep["L0_attn_Wo_bf"],
        })
    return maps

EPS32 = float(np.finfo(np.float32).eps)

def emit_lc(ctx, io):
    xn2_i = io["xn2_i"]
    xT1_i = io["xT1_i"]
    wgu = io["wgu"]
    wproj = io["wproj"]

    xT2_o = io["xT2_o"]
    xn3_o = io["xn3_o"]

    NHT = HID0 // 128
    GS = 2
    NG = NHT // GS

    if True:
        acc = ctx.sb("acc", [128, 16 * QL], F32)
        hgrp = ctx.sb("hgrp", [128, GS * QL], BF16)
        onesb = ctx.sb("onesb", [128, 1], BF16)
        onesrow = ctx.sb("onesrow", [1, 128], F32)
        ones_col_f = ctx.sb("ones_col_f", [128, 1], F32)
        sg = ctx.sb("sg", [128, QL], BF16)
        sqt = ctx.sb("sqt", [128, QL], BF16)
        ibS = ctx.sb("ibS", [128, QL], F32)
        mbS = ctx.sb("mbS", [128, QL], F32)
        m_sb = ctx.sb("m_sb", [1, QL], F32)
        msq_sb = ctx.sb("msq_sb", [1, QL], F32)
        t_sb = ctx.sb("t_sb", [1, QL], F32)
        es2 = contextlib.ExitStack()
        sb2 = lambda n, s, d: es2.enter_context(ctx.nc.sbuf_tensor(ctx.prefix.replace(':', '_') + n, list(s), d))
        xn2 = sb2("xn2", [128, 16 * QL], BF16)
        gslab = [sb2(f"gslab{i}", [128, 16 * 256], BF16) for i in range(2)]
        uslab = [sb2(f"uslab{i}", [128, 16 * 256], BF16) for i in range(2)]
        pslab = [sb2(f"pslab{i}", [128, GS * D], BF16) for i in range(2)]

        pG = ctx.ps("pG", [128, QL], F32)
        pU = ctx.ps("pU", [128, QL], F32)
        pP = ctx.ps("pP", [128, QL], F32)
        pS = [ctx.ps(f"pS{i}", [1, 512], F32) for i in range(2)]
        blk = ctx.block if ctx.block is not None else ctx.start_block()

        @blk.vector
        def _(v):
            v.memset(onesb[:, :], 1.0)
            v.memset(onesrow[:, :], 1.0)
            ctx.inc(v.memset(ones_col_f[:, :], 1.0), "ones", 1)

        @blk.gpsimd
        def _(g):
            ctx.inc(g.dma_start(out=xn2[:].rearrange("p (t q) -> p t q", q=QL),
                                in_=xn2_i.rearrange("(t p) q -> p t q", p=128)), "xn2_ld", 16)

        @blk.sync
        def _(sy):
            ctx.agwait(sy, "agw", 2)
            for g in range(NG):
                if g >= 2:
                    ctx.waitv(sy, "up_mm", (g - 1) * GS)
                for d in range(16):
                    ctx.inc(sy.dma_start(out=gslab[g % 2][:, 256*d:256*(d+1)],
                                         in_=wgu[256*g:256*(g+1), 128*d:128*(d+1)], transpose=True), f"gsl{g % 2}", 16)
                    ctx.inc(sy.dma_start(out=uslab[g % 2][:, 256*d:256*(d+1)],
                                         in_=wgu[HID0 + 256*g:HID0 + 256*(g+1), 128*d:128*(d+1)], transpose=True), f"usl{g % 2}", 16)
                if g >= 2:
                    ctx.waitv(sy, "proj_mm", (g - 1) * 32)
                for htj in range(GS):
                    ctx.inc(sy.dma_start(out=pslab[g % 2][:, D*htj:D*(htj+1)],
                                         in_=wproj[:, 256*g + 128*htj:256*g + 128*(htj+1)], transpose=True), f"psl{g % 2}", 16)

        for g in range(NG):
            for htj in range(GS):
                ht = GS * g + htj

                @blk.tensor
                def _(t, g=g, htj=htj, ht=ht):
                    if ht == 0:
                        ctx.wait(t, "xn2_ld")
                        ctx.wait(t, "ones")
                    if htj == 0:
                        ctx.waitv(t, f"gsl{g % 2}", (g // 2 + 1) * 16 * 16)
                        ctx.waitv(t, f"usl{g % 2}", (g // 2 + 1) * 16 * 16)
                    ctx.wait(t, "silu")
                    for d in range(16):
                        for ch in range(2):
                            ins = t.matmul(pG[:, 512*ch:512*(ch+1)],
                                           gslab[g % 2][:, 256*d + 128*htj:256*d + 128*(htj+1)],
                                           xn2[:, 1024*d + 512*ch:1024*d + 512*(ch+1)],
                                           start=(d == 0), stop=(d == 15))
                    ctx.inc(ins, "gate_mm", 1)
                    ctx.wait(t, "hmul")
                    for d in range(16):
                        for ch in range(2):
                            ins = t.matmul(pU[:, 512*ch:512*(ch+1)],
                                           uslab[g % 2][:, 256*d + 128*htj:256*d + 128*(htj+1)],
                                           xn2[:, 1024*d + 512*ch:1024*d + 512*(ch+1)],
                                           start=(d == 0), stop=(d == 15))
                    ctx.inc(ins, "up_mm", 1)

                @blk.scalar
                def _(sc):
                    ctx.wait(sc, "gate_mm")
                    ctx.wait(sc, "hmul")
                    ctx.inc(sc.activation(out=sg[:, :], in_=pG[:, :], func=AF.Silu), "silu", 1)

                @blk.vector
                def _(v, htj=htj, ht=ht):
                    ctx.wait(v, "up_mm")
                    ctx.wait(v, "silu")
                    ins = v.tensor_mul(out=hgrp[:, QL*htj:QL*(htj+1)], in0=pU[:, :], in1=sg[:, :])
                    ctx.inc(ins, "hmul", 1)
                    ctx.wait(v, "hmul")
                    ctx.inc(v.tensor_mul(out=sqt[:, :], in0=hgrp[:, QL*htj:QL*(htj+1)],
                                         in1=hgrp[:, QL*htj:QL*(htj+1)]), "hsq", 1)

                @blk.tensor
                def _(t, ht=ht):
                    ctx.wait(t, "hsq")
                    st, sp = (ht == 0), (ht == NHT - 1)
                    for ch in range(2):
                        ins = t.matmul(pS[ch][0:1, :], onesb[:, 0:1], sqt[:, 512*ch:512*(ch+1)],
                                       start=st, stop=sp)
                    ctx.inc(ins, "stat_mm", 1)

            @blk.tensor
            def _(t, g=g):
                ctx.waitv(t, f"psl{g % 2}", (g // 2 + 1) * GS * 16)
                ctx.wait(t, "hmul")
                for dt in range(16):
                    for ch in range(2):
                        i = g * 32 + dt * 2 + ch
                        if i >= 2:
                            ctx.waitv(t, "acc_add", i - 1)
                        for htj in range(GS):
                            ins = t.matmul(pP[:, 512*(i % 2):512*(i % 2 + 1)],
                                           pslab[g % 2][:, D*htj + 128*dt:D*htj + 128*(dt+1)],
                                           hgrp[:, QL*htj + 512*ch:QL*htj + 512*(ch+1)],
                                           start=(htj == 0), stop=(htj == GS - 1))
                        ctx.inc(ins, "proj_mm", 1)

            @blk.vector
            def _(v, g=g):
                for dt in range(16):
                    for ch in range(2):
                        i = g * 32 + dt * 2 + ch
                        ctx.waitv(v, "proj_mm", i + 1)
                        if g == 0:
                            ins = v.tensor_copy(out=acc[:, QL*dt + 512*ch:QL*dt + 512*(ch+1)],
                                                in_=pP[:, 512*(i % 2):512*(i % 2 + 1)])
                        else:
                            ins = v.tensor_add(out=acc[:, QL*dt + 512*ch:QL*dt + 512*(ch+1)],
                                               in0=acc[:, QL*dt + 512*ch:QL*dt + 512*(ch+1)],
                                               in1=pP[:, 512*(i % 2):512*(i % 2 + 1)])
                        ctx.inc(ins, "acc_add", 1)

        es2.close()
        xn3 = ctx.sb("xn3", [128, 16 * QL], BF16)
        sqbuf = [ctx.sb(f"sqb{i}", [128, QL], BF16) for i in range(2)]
        nsc = [ctx.sb(f"nsc{i}", [128, QL], F32) for i in range(2)]
        x1d = [ctx.sb(f"x1d{i}", [128, QL], F32) for i in range(2)]

        @blk.vector
        def _(v):
            ctx.wait(v, "stat_mm")
            for ch in range(2):
                ins = v.tensor_scalar(out=m_sb[0:1, 512*ch:512*(ch+1)], in0=pS[ch][0:1, :],
                                      scalar1=1.0 / HID0, scalar2=EPS32, op0=ALU.mult, op1=ALU.add)
            ctx.inc(ins, "rms_m", 1)

        @blk.scalar
        def _(sc):
            ctx.wait(sc, "rms_m")
            ctx.inc(sc.activation(out=t_sb[0:1, :], in_=m_sb[0:1, :], func=AF.Sqrt), "rms_sd", 1)

        @blk.vector
        def _(v):
            ctx.wait(v, "rms_sd")
            ctx.inc(v.reciprocal(out=msq_sb[0:1, :], in_=t_sb[0:1, :]), "rms_inv", 1)

        @blk.tensor
        def _(t):
            ctx.wait(t, "rms_inv")
            ctx.wait(t, "acc_add")
            for ch in range(2):
                ins = t.matmul(pG[:, 512*ch:512*(ch+1)], onesrow[0:1, :],
                               msq_sb[0:1, 512*ch:512*(ch+1)], start=True, stop=True)
            ctx.inc(ins, "rms_bc", 1)

        @blk.vector
        def _(v):
            ctx.wait(v, "rms_bc")
            ctx.inc(v.tensor_copy(out=ibS[:, :], in_=pG[:, :]), "rms_bcev", 1)

        @blk.gpsimd
        def _(g):
            ctx.wait(g, "acc_add")
            for dt in range(16):
                if dt >= 2:
                    ctx.waitv(g, "resid", dt - 1)
                ctx.inc(g.dma_start(out=x1d[dt % 2][:, :], in_=xT1_i[128*dt:128*(dt+1), :]), f"x1l{dt % 2}", 16)

        for dt in range(16):
            @blk.vector
            def _(v, dt=dt):
                if dt == 0:
                    ctx.wait(v, "rms_bcev")
                ctx.waitv(v, f"x1l{dt % 2}", (dt // 2 + 1) * 16)
                ctx.inc(v.tensor_mul(out=nsc[dt % 2][:, :], in0=acc[:, QL*dt:QL*(dt+1)], in1=ibS[:, :]), "vv", 1)
                ctx.wait(v, "vv")
                ctx.inc(v.tensor_add(out=acc[:, QL*dt:QL*(dt+1)], in0=nsc[dt % 2][:, :], in1=x1d[dt % 2][:, :]),
                        "resid", 1)

        @blk.scalar
        def _(sc):
            ctx.wait(sc, "resid")
            ctx.inc(sc.dma_start(out=xT2_o.rearrange("(t p) q -> p t q", p=128),
                                 in_=acc[:].rearrange("p (t q) -> p t q", q=QL)), "x2_st", 16)

        @blk.vector
        def _(v):
            ctx.wait(v, "resid")
            ctx.inc(v.memset(nsc[0][:, 0:4], 0.0), "ln3_start", 1)

        psum4 = [pG[:, 0:512], pG[:, 512:QL], pU[:, 0:512], pU[:, 512:QL]]
        emit_ln_featmajor(ctx, blk, acc, xn3, psum4, sqbuf, nsc, mbS, ibS,
                          dict(m=m_sb, msq=msq_sb, t=t_sb),
                          dict(ones_col_f=ones_col_f, ones_col_b=onesb, ones_row_f=onesrow),
                          "ln3")

        @blk.gpsimd
        def _(g):
            ctx.wait(g, "ln3_done")
            ctx.inc(g.dma_start(out=xn3_o.rearrange("(t p) q -> p t q", p=128),
                                in_=xn3[:].rearrange("p (t q) -> p t q", q=QL)), "xn3_st", 16)
            ctx.wait(g, "xn3_st")
            ctx.wait(g, "x2_st")
    return


def lc_inmaps(lb_results, prep):
    return [{
        "xn2_i": lb_results[c]["xn2_o"].view(BF),
        "xT1_i": lb_results[c]["xT1_o"],
        "wgu": prep["wgu_hat"],
        "wproj": prep["wproj_hat"],
    } for c in range(NC)]

def emit_ld(ctx, io):
    xn3_i = io["xn3_i"]
    wq1 = io["wq1"]
    wk1 = io["wk1"]
    wv1 = io["wv1"]
    selc = io["selc"]

    pqext_o = io["pqext_o"]
    pkext_o = io["pkext_o"]
    pvext_o = io["pvext_o"]

    if True:
        xn3 = ctx.sb("xn3", [128, 16 * QL], BF16)
        wq1s = ctx.sb("wq1s", [128, 16 * 256], BF16)
        wk1s = ctx.sb("wk1s", [128, 16 * 256], BF16)
        wv1s = ctx.sb("wv1s", [128, 16 * 256], BF16)
        selb = ctx.sb("selb", [128, 16], BF16)
        pq = ctx.sb("pq", [128, 2 * QL], BF16)
        pk2 = ctx.sb("pk2", [128, 2 * QL], BF16)
        rk = ctx.sb("rk", [16, 2 * QL], BF16)
        pvext = ctx.sb("pvext", [128, 8 * 288], BF16)
        zb = ctx.sb("zb", [128, 8 * 512], BF16)
        m1 = ctx.sb("m1", [32, QL], BF16)
        pQ = ctx.ps("pQ", [128, QL], F32)
        pRK = [ctx.ps(f"pRK{i}", [16, 512], F32) for i in range(2)]
        pV = ctx.ps("pV", [128, 256], F32)
        blk = ctx.block if ctx.block is not None else ctx.start_block()

        @blk.sync
        def _(sy):
            ctx.agwait(sy, "aga", 1)
            ctx.inc(sy.dma_start(out=xn3[:].rearrange("p (t q) -> p t q", q=QL),
                                 in_=xn3_i.rearrange("(t p) q -> p t q", p=128)), "ld0", 16)
            ctx.inc(sy.dma_start(out=wq1s[:].rearrange("p (t f) -> p t f", f=256),
                                 in_=wq1.rearrange("(t p) f -> p t f", p=128)), "ld0", 16)
            ctx.inc(sy.dma_start(out=wk1s[:].rearrange("p (t f) -> p t f", f=256),
                                 in_=wk1.rearrange("(t p) f -> p t f", p=128)), "ld0", 16)
            ctx.inc(sy.dma_start(out=wv1s[:].rearrange("p (t f) -> p t f", f=256),
                                 in_=wv1.rearrange("(t p) f -> p t f", p=128)), "ld0", 16)
            ctx.inc(sy.dma_start(out=selb[:, :], in_=selc[:, :]), "ld0", 16)

        @blk.vector
        def _(v):
            ctx.inc(v.memset(zb[:, :], 0.0), "csts", 1)
            ctx.inc(v.memset(m1[:, :], -1.0), "csts", 1)
            ctx.inc(v.memset(pvext[:].rearrange("p (s h n) -> p s h n", h=32, n=9)[:, :, :, 8:9], 1.0), "csts", 1)

        for mat, wst, name in [("q", None, "pq"), ("k", None, "pk")]:
            ws = wq1s if mat == "q" else wk1s
            for f in range(2):
                @blk.tensor
                def _(t, ws=ws, f=f, mat=mat):
                    if mat == "q" and f == 0:
                        ctx.wait(t, "ld0")
                        ctx.wait(t, "csts")
                    ctx.wait(t, "sig")
                    for d in range(16):
                        for ch in range(2):
                            ins = t.matmul(pQ[:, 512*ch:512*(ch+1)], ws[:, 256*d + 128*f:256*d + 128*(f+1)],
                                           xn3[:, 1024*d + 512*ch:1024*d + 512*(ch+1)],
                                           start=(d == 0), stop=(d == 15))
                    ctx.inc(ins, "qk_mm", 1)

                @blk.scalar
                def _(sc, f=f, mat=mat):
                    ctx.wait(sc, "qk_mm")
                    if mat == "q":
                        ins = sc.activation(out=pq[:, QL*f:QL*(f+1)], in_=pQ[:, :], func=AF.Sigmoid)
                        ctx.inc(ins, "sig", 1)
                    else:
                        ins = sc.activation(out=pk2[:, QL*f:QL*(f+1)], in_=pQ[:, :], func=AF.Sigmoid)
                        ctx.inc(ins, "sig", 1)

                if mat == "k":
                    @blk.tensor
                    def _(t, f=f):
                        ctx.wait(t, "sig")
                        if f == 1:
                            ctx.wait(t, "rk_ev")
                        for ch in range(2):
                            ins = t.matmul(pRK[ch][0:16, :], selb[:, :],
                                           pk2[:, QL*f + 512*ch:QL*f + 512*(ch+1)], start=True, stop=True)
                        ctx.inc(ins, "rk_mm", 1)

                    @blk.vector
                    def _(v, f=f):
                        ctx.wait(v, "rk_mm")
                        for ch in range(2):
                            ins = v.tensor_copy(out=rk[0:16, QL*f + 512*ch:QL*f + 512*(ch+1)], in_=pRK[ch][0:16, :])
                            ctx.inc(ins, "rk_ev", 1)
                        ctx.wait(v, "rk_ev")
                        ins = v.tensor_scalar(out=pk2[:, QL*f:QL*(f+1)], in0=pk2[:, QL*f:QL*(f+1)],
                                              scalar1=2.0, scalar2=None, op0=ALU.mult)
                        ctx.inc(ins, "pkscaled", 1)

        for s in range(8):
            @blk.tensor
            def _(t, s=s):
                if s == 0:
                    ctx.wait(t, "ld0")
                ctx.wait(t, "pv_ev")
                for d in range(16):
                    ins = t.matmul(pV[:, :], xn3[:, 1024*d + 128*s:1024*d + 128*(s+1)],
                                   wv1s[:, 256*d:256*(d+1)], start=(d == 0), stop=(d == 15))
                ctx.inc(ins, "pv_mm", 1)

            @blk.scalar
            def _(sc, s=s):
                ctx.wait(sc, "pv_mm")
                ctx.wait(sc, "csts")
                ins = sc.activation(
                    out=pvext[:].rearrange("p (s2 h n) -> p s2 h n", h=32, n=9)[:, s, :, 0:8],
                    in_=pV[:].rearrange("p (h n) -> p h n", n=8), func=AF.Sigmoid)
                ctx.inc(ins, "pv_ev", 1)

        @blk.gpsimd
        def _(g):
            ctx.wait(g, "csts")
            for b in range(16):
                ctx.inc(g.dma_start(out=pqext_o[128*b:128*(b+1), :], in_=zb[:, 0:512]), "zst", 16)
            for hf in range(2):
                for b in range(8):
                    ctx.inc(g.dma_start(out=pkext_o[QL*hf + 128*b:QL*hf + 128*(b+1), :], in_=zb[:, 0:512]), "zst", 16)
            ctx.wait(g, "zst")
            ctx.wait(g, "sig")
            for qc in range(2):
                for h in range(32):
                    f, hh = h // 16, h % 16
                    ctx.inc(g.dma_start(out=pqext_o[QL*qc + 32*h:QL*qc + 32*h+8, :],
                                        in_=pq[8*hh:8*hh+8, QL*f + 512*qc:QL*f + 512*(qc+1)]), "ast", 16)
                ctx.inc(g.dma_start(out=pqext_o[QL*qc:QL*(qc+1), :].rearrange("(h r) k -> h r k", r=32)[:, 8:9, :],
                                    in_=m1[0:32, 0:512].unsqueeze(1)), "ast", 16)
            ctx.wait(g, "pkscaled")
            for hf in range(2):
                for h in range(32):
                    f, hh = h // 16, h % 16
                    ctx.inc(g.dma_start(out=pkext_o[QL*hf + 32*h:QL*hf + 32*h+8, :],
                                        in_=pk2[8*hh:8*hh+8, QL*f + 512*hf:QL*f + 512*(hf+1)]), "ast", 16)
                for f in range(2):
                    ctx.inc(g.dma_start(
                        out=pkext_o[QL*hf:QL*(hf+1), :].rearrange("(h r) k -> h r k", r=32)[16*f:16*(f+1), 8:9, :],
                        in_=rk[0:16, QL*f + 512*hf:QL*f + 512*(hf+1)].unsqueeze(1)), "ast", 16)
            ctx.wait(g, "pv_ev")
            ctx.inc(g.dma_start(out=pvext_o.rearrange("(s p) e -> p s e", p=128),
                                in_=pvext[:].rearrange("p (s e) -> p s e", e=288)), "ast", 16)
            ctx.wait(g, "ast")
    return


def sel_const():
    s = np.zeros((128, 16), dtype=BF)
    for hh in range(16):
        s[8*hh:8*(hh+1), hh] = 1.0
    return s

def ld_inmaps(lc_results, prep):
    sc = sel_const()
    return [{
        "xn3_i": lc_results[c]["xn3_o"].view(BF),
        "wq1": prep["L1_attn_Wq_bf"], "wk1": prep["L1_attn_Wk_bf"], "wv1": prep["L1_attn_Wv_bf"],
        "selc": sc,
    } for c in range(NC)]

def emit_le(ctx, io):
    pqext_i = io["pqext_i"]
    pkext_own = io["pkext_own"]
    pvext_own = io["pvext_own"]
    pkext_all = io["pkext_all"]
    pvext_all = io["pvext_all"]
    idxs_i = io["idxs_i"]
    mtab_i = io["mtab_i"]
    trirep_i = io["trirep_i"]
    econst_i = io["econst_i"]
    wo1s_i = io["wo1s_i"]
    wo1p_i = io["wo1p_i"]
    ve0_i = io["ve0_i"]
    wg_i = io["wg_i"]
    xn3_i = io["xn3_i"]
    xT2_i = io["xT2_i"]

    xT3_o = io["xT3_o"]
    xn4_o = io["xn4_o"]

    numdram = ctx.dint("numdram", [2 * 256, 512], F32)
    dendram = ctx.dint("dendram", [2 * 32, 512], F32)

    if True:
        x3 = ctx.sb("x3", [128, 16 * QL], F32)
        oT = ctx.sb("oT", [128, 4 * 512], BF16)
        wo1s = ctx.sb("wo1s", [128, 2 * D], BF16)
        biasS = ctx.sb("biasS", [128, 16], F32)
        xn3 = ctx.sb("xn3", [128, 16 * QL], BF16)
        onesb = ctx.sb("onesb", [128, 1], BF16)
        onesrow = ctx.sb("onesrow", [1, 128], F32)
        ones_col_f = ctx.sb("ones_col_f", [128, 1], F32)
        esA = contextlib.ExitStack()
        sbA = lambda n, s, d: esA.enter_context(ctx.nc.sbuf_tensor(ctx.prefix.replace(':', '_') + n, list(s), d))
        wo1p = sbA("wo1p", [128, 2 * D], BF16)
        ve0s = sbA("ve0s", [128, 2], BF16)
        acc = [sbA(f"acc{i}", [128, 8 * 512], F32) for i in range(2)]
        pkslot = [sbA(f"pkslot{i}", [128, 8 * 512], BF16) for i in range(2)]
        pqslot = [sbA(f"pqslot{i}", [128, 8 * 512], BF16) for i in range(2)]
        pvslot = [sbA(f"pvslot{i}", [128, 4 * 288 + 32], BF16) for i in range(2)]
        ppt = sbA("ppt", [128, 4 * 2048], BF16)
        sidx = sbA("sidx", [128, 15 * 20], I32)
        mtab = sbA("mtab", [128, 15 * 2], F32)
        trirep = sbA("trirep", [128, 4 * 512], BF16)

        psch = [ctx.ps(f"psch{i}", [128, 1024], F32) for i in range(2)]
        pAV = [ctx.ps(f"pAV{i}", [128, 512], F32) for i in range(2)]
        pW = ctx.ps("pW", [128, QL], F32)
        blk = ctx.block if ctx.block is not None else ctx.start_block()

        @blk.sync
        def _(sy):
            ctx.agwait(sy, "aga", 1)
            ctx.agwait(sy, "agw", 3)
            ctx.inc(sy.dma_start(out=sidx[:, :], in_=idxs_i[:, :]), "c_ld", 16)
            ctx.inc(sy.dma_start(out=mtab[:, :], in_=mtab_i[:, :]), "c_ld", 16)
            ctx.inc(sy.dma_start(out=trirep[:].rearrange("p (s q) -> p s q", q=512),
                                 in_=trirep_i[:, 0:512].rearrange("(s p) q -> p s q", p=128)), "c_ld", 16)
            ctx.inc(sy.dma_start(out=xn3[:].rearrange("p (t q) -> p t q", q=QL),
                                 in_=xn3_i.rearrange("(t p) q -> p t q", p=128)), "c_ld", 16)
            ctx.inc(sy.dma_start(out=wo1s[:].rearrange("p (t f) -> p t f", f=D),
                                 in_=wo1s_i.rearrange("(t p) f -> p t f", p=128)), "c_ld", 16)
            ctx.inc(sy.dma_start(out=wo1p[:].rearrange("p (t f) -> p t f", f=D),
                                 in_=wo1p_i.rearrange("(t p) f -> p t f", p=128)), "c_ld", 16)
            ctx.inc(sy.dma_start(out=ve0s[:, :], in_=ve0_i[:, :]), "c_ld", 16)

        @blk.vector
        def _(v):
            v.memset(onesb[:, :], 1.0)
            v.memset(onesrow[:, :], 1.0)
            v.memset(ones_col_f[:, :], 1.0)
            v.memset(acc[0][:, :], 0.0)
            v.memset(pvslot[0][:, 1152:1184], 0.0)
            v.memset(pvslot[1][:, 1152:1184], 0.0)
            ctx.inc(v.memset(acc[1][:, :], 0.0), "acc0", 1)

        @blk.tensor
        def _(t):
            ctx.wait(t, "c_ld")
            for ht in range(16):
                for vt in range(2):
                    ins = t.matmul(pW[:, ht:ht+1], wo1p[:, D*vt + 128*ht:D*vt + 128*(ht+1)],
                                   ve0s[:, vt:vt+1], start=(vt == 0), stop=(vt == 1))
            ctx.inc(ins, "bias_mm", 1)

        @blk.vector
        def _(v):
            ctx.wait(v, "bias_mm")
            ctx.inc(v.tensor_copy(out=biasS[:, :], in_=pW[:, 0:16]), "bias_ev", 1)

        from concourse.bass_core_rust import IndirectOffsetOnAxis
        eidx = [0]
        aidx = [0]

        def slot_loads(t):
            @blk.gpsimd
            def _(g, t=t):
                if t == 0:
                    ctx.waitv(g, "c_ld", 112)
                if t == 2:
                    ctx.agwait(g, "agk", 4)
                if t >= 2:
                    ctx.waitv(g, "avadd", max(0, (t - 1) * 8))
                if t < 2:
                    hf = t
                    ctx.inc(g.dma_start(out=pkslot[t % 2][:].rearrange("p (a k) -> p a k", k=512),
                                        in_=pkext_own[QL*hf:QL*(hf+1), :].rearrange("(a p) k -> p a k", p=128)),
                            f"sl{t % 2}", 16)
                    ctx.inc(g.dma_start(out=pqslot[t % 2][:].rearrange("p (a k) -> p a k", k=512),
                                        in_=pqext_i[QL*hf:QL*(hf+1), :].rearrange("(a p) k -> p a k", p=128)),
                            f"sl{t % 2}", 16)
                    ctx.inc(g.dma_start(out=pvslot[t % 2][:, 0:1152].rearrange("p (a e) -> p a e", e=288),
                                        in_=pvext_own[512*hf:512*(hf+1), :].rearrange("(a p) e -> p a e", p=128)),
                            f"sl{t % 2}", 16)
                else:
                    ts = t - 2
                    for j in range(8):
                        ctx.inc(g.indirect_dma_start(
                            out=pkslot[t % 2][:, 512*j:512*(j+1)], out_offset=None, in_=pkext_all[:, :],
                            in_offset=IndirectOffsetOnAxis(sidx[:, 20*ts + j:20*ts + j + 1], 0)), f"sl{t % 2}", 16)
                    for j in range(8):
                        ctx.inc(g.indirect_dma_start(
                            out=pqslot[t % 2][:, 512*j:512*(j+1)], out_offset=None, in_=pqext_i[:, :],
                            in_offset=IndirectOffsetOnAxis(sidx[:, 20*ts + 8 + j:20*ts + 9 + j], 0)), f"sl{t % 2}", 16)
                    for j in range(4):
                        ctx.inc(g.indirect_dma_start(
                            out=pvslot[t % 2][:, 288*j:288*(j+1)], out_offset=None, in_=pvext_all[:, :],
                            in_offset=IndirectOffsetOnAxis(sidx[:, 20*ts + 16 + j:20*ts + 17 + j], 0)), f"sl{t % 2}", 16)

        def slot_compute(t):
            ldtotal = ctx.val(f"sl{t % 2}")
            for g in range(8):
                gidx = aidx[0] // 8 * 8 + g
                gi = t * 8 + g

                @blk.tensor
                def _(te, t=t, g=g, gi=gi, ldtotal=ldtotal):
                    if g == 0:
                        ctx.waitv(te, f"sl{t % 2}", ldtotal)
                        if t <= 1:
                            ctx.wait(te, "acc0")
                        if t == 0:
                            ctx.wait(te, "bias_ev")
                    for ks in range(4):
                        for hh in range(2):
                            ctx.waitv(te, "exp", max(0, eidx[0] + 8*g + 2*ks + hh - 1))
                            for j in (2*hh, 2*hh + 1):
                                ins = te.matmul(psch[hh][:, 512*(j - 2*hh):512*(j - 2*hh) + 512],
                                                pkslot[t % 2][32*j:32*(j+1), 512*g + 128*ks:512*g + 128*(ks+1)],
                                                pqslot[t % 2][32*j:32*(j+1), 512*g:512*(g+1)],
                                                start=True, stop=True, tile_position=(32*j, 0))
                            ctx.inc(ins, "sc_mm", 1)
                    ctx.waitv(te, "exp", eidx[0] + 8 * g + 8)
                    if gi >= 2:
                        ctx.waitv(te, "avadd", gi - 1)
                    for j in range(4):
                        for ks in range(4):
                            off = 288*ks + 36*g + 9*j
                            ins = te.matmul(pAV[gi % 2][32*j:32*(j+1), :],
                                            pvslot[t % 2][:, off:off+32],
                                            ppt[:, 2048*ks + 512*j:2048*ks + 512*(j+1)],
                                            start=(ks == 0), stop=(ks == 3),
                                            tile_position=(0, 32*j))
                    ctx.inc(ins, "av_grp", 1)

                @blk.scalar
                def _(sc, t=t, g=g, gi=gi):
                    for ks in range(4):
                        for hh in range(2):
                            e = eidx[0] + 8 * g + 2 * ks + hh
                            ctx.waitv(sc, "sc_mm", e + 1)
                            if gi >= 1 and ks == 0 and hh == 0:
                                ctx.waitv(sc, "av_grp", gi)
                            dst = ppt[:, 2048*ks + 1024*hh:2048*ks + 1024*(hh+1)]
                            ctx.inc(sc.activation(out=dst, in_=psch[hh][:, :], func=AF.Exp),
                                    "exp_pre" if t < 2 else "exp", 1)

                if t < 2:
                    @blk.vector
                    def _(v, t=t, g=g):
                        for ks in range(4):
                            for hh in range(2):
                                e = eidx[0] + 8 * g + 2 * ks + hh
                                ctx.waitv(v, "exp_pre", e + 1)
                                for j in (2*hh, 2*hh + 1):
                                    ins = v.tensor_mul(out=ppt[:, 2048*ks + 512*j:2048*ks + 512*(j+1)],
                                                       in0=ppt[:, 2048*ks + 512*j:2048*ks + 512*(j+1)],
                                                       in1=trirep[:, 512*ks:512*(ks+1)])
                                ctx.inc(ins, "exp", 1)

                @blk.vector
                def _(v, t=t, g=g, gi=gi):
                    ctx.waitv(v, "av_grp", gi + 1)
                    if t == 0:
                        ins = v.tensor_add(out=acc[0][:, 512*g:512*(g+1)], in0=acc[0][:, 512*g:512*(g+1)],
                                           in1=pAV[gi % 2][:, :])
                    elif t == 1:
                        ins = v.tensor_add(out=acc[1][:, 512*g:512*(g+1)], in0=acc[1][:, 512*g:512*(g+1)],
                                           in1=pAV[gi % 2][:, :])
                    else:
                        ts = t - 2
                        v.scalar_tensor_tensor(out=acc[0][:, 512*g:512*(g+1)], in0=pAV[gi % 2][:, :],
                                               scalar=mtab[:, 2*ts:2*ts+1], in1=acc[0][:, 512*g:512*(g+1)],
                                               op0=ALU.mult, op1=ALU.add)
                        ins = v.scalar_tensor_tensor(out=acc[1][:, 512*g:512*(g+1)], in0=pAV[gi % 2][:, :],
                                                     scalar=mtab[:, 2*ts+1:2*ts+2], in1=acc[1][:, 512*g:512*(g+1)],
                                                     op0=ALU.mult, op1=ALU.add)
                    ctx.inc(ins, "avadd", 1)
            eidx[0] += 64
            aidx[0] += 8

        for t in range(17):
            slot_loads(t)
            slot_compute(t)

        @blk.scalar
        def _(sc):
            ctx.wait(sc, "avadd")
            for qc in range(2):
                for j in range(4):
                    ctx.inc(sc.dma_start(
                        out=numdram.rearrange("(qc g j2 d2) k -> qc j2 d2 g k", g=8, j2=4, d2=8)[qc, j, :, :, :],
                        in_=acc[qc][32*j:32*j+8, :].rearrange("p (g k) -> p g k", k=512)), "bnc", 16)
                    ctx.inc(sc.dma_start(
                        out=dendram.rearrange("(qc j2 g) k -> qc j2 g k", j2=4, g=8)[qc, j, :, :].unsqueeze(0),
                        in_=acc[qc][32*j+8:32*j+9, :].rearrange("p (g k) -> p g k", k=512)), "bnc", 16)

        esA.close()
        gateT = ctx.sb("gateT", [128, 16 * QL], BF16)
        esB = contextlib.ExitStack()
        sbB = lambda n, s, d: esB.enter_context(ctx.nc.sbuf_tensor(ctx.prefix.replace(':', '_') + n, list(s), d))
        numT = sbB("numT", [128, 4 * 512], F32)
        rex = sbB("rex", [128, 4 * 512], F32)
        den32 = sbB("den32", [32, 2 * 512], F32)
        rec32 = sbB("rec32", [32, 2 * 512], F32)
        econ = sbB("econ", [32, 256], F32)

        @blk.scalar
        def _(sc):
            ctx.wait(sc, "bnc")
            ctx.inc(sc.dma_start(out=econ[:, :], in_=econst_i[:, :]), "rl", 16)
            for pt in range(2):
                for qc in range(2):
                    ctx.inc(sc.dma_start(out=numT[:, 512*(2*pt+qc):512*(2*pt+qc+1)],
                                         in_=numdram[256*qc + 128*pt:256*qc + 128*(pt+1), :]), "rl", 16)
            for qc in range(2):
                ctx.inc(sc.dma_start(out=den32[0:32, 512*qc:512*(qc+1)],
                                     in_=dendram[32*qc:32*(qc+1), :]), "rl", 16)

        @blk.vector
        def _(v):
            ctx.wait(v, "rl")
            ctx.inc(v.reciprocal(out=rec32[:, :], in_=den32[:, :]), "rcp", 1)

        @blk.tensor
        def _(t):
            ctx.wait(t, "rcp")
            for pt in range(2):
                if pt == 1:
                    ctx.waitv(t, "rex_ev", 1)
                for qc in range(2):
                    ins = t.matmul(pW[:, 512*qc:512*(qc+1)],
                                   econ[0:32, 128*pt:128*(pt+1)], rec32[0:32, 512*qc:512*(qc+1)],
                                   start=True, stop=True)
                    ctx.inc(ins, "rex_mm", 1)

        @blk.vector
        def _(v):
            for pt in range(2):
                ctx.waitv(v, "rex_mm", 2 * pt + 2)
                ins = v.tensor_copy(out=rex[:, 1024*pt:1024*(pt+1)], in_=pW[:, :])
                ctx.inc(ins, "rex_ev", 1)

        @blk.vector
        def _(v):
            ctx.wait(v, "rex_ev")
            for i in range(4):
                ins = v.tensor_mul(out=oT[:, 512*i:512*(i+1)], in0=numT[:, 512*i:512*(i+1)],
                                   in1=rex[:, 512*i:512*(i+1)])
            ctx.inc(ins, "oT_done", 1)

        esB.close()
        wgslab = ctx.sb("wgslab", [128, 256], BF16)
        x2d = [ctx.sb(f"x2d{i}", [128, QL], F32) for i in range(2)]
        xn4 = ctx.sb("xn4", [128, 16 * QL], BF16)
        sqbuf = [ctx.sb(f"sqb{i}", [128, QL], BF16) for i in range(2)]
        nsc = [ctx.sb(f"nsc{i}", [128, QL], F32) for i in range(2)]
        mbS = ctx.sb("mbS", [128, QL], F32)
        ibS = ctx.sb("ibS", [128, QL], F32)
        m_sb = nsc[0][0:1, :]
        msq_sb = nsc[1][0:1, :]
        t_sb = mbS[0:1, :]

        @blk.sync
        def _(sy):
            for hg in range(8):
                for d in range(16):
                    i = hg * 16 + d
                    if i == 0:
                        ctx.waitv(sy, "oT_done", 1)
                    if i >= 1:
                        ctx.waitv(sy, "gate_mm", i)
                    ctx.inc(sy.dma_start(out=wgslab[:, :], in_=wg_i[128*d:128*(d+1), 256*hg:256*(hg+1)]), "wgl", 16)

        for hg in range(8):
            @blk.tensor
            def _(t, hg=hg):
                for d in range(16):
                    i = hg * 16 + d
                    ctx.waitv(t, "wgl", (i + 1) * 16)
                    if d == 0:
                        ctx.wait(t, "gate_ev")
                    for j in range(2):
                        for ch in range(2):
                            ins = t.matmul(psch[j][:, 512*ch:512*ch+512],
                                           wgslab[:, 128*j:128*(j+1)],
                                           xn3[:, 1024*d + 512*ch:1024*d + 512*(ch+1)],
                                           start=(d == 0), stop=(d == 15))
                    ctx.inc(ins, "gate_mm", 1)

            @blk.scalar
            def _(sc, hg=hg):
                ctx.wait(sc, "gate_mm")
                for j in range(2):
                    for ch in range(2):
                        col = (2*hg + j) * QL + 512 * ch
                        ins = sc.activation(out=gateT[:, col:col+512], in_=psch[j][:, 512*ch:512*ch+512],
                                            func=AF.Sigmoid)
                ctx.inc(ins, "gate_ev", 1)

        @blk.gpsimd
        def _(g):
            ctx.waitv(g, "oT_done", 1)
            for dt in range(16):
                if dt >= 2:
                    ctx.waitv(g, "resid", dt - 1)
                ctx.inc(g.dma_start(out=x2d[dt % 2][:, :], in_=xT2_i[128*dt:128*(dt+1), :]), f"x2l{dt % 2}", 16)

        for ht in range(16):
            @blk.tensor
            def _(t, ht=ht):
                if ht == 0:
                    ctx.wait(t, "oT_done")
                    ctx.wait(t, "gate_ev")
                ctx.wait(t, "wo_ev")
                for qc in range(2):
                    for vt in range(2):
                        ins = t.matmul(pW[:, 512*qc:512*(qc+1)], wo1s[:, D*vt + 128*ht:D*vt + 128*(ht+1)],
                                       oT[:, 512*(2*vt+qc):512*(2*vt+qc)+512],
                                       start=(vt == 0), stop=(vt == 1))
                ctx.inc(ins, "wo_mm", 1)

            @blk.vector
            def _(v, ht=ht):
                ctx.wait(v, "wo_mm")
                ctx.waitv(v, f"x2l{ht % 2}", (ht // 2 + 1) * 16)
                ctx.inc(v.tensor_scalar(out=nsc[ht % 2][:, :], in0=pW[:, :], scalar1=biasS[:, ht:ht+1],
                                        scalar2=None, op0=ALU.add), "wo_ev", 1)
                ctx.wait(v, "wo_ev")
                ctx.inc(v.tensor_mul(out=nsc[ht % 2][:, :], in0=nsc[ht % 2][:, :],
                                     in1=gateT[:, QL*ht:QL*(ht+1)]), "vv", 1)
                ctx.wait(v, "vv")
                ins = v.tensor_add(out=x3[:, QL*ht:QL*(ht+1)], in0=nsc[ht % 2][:, :], in1=x2d[ht % 2][:, :])
                ctx.inc(ins, "resid", 1)

        @blk.scalar
        def _(sc):
            ctx.wait(sc, "resid")
            ctx.inc(sc.dma_start(out=xT3_o.rearrange("(t p) q -> p t q", p=128),
                                 in_=x3[:].rearrange("p (t q) -> p t q", q=QL)), "x3_st", 16)

        @blk.vector
        def _(v):
            ctx.wait(v, "resid")
            ctx.inc(v.memset(nsc[0][:, 0:4], 0.0), "ln4_start", 1)

        psum4 = [psch[0][:, 0:512], psch[0][:, 512:1024], psch[1][:, 0:512], psch[1][:, 512:1024]]
        emit_ln_featmajor(ctx, blk, x3, xn4, psum4, sqbuf, nsc, mbS, ibS,
                          dict(m=m_sb, msq=msq_sb, t=t_sb),
                          dict(ones_col_f=ones_col_f, ones_col_b=onesb, ones_row_f=onesrow),
                          "ln4")

        @blk.gpsimd
        def _(g):
            ctx.wait(g, "ln4_done")
            ctx.inc(g.dma_start(out=xn4_o.rearrange("(t p) q -> p t q", p=128),
                                in_=xn4[:].rearrange("p (t q) -> p t q", q=QL)), "xn4_st", 16)
            ctx.wait(g, "xn4_st")
            ctx.wait(g, "x3_st")
    return


def le_slot_schedule(c):
    slots = []
    for b in range(15 - c):
        slots.append((b, 1))
    for b in range(c):
        slots.append((b, 0))
    assert len(slots) == 15
    return slots

def le_tables(c):
    slots = le_slot_schedule(c)
    idxs = np.zeros((128, 15 * 20), np.int32)
    mtab = np.zeros((128, 15 * 2), np.float32)
    for t, (b, qc) in enumerate(slots):
        r = b if b <= 7 else 15 - b
        hf = 0 if b <= 7 else 1
        for j in range(8):
            idxs[:, 20*t + j] = 2048*r + 1024*hf + 128*j + np.arange(128)
        for j in range(8):
            idxs[:, 20*t + 8 + j] = 1024*qc + 128*j + np.arange(128)
        for j in range(4):
            idxs[:, 20*t + 16 + j] = 1024*r + 512*hf + 128*j + np.arange(128)
        mtab[:, 2*t + 0] = 1.0 if qc == 0 else 0.0
        mtab[:, 2*t + 1] = 1.0 if qc == 1 else 0.0
    return idxs, mtab

def tri_rep():
    tri = (np.arange(512)[None, :] >= np.arange(512)[:, None]).astype(BF)
    return np.ascontiguousarray(np.concatenate([tri, tri, tri, tri], 1))

def e_const():
    E = np.zeros((32, 256), np.float32)
    for p in range(32):
        j, g = p // 8, p % 8
        h = 4 * g + j
        E[p, 8*h:8*h+8] = 1.0
    return E

def le_inmaps(ld_results, lc_results, prep, inputs):
    pkall = np.ascontiguousarray(np.concatenate([r["pkext_o"].view(BF) for r in ld_results], 0))
    pvall = np.ascontiguousarray(np.concatenate([r["pvext_o"].view(BF) for r in ld_results], 0))
    tri = tri_rep()
    ec = e_const()
    ve0 = np.ascontiguousarray(inputs["L1_attn_v_emb0"].reshape(2, 128).T.astype(BF))
    maps = []
    for c in range(NC):
        idxs, mtab = le_tables(c)
        maps.append({
            "pqext_i": ld_results[c]["pqext_o"].view(BF),
            "pkext_own": ld_results[c]["pkext_o"].view(BF),
            "pvext_own": ld_results[c]["pvext_o"].view(BF),
            "pkext_all": pkall, "pvext_all": pvall,
            "idxs_i": idxs, "mtab_i": mtab, "trirep_i": tri, "econst_i": ec,
            "wo1s_i": prep["wo1_scaled"], "wo1p_i": prep["L1_attn_Wo_bf"],
            "ve0_i": ve0, "wg_i": prep["L1_attn_Wg_bf"],
            "xn3_i": lc_results[c]["xn3_o"].view(BF),
            "xT2_i": lc_results[c]["xT2_o"],
        })
    return maps

def emit_lf(ctx, io):
    xn4_i = io["xn4_i"]
    xT3_i = io["xT3_i"]
    wqg = io["wqg"]
    wqu = io["wqu"]
    wqd = io["wqd"]
    identf = io["identf"]

    xrows_o = io["xrows_o"]

    NHT = HID1 // 128
    GS = 2
    NG = NHT // GS

    if True:
        acc = ctx.sb("acc", [128, 16 * QL], F32)
        hgrp = ctx.sb("hgrp", [128, GS * QL], BF16)
        sg = ctx.sb("sg", [128, QL], BF16)
        sid = ctx.sb("sid", [128, 128], F32)
        xn4 = ctx.sb("xn4", [128, 16 * QL], BF16)
        gslab = [ctx.sb(f"gslab{i}", [128, 16 * 256], BF16) for i in range(2)]
        uslab = [ctx.sb(f"uslab{i}", [128, 16 * 256], BF16) for i in range(2)]
        pslab = [ctx.sb(f"pslab{i}", [128, GS * D], BF16) for i in range(2)]
        x3d = [ctx.sb(f"x3d{i}", [128, QL], F32) for i in range(2)]
        xstage = ctx.sb("xstage", [128, D], F32)

        pG = ctx.ps("pG", [128, QL], F32)
        pU = ctx.ps("pU", [128, QL], F32)
        pP = ctx.ps("pP", [128, QL], F32)
        blk = ctx.block if ctx.block is not None else ctx.start_block()

        @blk.gpsimd
        def _(g):
            ctx.inc(g.dma_start(out=xn4[:].rearrange("p (t q) -> p t q", q=QL),
                                in_=xn4_i.rearrange("(t p) q -> p t q", p=128)), "xn4_ld", 16)
            ctx.inc(g.dma_start(out=sid[:, :], in_=identf[:, :]), "xn4_ld", 16)

        @blk.sync
        def _(sy):
            ctx.agwait(sy, "agw", 6)
            for g in range(NG):
                if g >= 2:
                    ctx.waitv(sy, "up_mm", (g - 1) * GS)
                for d in range(16):
                    ctx.inc(sy.dma_start(out=gslab[g % 2][:, 256*d:256*(d+1)],
                                         in_=wqg[256*g:256*(g+1), 128*d:128*(d+1)], transpose=True), f"gsl{g % 2}", 16)
                    ctx.inc(sy.dma_start(out=uslab[g % 2][:, 256*d:256*(d+1)],
                                         in_=wqu[256*g:256*(g+1), 128*d:128*(d+1)], transpose=True), f"usl{g % 2}", 16)
                if g >= 2:
                    ctx.waitv(sy, "proj_mm", (g - 1) * 32)
                for htj in range(GS):
                    ctx.inc(sy.dma_start(out=pslab[g % 2][:, D*htj:D*(htj+1)],
                                         in_=wqd[:, 256*g + 128*htj:256*g + 128*(htj+1)], transpose=True), f"psl{g % 2}", 16)

        for g in range(NG):
            for htj in range(GS):
                ht = GS * g + htj

                @blk.tensor
                def _(t, g=g, htj=htj, ht=ht):
                    if ht == 0:
                        ctx.wait(t, "xn4_ld")
                    if htj == 0:
                        ctx.waitv(t, f"gsl{g % 2}", (g // 2 + 1) * 16 * 16)
                        ctx.waitv(t, f"usl{g % 2}", (g // 2 + 1) * 16 * 16)
                    ctx.wait(t, "silu")
                    for d in range(16):
                        for ch in range(2):
                            ins = t.matmul(pG[:, 512*ch:512*(ch+1)],
                                           gslab[g % 2][:, 256*d + 128*htj:256*d + 128*(htj+1)],
                                           xn4[:, 1024*d + 512*ch:1024*d + 512*(ch+1)],
                                           start=(d == 0), stop=(d == 15))
                    ctx.inc(ins, "gate_mm", 1)
                    ctx.wait(t, "hmul")
                    for d in range(16):
                        for ch in range(2):
                            ins = t.matmul(pU[:, 512*ch:512*(ch+1)],
                                           uslab[g % 2][:, 256*d + 128*htj:256*d + 128*(htj+1)],
                                           xn4[:, 1024*d + 512*ch:1024*d + 512*(ch+1)],
                                           start=(d == 0), stop=(d == 15))
                    ctx.inc(ins, "up_mm", 1)

                @blk.scalar
                def _(sc):
                    ctx.wait(sc, "gate_mm")
                    ctx.wait(sc, "hmul")
                    ctx.inc(sc.activation(out=sg[:, :], in_=pG[:, :], func=AF.Silu), "silu", 1)

                @blk.vector
                def _(v, htj=htj):
                    ctx.wait(v, "up_mm")
                    ctx.wait(v, "silu")
                    ctx.inc(v.tensor_mul(out=hgrp[:, QL*htj:QL*(htj+1)], in0=pU[:, :], in1=sg[:, :]), "hmul", 1)

            @blk.tensor
            def _(t, g=g):
                ctx.waitv(t, f"psl{g % 2}", (g // 2 + 1) * GS * 16)
                ctx.wait(t, "hmul")
                for dt in range(16):
                    for ch in range(2):
                        i = g * 32 + dt * 2 + ch
                        if i >= 2:
                            ctx.waitv(t, "acc_add", i - 1)
                        for htj in range(GS):
                            ins = t.matmul(pP[:, 512*(i % 2):512*(i % 2 + 1)],
                                           pslab[g % 2][:, D*htj + 128*dt:D*htj + 128*(dt+1)],
                                           hgrp[:, QL*htj + 512*ch:QL*htj + 512*(ch+1)],
                                           start=(htj == 0), stop=(htj == GS - 1))
                        ctx.inc(ins, "proj_mm", 1)

            @blk.vector
            def _(v, g=g):
                for dt in range(16):
                    for ch in range(2):
                        i = g * 32 + dt * 2 + ch
                        ctx.waitv(v, "proj_mm", i + 1)
                        if g == 0:
                            ins = v.tensor_copy(out=acc[:, QL*dt + 512*ch:QL*dt + 512*(ch+1)],
                                                in_=pP[:, 512*(i % 2):512*(i % 2 + 1)])
                        else:
                            ins = v.tensor_add(out=acc[:, QL*dt + 512*ch:QL*dt + 512*(ch+1)],
                                               in0=acc[:, QL*dt + 512*ch:QL*dt + 512*(ch+1)],
                                               in1=pP[:, 512*(i % 2):512*(i % 2 + 1)])
                        ctx.inc(ins, "acc_add", 1)

        @blk.gpsimd
        def _(g):
            ctx.wait(g, "xn4_ld")
            for dt in range(16):
                if dt >= 2:
                    ctx.waitv(g, "resid", dt - 1)
                ctx.inc(g.dma_start(out=x3d[dt % 2][:, :], in_=xT3_i[128*dt:128*(dt+1), :]), f"x3l{dt % 2}", 16)

        for dt in range(16):
            @blk.vector
            def _(v, dt=dt):
                if dt == 0:
                    ctx.wait(v, "acc_add")
                ctx.waitv(v, f"x3l{dt % 2}", (dt // 2 + 1) * 16)
                ctx.inc(v.tensor_add(out=acc[:, QL*dt:QL*(dt+1)], in0=acc[:, QL*dt:QL*(dt+1)],
                                     in1=x3d[dt % 2][:, :]), "resid", 1)

        for s in range(8):
            for b in range(2):
                @blk.tensor
                def _(t, s=s, b=b):
                    if s == 0 and b == 0:
                        ctx.wait(t, "resid")
                    ctx.wait(t, "xtr_ev")
                    for j in range(8):
                        dtl = b * 8 + j
                        ins = t.transpose(pG[:, 128*j:128*(j+1)], acc[:, QL*dtl + 128*s:QL*dtl + 128*(s+1)], sid[:, :])
                    ctx.inc(ins, "xtr_mm", 1)

                @blk.vector
                def _(v, s=s, b=b):
                    ctx.wait(v, "xtr_mm")
                    if b == 0:
                        ctx.waitv(v, "xr_st", max(0, s) * 16)
                    ins = v.tensor_copy(out=xstage[:, 1024*b:1024*(b+1)], in_=pG[:, :])
                    ctx.inc(ins, "xtr_ev", 1)

            @blk.scalar
            def _(sc, s=s):
                ctx.wait(sc, "xtr_ev")
                ctx.inc(sc.dma_start(out=xrows_o[128*s:128*(s+1), :], in_=xstage[:, :]), "xr_st", 16)

        @blk.gpsimd
        def _(g):
            ctx.wait(g, "xr_st")
    return


def lf_inmaps(ler, prep):
    ident = np.eye(128, dtype=np.float32)
    return [{
        "xn4_i": ler[c]["xn4_o"].view(BF),
        "xT3_i": ler[c]["xT3_o"],
        "wqg": prep["wqg_bf"], "wqu": prep["wqu_bf"], "wqd": prep["wqd_bf"],
        "identf": ident,
    } for c in range(NC)]


def _spec(CM=26112):
    return {
      "pre": dict(
        ins=[("castflat", [128, CM], F32), ("wgu_sl", [2048, 2048], F32), ("wproj_sl", [256, 8192], F32),
             ("wo1_sl", [32, 2048], F32), ("ve0_sl", [32, 1], F32), ("ve1_sl", [32, 1], F32),
             ("wqg_sl", [704, 2048], I32), ("sg_sl", [704, 1], F32), ("wqu_sl", [704, 2048], I32),
             ("su_sl", [704, 1], F32), ("wqd_sl", [256, 5632], I32), ("sd_sl", [256, 1], F32)],
        outs=[("castflat_o", [128, CM], BF16), ("wgu_o", [2048, 2048], BF16), ("wproj_o", [256, 8192], BF16),
              ("wo1_o", [32, 2048], BF16), ("wqg_o", [704, 2048], BF16), ("wqu_o", [704, 2048], BF16),
              ("wqd_o", [256, 5632], BF16)]),
      "la": dict(
        ins=[("xrows", [QL, D], F32), ("wqbf", [QSZ, D], BF16), ("wkbf", [QSZ, D], BF16), ("wvbf", [D, D], BF16),
             ("cosrep", [128, QL], F32), ("sinrep", [128, QL], F32), ("identf", [128, 128], F32),
             ("identb", [128, 128], BF16), ("permf", [128, 128], F32)],
        outs=[("qT_o", [QSZ, QL], BF16), ("kT_o", [QSZ, QL], BF16), ("vnat_o", [QL, D], BF16), ("xT_o", [D, QL], F32)]),
      "lb": dict(
        ins=[("qT_i", [QSZ, QL], BF16), ("kTfull", [NC * QSZ, QL], BF16), ("vfull", [S, D], BF16),
             ("xT_i", [D, QL], F32), ("wobf", [D, D], BF16)],
        outs=[("xT1_o", [D, QL], F32), ("xn2_o", [D, QL], BF16)]),
      "lc": dict(
        ins=[("xn2_i", [D, QL], BF16), ("xT1_i", [D, QL], F32), ("wgu", [2 * HID0, D], BF16), ("wproj", [D, HID0], BF16)],
        outs=[("xT2_o", [D, QL], F32), ("xn3_o", [D, QL], BF16)]),
      "ld": dict(
        ins=[("xn3_i", [D, QL], BF16), ("wq1", [D, 256], BF16), ("wk1", [D, 256], BF16), ("wv1", [D, 256], BF16),
             ("selc", [128, 16], BF16)],
        outs=[("pqext_o", [2 * QL, 512], BF16), ("pkext_o", [2 * QL, 512], BF16), ("pvext_o", [QL, 288], BF16)]),
      "le": dict(
        ins=[("pqext_i", [2 * QL, 512], BF16), ("pkext_own", [2 * QL, 512], BF16), ("pvext_own", [QL, 288], BF16),
             ("pkext_all", [NC * 2 * QL, 512], BF16), ("pvext_all", [NC * QL, 288], BF16),
             ("idxs_i", [128, 300], I32), ("mtab_i", [128, 30], F32), ("trirep_i", [512, 2048], BF16),
             ("econst_i", [32, 256], F32), ("wo1s_i", [256, D], BF16), ("wo1p_i", [256, D], BF16),
             ("ve0_i", [128, 2], BF16), ("wg_i", [D, D], BF16), ("xn3_i", [D, QL], BF16), ("xT2_i", [D, QL], F32)],
        outs=[("xT3_o", [D, QL], F32), ("xn4_o", [D, QL], BF16)]),
      "lf": dict(
        ins=[("xn4_i", [D, QL], BF16), ("xT3_i", [D, QL], F32), ("wqg", [HID1, D], BF16), ("wqu", [HID1, D], BF16),
             ("wqd", [D, HID1], BF16), ("identf", [128, 128], F32)],
        outs=[("xrows_o", [QL, D], F32)]),
    }



def _build_standalone(name, CM=None):
    ctx = Ctx(name)
    spec = _spec(CM or 26112)[name]
    io = {}
    for nm, shape, dt in spec["ins"]:
        io[nm] = ctx.nc.dram_tensor(nm, list(shape), dt, kind="ExternalInput").ap()
    for nm, shape, dt in spec["outs"]:
        io[nm] = ctx.nc.dram_tensor(nm, list(shape), dt, kind="ExternalOutput").ap()
    if name == "pre":
        emit_pre(ctx, io, CM)
    else:
        {"la": emit_la, "lb": emit_lb, "lc": emit_lc, "ld": emit_ld,
         "le": emit_le, "lf": emit_lf}[name](ctx, io)
    return ctx

def build_pre(CM):
    return _build_standalone("pre", CM)
def build_la():
    return _build_standalone("la")
def build_lb():
    return _build_standalone("lb")
def build_lc():
    return _build_standalone("lc")
def build_ld():
    return _build_standalone("ld")
def build_le():
    return _build_standalone("le")
def build_lf():
    return _build_standalone("lf")


CAST_SHAPES = {"L0_attn_Wq": (3072, 2048), "L0_attn_Wk": (3072, 2048),
               "L0_attn_Wv": (2048, 2048), "L0_attn_Wo": (2048, 2048),
               "L1_attn_Wq": (2048, 256), "L1_attn_Wk": (2048, 256),
               "L1_attn_Wv": (2048, 256), "L1_attn_Wg": (2048, 2048),
               "L1_attn_Wo": (256, 2048)}

def build_fused(CM):
    ctx = Ctx("fused")
    nc = ctx.nc
    din = lambda n, s, d: nc.dram_tensor(n, list(s), d, kind="ExternalInput").ap()
    dint = lambda n, s, d: nc.dram_tensor(n, list(s), d, kind="Internal").ap()

    I = {}
    for nm, shape, dt in _spec(CM)["pre"]["ins"]:
        I[nm] = din(nm, shape, dt)
    for nm, shape, dt in [("xrows", [QL, D], F32), ("cosrep", [128, QL], F32), ("sinrep", [128, QL], F32),
                          ("identf", [128, 128], F32), ("identb", [128, 128], BF16), ("permf", [128, 128], F32),
                          ("selc", [128, 16], BF16), ("idxs_i", [128, 300], I32), ("mtab_i", [128, 30], F32),
                          ("trirep_i", [512, 2048], BF16), ("econst_i", [32, 256], F32), ("ve0_i", [128, 2], BF16)]:
        I[nm] = din(nm, shape, dt)
    xrows_o = nc.dram_tensor("xrows_o", [QL, D], F32, kind="ExternalOutput").ap()

    ag = {}
    for nm, lsh, fsh, dt in [
        ("cast", [128, CM], [NC * 128, CM], BF16),
        ("wguw", [2048, 2048], [2 * HID0, 2048], BF16),
        ("wprojw", [256, 8192], [2048, 8192], BF16),
        ("wo1w", [32, 2048], [256, 2048], BF16),
        ("wqgw", [704, 2048], [HID1, 2048], BF16),
        ("wquw", [704, 2048], [HID1, 2048], BF16),
        ("wqdw", [256, 5632], [2048, 5632], BF16),
        ("kT", [QSZ, QL], [NC * QSZ, QL], BF16),
        ("vn", [QL, D], [S, D], BF16),
        ("pk", [2 * QL, 512], [NC * 2 * QL, 512], BF16),
        ("pv", [QL, 288], [NC * QL, 288], BF16),
    ]:
        ag[nm] = (dint(nm + "_l", lsh, dt),
                  nc.dram_tensor(nm + "_f", list(fsh), dt, kind="Internal",
                                 addr_space="Shared" if SHARED_AG else "Local").ap())

    loc = {}
    for nm, shape, dt in [("qT", [QSZ, QL], BF16), ("xT", [D, QL], F32), ("xn2", [D, QL], BF16),
                          ("xT1", [D, QL], F32), ("xT2", [D, QL], F32), ("xn3", [D, QL], BF16),
                          ("xT3", [D, QL], F32), ("xn4", [D, QL], BF16), ("pq", [2 * QL, 512], BF16)]:
        loc[nm] = dint(nm + "_h", shape, dt)

    flat = ag["cast"][1].rearrange("a b -> (a b)")
    wview = {}
    off = 0
    for k in CAST_KEYS:
        r, ccol = CAST_SHAPES[k]
        n = r * ccol
        wview[k] = flat[off:off + n].rearrange("(r c) -> r c", c=ccol)
        off += n

    blk = ctx.start_block()
    ctx.ags_on = True

    def _issue(g, pair, semname):
        ain, aout = pair
        ctx.inc(g.collective_compute(
            "AllGather", mybir.AluOpType.bypass,
            replica_groups=[list(range(NC))],
            ins=[ain.opt()], outs=[aout.opt()]), semname, 1)

    ctx.ag_cast_issue = lambda g: _issue(g, ag["cast"], "aga")

    def prologue():
        pass

    def issue_ags(names, semname):
        @blk.gpsimd
        def _(g):
            for nm in names:
                _issue(g, ag[nm], semname)

    ctx.set_phase("pre")
    prologue()
    io = dict(I)
    io.update({"castflat_o": ag["cast"][0], "wgu_o": ag["wguw"][0], "wproj_o": ag["wprojw"][0],
               "wo1_o": ag["wo1w"][0], "wqg_o": ag["wqgw"][0], "wqu_o": ag["wquw"][0], "wqd_o": ag["wqdw"][0]})
    emit_pre(ctx, io, CM)
    issue_ags(["wguw", "wprojw", "wo1w", "wqgw", "wquw", "wqdw"], "agw")

    ctx.set_phase("la")
    prologue()
    io = {"xrows": I["xrows"], "wqbf": wview["L0_attn_Wq"], "wkbf": wview["L0_attn_Wk"],
          "wvbf": wview["L0_attn_Wv"], "cosrep": I["cosrep"], "sinrep": I["sinrep"],
          "identf": I["identf"], "identb": I["identb"], "permf": I["permf"],
          "qT_o": loc["qT"], "kT_o": ag["kT"][0], "vnat_o": ag["vn"][0], "xT_o": loc["xT"]}
    emit_la(ctx, io)
    issue_ags(["kT", "vn"], "agk")

    ctx.set_phase("lb")
    prologue()
    io = {"qT_i": loc["qT"], "kTfull": ag["kT"][1], "vfull": ag["vn"][1], "xT_i": loc["xT"],
          "wobf": wview["L0_attn_Wo"], "xT1_o": loc["xT1"], "xn2_o": loc["xn2"]}
    emit_lb(ctx, io)

    ctx.set_phase("lc")
    prologue()
    io = {"xn2_i": loc["xn2"], "xT1_i": loc["xT1"], "wgu": ag["wguw"][1], "wproj": ag["wprojw"][1],
          "xT2_o": loc["xT2"], "xn3_o": loc["xn3"]}
    emit_lc(ctx, io)

    ctx.set_phase("ld")
    prologue()
    io = {"xn3_i": loc["xn3"], "wq1": wview["L1_attn_Wq"], "wk1": wview["L1_attn_Wk"],
          "wv1": wview["L1_attn_Wv"], "selc": I["selc"],
          "pqext_o": loc["pq"], "pkext_o": ag["pk"][0], "pvext_o": ag["pv"][0]}
    emit_ld(ctx, io)
    issue_ags(["pk", "pv"], "agk")

    ctx.set_phase("le")
    prologue()
    io = {"pqext_i": loc["pq"], "pkext_own": ag["pk"][0], "pvext_own": ag["pv"][0],
          "pkext_all": ag["pk"][1], "pvext_all": ag["pv"][1],
          "idxs_i": I["idxs_i"], "mtab_i": I["mtab_i"], "trirep_i": I["trirep_i"], "econst_i": I["econst_i"],
          "wo1s_i": ag["wo1w"][1], "wo1p_i": wview["L1_attn_Wo"], "ve0_i": I["ve0_i"],
          "wg_i": wview["L1_attn_Wg"], "xn3_i": loc["xn3"], "xT2_i": loc["xT2"],
          "xT3_o": loc["xT3"], "xn4_o": loc["xn4"]}
    emit_le(ctx, io)

    ctx.set_phase("lf")
    prologue()
    io = {"xn4_i": loc["xn4"], "xT3_i": loc["xT3"], "wqg": ag["wqgw"][1], "wqu": ag["wquw"][1],
          "wqd": ag["wqdw"][1], "identf": I["identf"], "xrows_o": xrows_o}
    emit_lf(ctx, io)
    ctx.end_phases()
    return ctx


def fused_inmaps(inputs):
    premaps, CM = pre_inmaps(inputs)
    x = np.asarray(inputs["L0_attn_hidden_states"]).reshape(S, D)
    ident = np.eye(128, dtype=np.float32)
    identb = np.eye(128, dtype=np.float32).astype(BF)
    perm = perm_matrix()
    sc = sel_const()
    tri = tri_rep()
    ec = e_const()
    ve0 = np.ascontiguousarray(np.asarray(inputs["L1_attn_v_emb0"]).reshape(2, 128).T.astype(BF))
    maps = []
    for c in range(NC):
        lo, hi = blocks_of(c)
        xr = np.ascontiguousarray(np.concatenate([x[lo*BLK:(lo+1)*BLK], x[hi*BLK:(hi+1)*BLK]], 0))
        crep, srep = core_cos_sin(c)
        idxs, mtab = le_tables(c)
        m = dict(premaps[c])
        m.update({"xrows": xr, "cosrep": crep, "sinrep": srep, "identf": ident, "identb": identb,
                  "permf": perm, "selc": sc, "idxs_i": idxs, "mtab_i": mtab, "trirep_i": tri,
                  "econst_i": ec, "ve0_i": ve0})
        maps.append(m)
    return maps, CM


def kernel(**inputs):
    inputs = {k: np.asarray(v) for k, v in inputs.items()}
    maps, CM = fused_inmaps(inputs)
    ctx = build_fused(CM)
    res = run_bass_kernel_spmd(ctx.nc, maps, core_ids=list(range(NC)))
    out = np.zeros((S, D), np.float32)
    for c in range(NC):
        lo, hi = blocks_of(c)
        xr = res.results[c]["xrows_o"]
        out[lo*BLK:(lo+1)*BLK] = xr[0:BLK]
        out[hi*BLK:(hi+1)*BLK] = xr[BLK:QL]
    return out.reshape(1, S, D)


# version 08 (final)
# speedup vs baseline: 1.0523x; 1.0523x over previous
import math
import contextlib
import numpy as np
import ml_dtypes
import concourse.bass as bass
import concourse.mybir as mybir
from concourse.bass_utils import run_bass_kernel_spmd

F32 = mybir.dt.float32
BF16 = mybir.dt.bfloat16
I32 = mybir.dt.int32
AF = mybir.ActivationFunctionType
ALU = mybir.AluOpType
AX = mybir.AxisListType
BF = ml_dtypes.bfloat16

NC = 8
S, D = 8192, 2048
NBLK, BLK = 16, 512
QL = 1024
H0, DN, DR, DV = 16, 128, 64, 128
QSZ = H0 * (DN + DR)
H1, QB1, VB1 = 32, 8, 8
HID1 = 5632
HID0 = 8192

SHARED_AG = True


def blocks_of(c):
    return (c, 15 - c)


class Ctx:
    def __init__(self, name):
        self.nc = bass.Bass("TRN2", target_bir_lowering=False, debug=False)
        self.es = contextlib.ExitStack()
        self.cnt = {}
        self.sems = {}
        self.block = None
        self.name = name
        self.prefix = ""
        self.phase_es = self.es
        self.slots = []
        self.slot_of = {}
        self.stable_slot = {}
        self.reserved = set()
        self.used_now = set()
        self.base = {}
        self._slot_snap = {}
        self._barh = None
        self._nbar = 0
        self.phase_marks = []
        self.cls_seen = set()
        self.viol = set()

    def set_phase(self, prefix):
        if self.phase_es is not self.es:
            snap = self._slot_snap
            slots = self.slots
            blk = self.block
            if self._barh is None:
                self._barh = self.es.enter_context(self.nc.semaphore("barx"))
            barh = self._barh
            self._nbar += 1
            tgt = 5 * self._nbar

            @blk.sync
            def _(sy):
                for i, s in enumerate(slots):
                    if s["cls"] != "C" and s["cnt"] != snap.get(i, 0):
                        sy.wait_ge(s["h"], s["cnt"])
            for secname in ("sync", "scalar", "vector", "tensor", "gpsimd"):
                @getattr(blk, secname)
                def _(eng):
                    eng.wait_ge(barh, 0).then_inc(barh, 1)
                    eng.wait_ge(barh, tgt)
            self.phase_es.close()
        self.prefix = prefix + ":"
        self.used_now = set(self.reserved)
        self._slot_snap = {i: s["cnt"] for i, s in enumerate(self.slots)}
        self.phase_marks.append((prefix, self.nc.next_id()))
        self.phase_es = contextlib.ExitStack()

    def end_phases(self):
        if self.phase_es is not self.es:
            self.phase_es.close()
            self.phase_es = self.es
        self.prefix = ""

    def start_block(self):
        self.block = self.es.enter_context(self.nc.Block())
        return self.block

    SWDGE_SEMS = frozenset(["ast", "dq_ld", "tern_ld", "x_ld", "xn4_ld", "zst",
                            "cast_ld0", "cast_ld1", "sl0", "sl1",
                            "x1l0", "x1l1", "x2l0", "x2l1", "x3l0", "x3l1",
                            "kvv0", "kvv1", "x1_st", "xn2_ld", "xn2_st",
                            "xn3_st", "xn4_st"])
    STABLE_SEMS = frozenset(["aga", "agk", "agw"])

    def sem(self, name):
        full = self.prefix + name
        idx = self.slot_of.get(full)
        if idx is None:
            if name in self.STABLE_SEMS:
                cls = "C"
            elif name in self.SWDGE_SEMS:
                cls = "S"
            else:
                cls = "H"
            if cls == "C" and name in self.stable_slot:
                idx = self.stable_slot[name]
            else:
                idx = next((i for i, s in enumerate(self.slots)
                            if s["cls"] == cls and i not in self.used_now), None)
                if idx is None:
                    idx = len(self.slots)
                    self.slots.append({"h": self.es.enter_context(self.nc.semaphore(f"sem{idx}")),
                                       "cnt": 0, "cls": cls})
                if cls == "C":
                    self.stable_slot[name] = idx
                    self.reserved.add(idx)
            self.slot_of[full] = idx
            self.used_now.add(idx)
            self.base[full] = 0 if name in self.STABLE_SEMS else self.slots[idx]["cnt"]
            self.cnt[full] = 0
        return self.slots[idx]["h"]

    def val(self, name):
        self.sem(name)
        return self.cnt[self.prefix + name]

    def inc(self, instr, name, n=1):
        h = self.sem(name)
        full = self.prefix + name
        if full not in self.cls_seen and name not in self.STABLE_SEMS:
            c = instr.concise()
            swdge = c.lstrip().startswith("PL") and ("DMACopy" in c or "Dma" in c)
            want = "S" if swdge else "H"
            if self.slots[self.slot_of[full]]["cls"] != want:
                raise RuntimeError(f"sem {name} misclassified: should be {want}")
            self.cls_seen.add(full)
        instr.then_inc(h, n)
        self.cnt[full] += n
        self.slots[self.slot_of[full]]["cnt"] += n
        return self.cnt[full]

    def wait(self, eng, name, v=None):
        h = self.sem(name)
        full = self.prefix + name
        eng.wait_ge(h, self.base[full] + (self.cnt[full] if v is None else v))

    def waitv(self, eng, name, v):
        h = self.sem(name)
        full = self.prefix + name
        eng.wait_ge(h, self.base[full] + v)

    def agwait(self, eng, name, v):
        if getattr(self, "ags_on", False):
            self.waitv(eng, name, v)

    def sb(self, name, shape, dtype):
        return self.phase_es.enter_context(self.nc.sbuf_tensor(self.prefix.replace(":", "_") + name, list(shape), dtype))

    def ps(self, name, shape, dtype=F32):
        return self.phase_es.enter_context(self.nc.psum_tensor(self.prefix.replace(":", "_") + name, list(shape), dtype))

    def din(self, name, shape, dtype):
        return self.nc.dram_tensor(name, list(shape), dtype, kind="ExternalInput").ap()

    def dout(self, name, shape, dtype):
        return self.nc.dram_tensor(name, list(shape), dtype, kind="ExternalOutput").ap()

    def dint(self, name, shape, dtype):
        return self.nc.dram_tensor(self.prefix.replace(":", "_") + name, list(shape), dtype, kind="Internal").ap()


def emit_pre(ctx, io, CM=None):
    castflat = io["castflat"]
    wgu_sl = io["wgu_sl"]
    wproj_sl = io["wproj_sl"]
    wo1_sl = io["wo1_sl"]
    ve0_sl = io["ve0_sl"]
    ve1_sl = io["ve1_sl"]
    wqg_sl = io["wqg_sl"]
    sg_sl = io["sg_sl"]
    wqu_sl = io["wqu_sl"]
    su_sl = io["su_sl"]
    wqd_sl = io["wqd_sl"]
    sd_sl = io["sd_sl"]

    castflat_o = io["castflat_o"]
    wgu_o = io["wgu_o"]
    wproj_o = io["wproj_o"]
    wo1_o = io["wo1_o"]
    wqg_o = io["wqg_o"]
    wqu_o = io["wqu_o"]
    wqd_o = io["wqd_o"]

    FRMAX, GMAX = 8192, 128
    CCH = 8192
    ncast = (CM + CCH - 1) // CCH

    if True:
        cbuf = [ctx.sb(f"cbuf{i}", [128, CCH], BF16) for i in range(2)]
        wb = ctx.sb("wb", [128, FRMAX], BF16)
        ob = ctx.sb("ob", [128, FRMAX], BF16)
        absb = ctx.sb("absb", [128, FRMAX], BF16)
        scl = ctx.sb("scl", [128, GMAX], F32)
        ssum = ctx.sb("ssum", [128, GMAX], F32)
        smb = ctx.sb("smb", [128, GMAX], BF16)
        rec = ctx.sb("rec", [128, GMAX], F32)
        sgn = ctx.sb("sgn", [128, FRMAX], BF16)
        rbf = ctx.sb("rbf", [128, FRMAX], BF16)
        stp = ctx.sb("stp", [128, FRMAX], BF16)
        mhalfb = ctx.sb("mhalfb", [128, 1], F32)
        dqint = ctx.sb("dqint", [128, 5632], I32)
        dq = ctx.sb("dq", [128, 5632], BF16)
        dqs = ctx.sb("dqs", [128, 1], F32)
        dqb = ctx.sb("dqb", [128, 1], F32)
        wo1b = ctx.sb("wo1b", [32, 2048], F32)
        ved = ctx.sb("ved", [32, 1], F32)
        ve1b = ctx.sb("ve1b", [32, 1], F32)
        wo1ob = ctx.sb("wo1ob", [32, 2048], BF16)
        blk = ctx.block if ctx.block is not None else ctx.start_block()

        @blk.gpsimd
        def _(g):
            for i in range(ncast):
                c0, c1 = i * CCH, min(CM, (i + 1) * CCH)
                if i >= 2:
                    ctx.waitv(g, f"cast_st{i % 2}", (i // 2) * 16)
                ctx.inc(g.dma_start(out=cbuf[i % 2][:, 0:c1 - c0], in_=castflat[:, c0:c1]), f"cast_ld{i % 2}", 16)

        @blk.sync
        def _(sy):
            for i in range(ncast):
                c0, c1 = i * CCH, min(CM, (i + 1) * CCH)
                ctx.waitv(sy, f"cast_ld{i % 2}", (i // 2 + 1) * 16)
                ctx.inc(sy.dma_start(out=castflat_o[:, c0:c1], in_=cbuf[i % 2][:, 0:c1 - c0]), f"cast_st{i % 2}", 16)

        if getattr(ctx, "ag_cast_issue", None):
            @blk.gpsimd
            def _(g):
                ctx.wait(g, "cast_st0")
                ctx.wait(g, "cast_st1")
                ctx.ag_cast_issue(g)

        @blk.scalar
        def _(sc):
            ctx.inc(sc.dma_start(out=wo1b[:, :], in_=wo1_sl[:, :]), "wo1_ld", 16)
            ctx.inc(sc.dma_start(out=ved[:, :], in_=ve0_sl[:, :]), "wo1_ld", 16)
            ctx.inc(sc.dma_start(out=ve1b[:, :], in_=ve1_sl[:, :]), "wo1_ld", 16)

        @blk.vector
        def _(v):
            ctx.inc(v.memset(mhalfb[:, :], -0.5), "mhalf", 1)

        @blk.vector
        def _(v):
            ctx.wait(v, "wo1_ld")
            ctx.inc(v.tensor_sub(out=ved[:, :], in0=ve1b[:, :], in1=ved[:, :]), "vv", 1)
            ctx.wait(v, "vv")
            ctx.inc(v.tensor_scalar(out=wo1ob[:, :], in0=wo1b[:, :], scalar1=ved[:, 0:1],
                                    scalar2=None, op0=ALU.mult), "wo1_done", 1)

        @blk.scalar
        def _(sc):
            ctx.wait(sc, "wo1_done")
            ctx.inc(sc.dma_start(out=wo1_o[:, :], in_=wo1ob[:, :]), "wo1_st", 16)

        def dequant(qin, sin, qout, nrows, ncols):
            r = 0
            while r < nrows:
                rr = min(128, nrows - r)

                @blk.gpsimd
                def _(g, r=r, rr=rr):
                    ctx.wait(g, "dq_st")
                    ctx.inc(g.dma_start(out=dqint[0:rr, 0:ncols], in_=qin[r:r + rr, 0:ncols]), "dq_ld", 16)
                    ctx.inc(g.dma_start(out=dqs[0:rr, :], in_=sin[r:r + rr, :]), "dq_ld", 16)

                @blk.vector
                def _(v, r=r, rr=rr):
                    ctx.wait(v, "dq_ld")
                    ctx.inc(v.tensor_scalar(out=dqb[0:rr, :], in0=dqs[0:rr, :], scalar1=-128.0, scalar2=None, op0=ALU.mult), "vv", 1)
                    ctx.wait(v, "vv")
                    ctx.inc(v.tensor_scalar(out=dq[0:rr, 0:ncols], in0=dqint[0:rr, 0:ncols],
                                            scalar1=dqs[0:rr, 0:1], scalar2=dqb[0:rr, 0:1],
                                            op0=ALU.mult, op1=ALU.add), "dq_done", 1)

                @blk.sync
                def _(sy, r=r, rr=rr):
                    ctx.wait(sy, "dq_done")
                    ctx.inc(sy.dma_start(out=qout[r:r + rr, 0:ncols], in_=dq[0:rr, 0:ncols]), "dq_st", 16)
                r += 128

        dequant(wqg_sl, sg_sl, wqg_o, 704, 2048)
        dequant(wqu_sl, su_sl, wqu_o, 704, 2048)
        dequant(wqd_sl, sd_sl, wqd_o, 256, 5632)

        wgu_v = wgu_sl.rearrange("(t p) d -> t p d", p=128)
        wgu_ov = wgu_o.rearrange("(t p) d -> t p d", p=128)
        jobs = [("wgu", t) for t in range(16)] + [("wproj", t) for t in range(2)]
        for kind, t in jobs:
            if kind == "wgu":
                src, dst, FRt, G = wgu_v[t, :, :], wgu_ov[t, :, :], 2048, 32
            else:
                src, dst = wproj_sl[128 * t:128 * (t + 1), :], wproj_o[128 * t:128 * (t + 1), :]
                FRt, G = 8192, 128

            @blk.gpsimd
            def _(g, src=src, FRt=FRt):
                ctx.wait(g, "tern_st")
                ctx.inc(g.dma_start(out=wb[:, 0:FRt], in_=src), "tern_ld", 16)

            @blk.scalar
            def _(sc, FRt=FRt):
                ctx.wait(sc, "tern_ld")
                ctx.wait(sc, "mhalf")
                ctx.wait(sc, "tern_v")
                ctx.inc(sc.activation(out=absb[:, 0:FRt], in_=wb[:, 0:FRt], func=AF.Abs), "tern_a", 1)

            @blk.vector
            def _(v, FRt=FRt, G=G):
                ctx.wait(v, "tern_a")
                ctx.inc(v.tensor_reduce(out=ssum[:, 0:G], in_=absb[:, 0:FRt].rearrange("p (g r) -> p g r", r=64),
                                axis=AX.X, op=ALU.add), "vv", 1)
                ctx.wait(v, "vv")
                ctx.inc(v.tensor_scalar(out=smb[:, 0:G], in0=ssum[:, 0:G], scalar1=1.0 / 64.0, scalar2=None, op0=ALU.mult), "vv", 1)
                ctx.wait(v, "vv")
                ctx.inc(v.tensor_copy(out=scl[:, 0:G], in_=smb[:, 0:G]), "vv", 1)
                ctx.wait(v, "vv")
                ctx.inc(v.tensor_scalar(out=scl[:, 0:G], in0=scl[:, 0:G], scalar1=1e-8, scalar2=None, op0=ALU.max), "vv", 1)
                ctx.wait(v, "vv")
                ctx.inc(v.reciprocal(out=rec[:, 0:G], in_=scl[:, 0:G]), "vv", 1)
                ctx.wait(v, "vv")
                for g in range(G):
                    ins = v.tensor_scalar(out=rbf[:, 64 * g:64 * (g + 1)], in0=wb[:, 64 * g:64 * (g + 1)],
                                    scalar1=rec[:, g:g + 1], scalar2=None, op0=ALU.mult)
                ctx.inc(ins, "tern_rb", 1)

            @blk.scalar
            def _(sc, FRt=FRt):
                ctx.wait(sc, "tern_rb")
                ctx.inc(sc.activation(out=absb[:, 0:FRt], in_=rbf[:, 0:FRt], func=AF.Abs), "ss", 1)
                ctx.wait(sc, "ss")
                ctx.inc(sc.activation(out=stp[:, 0:FRt], in_=absb[:, 0:FRt], func=AF.Sign, bias=mhalfb[:, 0:1], scale=1.0), "ss", 1)
                ctx.wait(sc, "ss")
                sc.activation(out=stp[:, 0:FRt], in_=stp[:, 0:FRt], func=AF.Relu)
                ctx.inc(sc.activation(out=sgn[:, 0:FRt], in_=rbf[:, 0:FRt], func=AF.Sign), "tern_s", 1)

            @blk.vector
            def _(v, FRt=FRt, G=G):
                ctx.wait(v, "tern_s")
                ctx.inc(v.tensor_mul(out=sgn[:, 0:FRt], in0=sgn[:, 0:FRt], in1=stp[:, 0:FRt]), "vv", 1)
                ctx.wait(v, "vv")
                for g in range(G):
                    ins = v.tensor_scalar(out=ob[:, 64 * g:64 * (g + 1)], in0=sgn[:, 64 * g:64 * (g + 1)],
                                    scalar1=scl[:, g:g + 1], scalar2=None, op0=ALU.mult)
                ctx.inc(ins, "tern_v", 1)

            @blk.sync
            def _(sy, dst=dst, FRt=FRt):
                ctx.wait(sy, "tern_v")
                ctx.inc(sy.dma_start(out=dst, in_=ob[:, 0:FRt]), "tern_st", 16)

        @blk.gpsimd
        def _(g):
            ctx.wait(g, "cast_st0")
            ctx.wait(g, "cast_st1")
            ctx.wait(g, "wo1_st")
            ctx.wait(g, "dq_st")
            ctx.wait(g, "tern_st")

    return


CAST_KEYS = ["L0_attn_Wq", "L0_attn_Wk", "L0_attn_Wv", "L0_attn_Wo",
             "L1_attn_Wq", "L1_attn_Wk", "L1_attn_Wv", "L1_attn_Wg", "L1_attn_Wo"]

def pre_inmaps(inputs):
    flat = np.concatenate([np.ascontiguousarray(inputs[k]).reshape(-1) for k in CAST_KEYS])
    tot = flat.size
    percore = tot // NC
    assert tot % NC == 0
    CM = percore // 128
    assert percore % 128 == 0
    wgu = inputs["L0_ffn_w_gate_up"]
    wproj = inputs["L0_ffn_w_proj"]
    wo1 = inputs["L1_attn_Wo"]
    ve0 = inputs["L1_attn_v_emb0"].reshape(256, 1)
    ve1 = inputs["L1_attn_v_emb1"].reshape(256, 1)
    maps = []
    for c in range(NC):
        maps.append({
            "castflat": np.ascontiguousarray(flat[c * percore:(c + 1) * percore].reshape(128, CM)),
            "wgu_sl": np.ascontiguousarray(wgu[2048 * c:2048 * (c + 1)]),
            "wproj_sl": np.ascontiguousarray(wproj[256 * c:256 * (c + 1)]),
            "wo1_sl": np.ascontiguousarray(wo1[32 * c:32 * (c + 1)]),
            "ve0_sl": np.ascontiguousarray(ve0[32 * c:32 * (c + 1)]),
            "ve1_sl": np.ascontiguousarray(ve1[32 * c:32 * (c + 1)]),
            "wqg_sl": np.ascontiguousarray(inputs["L1_ffn_wq_gate"][704 * c:704 * (c + 1)]),
            "sg_sl": np.ascontiguousarray(inputs["L1_ffn_s_gate"].reshape(-1, 1)[704 * c:704 * (c + 1)]),
            "wqu_sl": np.ascontiguousarray(inputs["L1_ffn_wq_up"][704 * c:704 * (c + 1)]),
            "su_sl": np.ascontiguousarray(inputs["L1_ffn_s_up"].reshape(-1, 1)[704 * c:704 * (c + 1)]),
            "wqd_sl": np.ascontiguousarray(inputs["L1_ffn_wq_down"][:, :].reshape(2048, 5632)[256 * c:256 * (c + 1)]),
            "sd_sl": np.ascontiguousarray(inputs["L1_ffn_s_down"].reshape(-1, 1)[256 * c:256 * (c + 1)]),
        })
    return maps, CM

def pre_assemble(results):
    flat = np.concatenate([r["castflat_o"].reshape(-1) for r in results])
    out = {}
    off = 0
    shapes = {"L0_attn_Wq": (3072, 2048), "L0_attn_Wk": (3072, 2048),
              "L0_attn_Wv": (2048, 2048), "L0_attn_Wo": (2048, 2048),
              "L1_attn_Wq": (2048, 256), "L1_attn_Wk": (2048, 256),
              "L1_attn_Wv": (2048, 256), "L1_attn_Wg": (2048, 2048),
              "L1_attn_Wo": (256, 2048)}
    for k in CAST_KEYS:
        n = shapes[k][0] * shapes[k][1]
        out[k + "_bf"] = flat[off:off + n].reshape(shapes[k])
        off += n
    out["wgu_hat"] = np.concatenate([r["wgu_o"] for r in results], 0)
    out["wproj_hat"] = np.concatenate([r["wproj_o"] for r in results], 0)
    out["wo1_scaled"] = np.concatenate([r["wo1_o"] for r in results], 0)
    out["wqg_bf"] = np.concatenate([r["wqg_o"] for r in results], 0)
    out["wqu_bf"] = np.concatenate([r["wqu_o"] for r in results], 0)
    out["wqd_bf"] = np.concatenate([r["wqd_o"] for r in results], 0)
    return out


def run_pre(inputs):
    maps, CM = pre_inmaps(inputs)
    ctx = build_pre(CM)
    res = run_bass_kernel_spmd(ctx.nc, maps, core_ids=list(range(NC)))
    return pre_assemble(res.results)




RSN = 1.0 / math.sqrt(DN)
RSR = 1.0 / math.sqrt(DR)

def emit_ln_featmajor(ctx, blk, xT, xn, psum4, sqbuf, nsc, mbS, ibS, stats_sb, consts, pfx, eps=1e-5):
    m, msq, tv = stats_sb["m"], stats_sb["msq"], stats_sb["t"]
    ocf, ocb, orf = consts["ones_col_f"], consts["ones_col_b"], consts["ones_row_f"]

    for d in range(16):
        @blk.vector
        def _(v, d=d):
            if d == 0:
                ctx.wait(v, pfx + "_start")
            if d >= 2:
                ctx.waitv(v, pfx + "_mmprog", d - 1)
            ctx.inc(v.tensor_mul(out=sqbuf[d % 2][:, :], in0=xT[:, 1024*d:1024*(d+1)],
                                 in1=xT[:, 1024*d:1024*(d+1)]), pfx + "_sq", 1)

        @blk.tensor
        def _(t, d=d):
            ctx.wait(t, pfx + "_sq")
            st, sp = (d == 0), (d == 15)
            for ch in range(2):
                t.matmul(psum4[ch][0:1, :], ocf[:, 0:1], xT[:, 1024*d + 512*ch:1024*d + 512*(ch+1)],
                         start=st, stop=sp)
            for ch in range(2):
                ins = t.matmul(psum4[2 + ch][0:1, :], ocb[:, 0:1], sqbuf[d % 2][:, 512*ch:512*(ch+1)],
                               start=st, stop=sp)
            ctx.inc(ins, pfx + "_mmprog", 1)

    @blk.vector
    def _(v):
        ctx.wait(v, pfx + "_mmprog")
        for ch in range(2):
            v.tensor_scalar(out=m[0:1, 512*ch:512*(ch+1)], in0=psum4[ch][0:1, :], scalar1=1.0/2048.0, scalar2=None, op0=ALU.mult)
            ctx.inc(v.tensor_scalar(out=msq[0:1, 512*ch:512*(ch+1)], in0=psum4[2+ch][0:1, :], scalar1=1.0/2048.0, scalar2=None, op0=ALU.mult), "vv", 1)
        ctx.wait(v, "vv")
        ctx.inc(v.tensor_mul(out=tv[0:1, :], in0=m[0:1, :], in1=m[0:1, :]), "vv", 1)
        ctx.wait(v, "vv")
        ctx.inc(v.tensor_sub(out=tv[0:1, :], in0=msq[0:1, :], in1=tv[0:1, :]), "vv", 1)
        ctx.wait(v, "vv")
        ctx.inc(v.tensor_scalar(out=tv[0:1, :], in0=tv[0:1, :], scalar1=eps, scalar2=None, op0=ALU.add), pfx + "_var", 1)

    @blk.scalar
    def _(sc):
        ctx.wait(sc, pfx + "_var")
        ctx.inc(sc.activation(out=tv[0:1, :], in_=tv[0:1, :], func=AF.Sqrt), pfx + "_sd", 1)

    @blk.vector
    def _(v):
        ctx.wait(v, pfx + "_sd")
        ctx.inc(v.reciprocal(out=msq[0:1, :], in_=tv[0:1, :]), pfx + "_inv", 1)

    @blk.tensor
    def _(t):
        ctx.wait(t, pfx + "_inv")
        for ch in range(2):
            t.matmul(psum4[ch][:, :], orf[0:1, :], m[0:1, 512*ch:512*(ch+1)], start=True, stop=True)
        for ch in range(2):
            ins = t.matmul(psum4[2+ch][:, :], orf[0:1, :], msq[0:1, 512*ch:512*(ch+1)], start=True, stop=True)
        ctx.inc(ins, pfx + "_bc", 1)

    @blk.vector
    def _(v):
        ctx.wait(v, pfx + "_bc")
        for ch in range(2):
            v.tensor_copy(out=mbS[:, 512*ch:512*(ch+1)], in_=psum4[ch][:, :])
            ins = v.tensor_copy(out=ibS[:, 512*ch:512*(ch+1)], in_=psum4[2+ch][:, :])
        ctx.inc(ins, pfx + "_bcev", 1)


    for d in range(16):
        @blk.vector
        def _(v, d=d):
            if d == 0:
                ctx.wait(v, pfx + "_bcev")
            ctx.inc(v.tensor_sub(out=nsc[d % 2][:, :], in0=xT[:, 1024*d:1024*(d+1)], in1=mbS[:, :]), "vv", 1)
            ctx.wait(v, "vv")
            ins = v.tensor_mul(out=xn[:, 1024*d:1024*(d+1)], in0=nsc[d % 2][:, :], in1=ibS[:, :])
            if d == 15:
                ctx.inc(ins, pfx + "_done", 1)


def emit_la(ctx, io):
    xrows = io["xrows"]
    wqbf = io["wqbf"]
    wkbf = io["wkbf"]
    wvbf = io["wvbf"]
    cosrep = io["cosrep"]
    sinrep = io["sinrep"]
    identf = io["identf"]
    identb = io["identb"]
    permf = io["permf"]

    qT_o = io["qT_o"]
    kT_o = io["kT_o"]
    vnat_o = io["vnat_o"]
    xT_o = io["xT_o"]

    if True:
        xin2 = [ctx.sb(f"xin{i}", [128, D], F32) for i in range(2)]
        xnat = [ctx.sb(f"xnat{i}", [128, D], BF16) for i in range(2)]
        sqsc = ctx.sb("sqsc", [128, D], F32)
        stat2 = [ctx.sb(f"stat{i}", [128, 8], F32) for i in range(2)]
        xn = ctx.sb("xn", [128, 16 * QL], BF16)
        xTp = [ctx.sb(f"xTp{i}", [128, QL], F32) for i in range(2)]
        ones_col_f = ctx.sb("ones_col_f", [128, 1], F32)
        sid = ctx.sb("sid", [128, 128], F32)
        sidb = ctx.sb("sidb", [128, 128], BF16)
        sperm = ctx.sb("sperm", [128, 128], F32)
        scos = ctx.sb("scos", [128, QL], F32)
        ssin = ctx.sb("ssin", [128, QL], F32)
        slab2 = [ctx.sb(f"slab{i}", [128, 256], BF16) for i in range(4)]
        qkstage = ctx.sb("qkstage", [128, 2 * QL], BF16)
        vTstage = ctx.sb("vTstage", [128, 2 * QL], BF16)
        vnstage = ctx.sb("vnstage", [128, 8 * 256], BF16)
        ropesc = ctx.sb("ropesc", [128, 2 * QL], F32)
        ropet1 = ctx.sb("ropet1", [128, 2 * QL], F32)
        pA = ctx.ps("pA", [128, QL], F32)
        pC = [ctx.ps(f"pC{i}", [128, 512], F32) for i in range(4)]
        pVT = ctx.ps("pVT", [128, QL], BF16)
        blk = ctx.block if ctx.block is not None else ctx.start_block()

        @blk.sync
        def _(sy):
            ctx.inc(sy.dma_start(out=sid[:, :], in_=identf[:, :]), "c_ld", 16)
            ctx.inc(sy.dma_start(out=sidb[:, :], in_=identb[:, :]), "c_ld", 16)
            ctx.inc(sy.dma_start(out=sperm[:, :], in_=permf[:, :]), "c_ld", 16)
            ctx.inc(sy.dma_start(out=scos[:, :], in_=cosrep[:, :]), "c_ld", 16)
            ctx.inc(sy.dma_start(out=ssin[:, :], in_=sinrep[:, :]), "c_ld", 16)
            for s in range(8):
                if s >= 2:
                    ctx.waitv(sy, "xntr_ev", (s - 1) * 2)
                ctx.inc(sy.dma_start(out=xin2[s % 2][:, :], in_=xrows[128*s:128*(s+1), :]), f"x_ld{s}", 16)

        @blk.vector
        def _(v):
            ctx.inc(v.memset(ones_col_f[:, :], 1.0), "ones", 1)

        for s in range(8):
            @blk.scalar
            def _(sc, s=s):
                ctx.waitv(sc, f"x_ld{s}", 16)
                if s >= 2:
                    ctx.waitv(sc, "ln_xn", s - 1)
                ctx.inc(sc.activation(out=sqsc[:, :], in_=xin2[s % 2][:, :], func=AF.Square,
                                      accum_out=stat2[s % 2][:, 1:2]), "ln_sq", 1)

            @blk.vector
            def _(v, s=s):
                ctx.waitv(v, f"x_ld{s}", 16)
                st = stat2[s % 2]
                ctx.inc(v.reduce_sum(out=st[:, 0:1], in_=xin2[s % 2][:, :], axis=AX.X), "vv", 1)
                ctx.wait(v, "ln_sq")
                ctx.wait(v, "vv")
                ctx.inc(v.tensor_scalar(out=st[:, 2:3], in0=st[:, 0:1], scalar1=1.0/2048.0, scalar2=None, op0=ALU.mult), "vv", 1)
                ctx.inc(v.tensor_scalar(out=st[:, 3:4], in0=st[:, 1:2], scalar1=1.0/2048.0, scalar2=None, op0=ALU.mult), "vv", 1)
                ctx.wait(v, "vv")
                ctx.inc(v.tensor_mul(out=st[:, 4:5], in0=st[:, 2:3], in1=st[:, 2:3]), "vv", 1)
                ctx.wait(v, "vv")
                ctx.inc(v.tensor_sub(out=st[:, 4:5], in0=st[:, 3:4], in1=st[:, 4:5]), "vv", 1)
                ctx.wait(v, "vv")
                ctx.inc(v.tensor_scalar(out=st[:, 4:5], in0=st[:, 4:5], scalar1=1e-5, scalar2=None, op0=ALU.add), "ln_var", 1)

            @blk.scalar
            def _(sc, s=s):
                ctx.wait(sc, "ln_var")
                ctx.inc(sc.activation(out=stat2[s % 2][:, 5:6], in_=stat2[s % 2][:, 4:5], func=AF.Sqrt), "ln_sd", 1)

            @blk.vector
            def _(v, s=s):
                ctx.wait(v, "ln_sd")
                st = stat2[s % 2]
                ctx.inc(v.reciprocal(out=st[:, 6:7], in_=st[:, 5:6]), "vv", 1)
                ctx.wait(v, "vv")
                ctx.inc(v.tensor_scalar(out=xnat[s % 2][:, :], in0=xin2[s % 2][:, :],
                                        scalar1=st[:, 2:3], scalar2=st[:, 6:7],
                                        op0=ALU.subtract, op1=ALU.mult), "ln_xn", 1)

            for b in range(2):
                @blk.tensor
                def _(t, s=s, b=b):
                    if s == 0 and b == 0:
                        ctx.wait(t, "c_ld")
                    ctx.waitv(t, f"x_ld{s}", 16)
                    ctx.wait(t, "xtr_ev")
                    for j in range(8):
                        dt = b * 8 + j
                        ins = t.transpose(pA[:, 128*j:128*(j+1)], xin2[s % 2][:, 128*dt:128*(dt+1)], sid[:, :])
                    ctx.inc(ins, "xtr_mm", 1)

                @blk.vector
                def _(v, s=s, b=b):
                    ctx.wait(v, "xtr_mm")
                    ctx.waitv(v, f"xT_st{b}", max(0, s) * 16)
                    ctx.inc(v.tensor_copy(out=xTp[b][:, :], in_=pA[:, :]), "xtr_ev", 1)

                @blk.scalar
                def _(sc, s=s, b=b):
                    ctx.wait(sc, "xtr_ev")
                    ins = sc.dma_start(
                        out=xT_o.rearrange("(t p) q -> p t q", p=128)[:, 8*b:8*b+8, 128*s:128*(s+1)],
                        in_=xTp[b][:].rearrange("p (j c) -> p j c", c=128))
                    ctx.inc(ins, f"xT_st{b}", 16)

            for b in range(2):
                @blk.tensor
                def _(t, s=s, b=b):
                    ctx.wait(t, "ln_xn")
                    ctx.wait(t, "xntr_ev")
                    for j in range(8):
                        dt = b * 8 + j
                        ins = t.transpose(pVT[:, 128*j:128*(j+1)], xnat[s % 2][:, 128*dt:128*(dt+1)], sidb[:, :])
                    ctx.inc(ins, "xntr_mm", 1)

                @blk.vector
                def _(v, s=s, b=b):
                    ctx.wait(v, "xntr_mm")
                    ins = v.tensor_copy(
                        out=xn[:].rearrange("p (t q) -> p t q", q=QL)[:, 8*b:8*b+8, 128*s:128*(s+1)],
                        in_=pVT[:].rearrange("p (j c) -> p j c", c=128))
                    ctx.inc(ins, "xntr_ev", 1)



        def slab_src(fg, d):
            if fg < 12:
                return wqbf[256*fg:256*(fg+1), 128*d:128*(d+1)]
            if fg < 24:
                return wkbf[256*(fg-12):256*(fg-11), 128*d:128*(d+1)]
            return wvbf[256*(fg-24):256*(fg-23), 128*d:128*(d+1)]

        @blk.sync
        def _(sy):
            ctx.agwait(sy, "aga", 1)
            for fg in range(32):
                for d in range(16):
                    i = fg * 16 + d
                    if i >= 4:
                        ctx.waitv(sy, "qkv_mmprog", i - 3)
                    ctx.inc(sy.dma_start(out=slab2[i % 4][:, :], in_=slab_src(fg, d), transpose=True), f"slab_ld{i % 4}", 16)

        for fg in range(32):
            @blk.tensor
            def _(t, fg=fg):
                if fg == 0:
                    ctx.wait(t, "xntr_ev")
                for d in range(16):
                    i = fg * 16 + d
                    ctx.waitv(t, f"slab_ld{i % 4}", (i // 4 + 1) * 16)
                    if d == 0:
                        ctx.wait(t, "qkv_ev")
                    for j in range(2):
                        for ch in range(2):
                            ins = t.matmul(pC[2*j + ch][:, :], slab2[i % 4][:, 128*j:128*(j+1)],
                                           xn[:, 1024*d + 512*ch:1024*d + 512*(ch+1)],
                                           start=(d == 0), stop=(d == 15))
                    ctx.inc(ins, "qkv_mmprog", 1)

            gf = 2 * fg
            if fg < 8 or (12 <= fg < 20):
                isq = fg < 8
                dsto = qT_o if isq else kT_o
                fwithin = gf if isq else gf - 24

                @blk.vector
                def _(v, isq=isq):
                    ctx.wait(v, "qkv_mmprog")
                    ctx.wait(v, "qk_st")
                    for j in range(2):
                        for ch in range(2):
                            col = j * QL + 512 * ch
                            if isq:
                                ins = v.tensor_scalar(out=qkstage[:, col:col+512], in0=pC[2*j+ch][:, :],
                                                      scalar1=RSN, scalar2=None, op0=ALU.mult)
                            else:
                                ins = v.tensor_copy(out=qkstage[:, col:col+512], in_=pC[2*j+ch][:, :])
                    ctx.inc(ins, "qkv_ev", 1)

                @blk.scalar
                def _(sc, dsto=dsto, fwithin=fwithin):
                    ctx.wait(sc, "qkv_ev")
                    ins = sc.dma_start(
                        out=dsto.rearrange("(t p) q -> p t q", p=128)[:, fwithin:fwithin+2, :],
                        in_=qkstage[:].rearrange("p (t q) -> p t q", q=QL))
                    ctx.inc(ins, "qk_st", 16)
            elif fg < 12 or (20 <= fg < 24):
                isq = fg < 12
                dsto = qT_o if isq else kT_o
                fwithin = gf if isq else gf - 24
                scl = RSR if isq else 1.0

                @blk.vector
                def _(v):
                    ctx.wait(v, "qkv_mmprog")
                    ctx.wait(v, "rope_out")
                    for j in range(2):
                        for ch in range(2):
                            ins = v.tensor_copy(out=ropesc[:, 1024*j + 512*ch:1024*j + 512*(ch+1)],
                                                in_=pC[2*j+ch][:, :])
                    ctx.inc(ins, "qkv_ev", 1)

                for j in range(2):
                    @blk.tensor
                    def _(t, j=j):
                        ctx.wait(t, "qkv_ev")
                        ctx.wait(t, "rope_yev")
                        for ch in range(2):
                            ins = t.matmul(pA[:, 512*ch:512*(ch+1)], sperm[:, :],
                                           ropesc[:, 1024*j + 512*ch:1024*j + 512*(ch+1)],
                                           start=True, stop=True)
                        ctx.inc(ins, "rope_y", 1)

                    @blk.vector
                    def _(v, j=j, scl=scl):
                        ctx.wait(v, "rope_y")
                        ctx.wait(v, "qk_st")
                        ctx.wait(v, "rope_out")
                        v.scalar_tensor_tensor(out=ropet1[:, 0:QL], in0=ropesc[:, 1024*j:1024*(j+1)],
                                               scalar=scl, in1=scos[:, :], op0=ALU.mult, op1=ALU.mult)
                        ins = v.scalar_tensor_tensor(out=ropet1[:, QL:2*QL], in0=pA[:, :],
                                                     scalar=scl, in1=ssin[:, :], op0=ALU.mult, op1=ALU.mult)
                        ctx.inc(ins, "rope_yev", 1)
                        ctx.wait(v, "rope_yev")
                        ins = v.tensor_add(out=qkstage[:, j*QL:(j+1)*QL], in0=ropet1[:, 0:QL], in1=ropet1[:, QL:2*QL])
                        ctx.inc(ins, "rope_out", 1)

                @blk.scalar
                def _(sc, dsto=dsto, fwithin=fwithin):
                    ctx.wait(sc, "rope_out")
                    ins = sc.dma_start(
                        out=dsto.rearrange("(t p) q -> p t q", p=128)[:, fwithin:fwithin+2, :],
                        in_=qkstage[:].rearrange("p (t q) -> p t q", q=QL))
                    ctx.inc(ins, "qk_st", 16)
            else:
                fv = 2 * (fg - 24)

                @blk.vector
                def _(v):
                    ctx.wait(v, "qkv_mmprog")
                    ctx.wait(v, "vT_ev")
                    for j in range(2):
                        for ch in range(2):
                            ins = v.tensor_copy(out=vTstage[:, 1024*j + 512*ch:1024*j + 512*(ch+1)],
                                                in_=pC[2*j+ch][:, :])
                    ctx.inc(ins, "qkv_ev", 1)

                for j in range(2):
                    @blk.tensor
                    def _(t, j=j):
                        ctx.wait(t, "qkv_ev")
                        ctx.wait(t, "vT_ev")
                        for s in range(8):
                            ins = t.transpose(pVT[:, 128*s:128*(s+1)],
                                              vTstage[:, 1024*j + 128*s:1024*j + 128*(s+1)], sidb[:, :])
                        ctx.inc(ins, "vT_mm", 1)

                    @blk.vector
                    def _(v, j=j):
                        ctx.wait(v, "vT_mm")
                        if j == 0:
                            ctx.wait(v, "vn_st")
                        ins = v.tensor_copy(
                            out=vnstage[:].rearrange("p (s c) -> p s c", c=256)[:, :, 128*j:128*(j+1)],
                            in_=pVT[:].rearrange("p (s c) -> p s c", c=128))
                        ctx.inc(ins, "vT_ev", 1)

                @blk.scalar
                def _(sc, fv=fv):
                    ctx.wait(sc, "vT_ev")
                    ins = sc.dma_start(
                        out=vnat_o.rearrange("(s p) d2 -> p s d2", p=128)[:, :, 128*fv:128*fv+256],
                        in_=vnstage[:].rearrange("p (s c) -> p s c", c=256))
                    ctx.inc(ins, "vn_st", 16)

        @blk.gpsimd
        def _(g):
            ctx.wait(g, "qk_st")
            ctx.wait(g, "vn_st")
            ctx.wait(g, "xT_st0")
            ctx.wait(g, "xT_st1")
    return


def rope_tables():
    inv_freq = 1.0 / (10000.0 ** (np.arange(0, DR, 2, dtype=np.float32) / DR))
    t = np.arange(S, dtype=np.float32)
    freqs = t[:, None] * inv_freq[None, :]
    return np.cos(freqs).astype(np.float32), np.sin(freqs).astype(np.float32)

def core_cos_sin(c):
    cos, sin = rope_tables()
    lo, hi = blocks_of(c)
    rows = np.concatenate([np.arange(lo*BLK, (lo+1)*BLK), np.arange(hi*BLK, (hi+1)*BLK)])
    cosT = cos[rows].T
    sinT = sin[rows].T
    crep = np.concatenate([cosT, cosT, cosT, cosT], 0)
    srep = np.concatenate([sinT, sinT, sinT, sinT], 0)
    return np.ascontiguousarray(crep), np.ascontiguousarray(srep)

def perm_matrix():
    P = np.zeros((128, 128), np.float32)
    for b in range(2):
        o = 64 * b
        for i in range(32):
            P[o + 32 + i, o + i] = -1.0
            P[o + i, o + 32 + i] = 1.0
    return P

def la_inmaps(inputs, prep):
    x = np.asarray(inputs["L0_attn_hidden_states"]).reshape(S, D)
    maps = []
    ident = np.eye(128, dtype=np.float32)
    identb = np.eye(128, dtype=np.float32).astype(BF)
    perm = perm_matrix()
    for c in range(NC):
        lo, hi = blocks_of(c)
        xr = np.concatenate([x[lo*BLK:(lo+1)*BLK], x[hi*BLK:(hi+1)*BLK]], 0)
        crep, srep = core_cos_sin(c)
        maps.append({
            "xrows": np.ascontiguousarray(xr),
            "wqbf": prep["L0_attn_Wq_bf"], "wkbf": prep["L0_attn_Wk_bf"], "wvbf": prep["L0_attn_Wv_bf"],
            "cosrep": crep, "sinrep": srep, "identf": ident, "identb": identb, "permf": perm,
        })
    return maps

def emit_lb(ctx, io):
    qT_i = io["qT_i"]
    kTfull = io["kTfull"]
    vfull = io["vfull"]
    xT_i = io["xT_i"]
    wobf = io["wobf"]

    xT1_o = io["xT1_o"]
    xn2_o = io["xn2_o"]

    NKT = 64

    if True:
        oT = ctx.sb("oT", [128, H0 * QL], BF16)
        onesb = ctx.sb("onesb", [128, 1], BF16)
        onesrow = ctx.sb("onesrow", [1, 128], F32)
        ones_col_f = ctx.sb("ones_col_f", [128, 1], F32)
        es2 = contextlib.ExitStack()
        sb2 = lambda n, s, d: es2.enter_context(ctx.nc.sbuf_tensor(ctx.prefix.replace(':', '_') + n, list(s), d))
        qn = sb2("qn", [128, H0 * QL], BF16)
        qr = sb2("qr", [64, H0 * QL], BF16)
        kn2 = [sb2(f"kn{i}", [128, NC * QL], BF16) for i in range(2)]
        kr2 = [sb2(f"kr{i}", [64, NC * QL], BF16) for i in range(2)]
        vh2 = [sb2(f"vh{i}", [128, NKT * 128], BF16) for i in range(2)]
        pp = [sb2(f"pp{i}", [128, QL], BF16) for i in range(2)]
        denr = sb2("denr", [1, QL], F32)
        dbc = sb2("dbc", [128, QL], F32)

        psA = [ctx.ps(f"psA{i}", [128, QL], F32) for i in range(2)]
        pav = ctx.ps("pav", [128, QL], F32)
        pden = [ctx.ps(f"pden{i}", [1, 512], F32) for i in range(2)]
        blk = ctx.block if ctx.block is not None else ctx.start_block()

        @blk.scalar
        def _(sc):
            ctx.inc(sc.dma_start(out=qn[:, :], in_=qT_i[0:H0*128, :].rearrange("(h p) q -> p h q", p=128)), "q_ld", 16)
            ctx.inc(sc.dma_start(out=qr[:, :], in_=qT_i[H0*128:QSZ, :].rearrange("(h r) q -> r h q", r=64)), "q_ld", 16)

        @blk.vector
        def _(v):
            v.memset(onesb[:, :], 1.0)
            v.memset(onesrow[:, :], 1.0)
            ctx.inc(v.memset(ones_col_f[:, :], 1.0), "ones", 1)

        for h in range(H0):
            @blk.sync
            def _(sy, h=h):
                if h == 0:
                    ctx.agwait(sy, "agk", 1)
                if h >= 2:
                    ctx.waitv(sy, "hdone", h - 1)
                for r in range(NC):
                    ctx.inc(sy.dma_start(out=kn2[h % 2][:, QL*r:QL*(r+1)],
                                         in_=kTfull[QSZ*r + 128*h:QSZ*r + 128*(h+1), :]), f"kvk{h % 2}", 16)
                    ctx.inc(sy.dma_start(out=kr2[h % 2][0:64, QL*r:QL*(r+1)],
                                         in_=kTfull[QSZ*r + 2048 + 64*h:QSZ*r + 2048 + 64*(h+1), :]), f"kvk{h % 2}", 16)

            @blk.gpsimd
            def _(g, h=h):
                if h == 0:
                    ctx.agwait(g, "agk", 2)
                if h >= 2:
                    ctx.waitv(g, "hdone", h - 1)
                for r in range(NC):
                    ctx.inc(g.dma_start(out=vh2[h % 2][:].rearrange("p (kt dv) -> p kt dv", dv=128)[:, 8*r:8*(r+1), :],
                                        in_=vfull[QL*r:QL*(r+1), 128*h:128*(h+1)].rearrange("(kt p) dv -> p kt dv", p=128)),
                            f"kvv{h % 2}", 16)

        for h in range(H0):
            @blk.tensor
            def _(t, h=h):
                if h == 0:
                    ctx.wait(t, "q_ld")
                    ctx.wait(t, "ones")
                ctx.waitv(t, f"kvk{h % 2}", (h // 2 + 1) * 16 * 16)
                ctx.waitv(t, f"kvv{h % 2}", (h // 2 + 1) * 8 * 16)
                if h > 0:
                    ctx.wait(t, "dbc_ev")
                    ctx.wait(t, "hdone")
                    ctx.wait(t, "dnr")
                kn, kr, vh = kn2[h % 2], kr2[h % 2], vh2[h % 2]
                xpbase = NKT * h

                def avden(t, kt):
                    st, sp = (kt == 0), (kt == NKT - 1)
                    for ch in range(2):
                        t.matmul(pav[:, 512*ch:512*(ch+1)], vh[:, 128*kt:128*(kt+1)],
                                 pp[kt % 2][:, 512*ch:512*(ch+1)], start=st, stop=sp)
                    for ch in range(2):
                        ins = t.matmul(pden[ch][0:1, :], onesb[:, 0:1],
                                       pp[kt % 2][:, 512*ch:512*(ch+1)], start=st, stop=sp)
                    ctx.inc(ins, "avd", 1)

                for kt in range(NKT):
                    if not (h == 0 and kt < 2):
                        ctx.waitv(t, "xp", xpbase + kt - 1)
                    for ch in range(2):
                        t.matmul(psA[kt % 2][:, 512*ch:512*(ch+1)], kn[:, 128*kt:128*(kt+1)],
                                 qn[:, QL*h + 512*ch:QL*h + 512*(ch+1)], start=True, stop=False)
                    for ch in range(2):
                        ins = t.matmul(psA[kt % 2][:, 512*ch:512*(ch+1)], kr[0:64, 128*kt:128*(kt+1)],
                                       qr[0:64, QL*h + 512*ch:QL*h + 512*(ch+1)], start=False, stop=True)
                    ctx.inc(ins, "scmm", 1)
                    if kt >= 1:
                        ctx.waitv(t, "xp", xpbase + kt)
                        avden(t, kt - 1)
                ctx.waitv(t, "xp", xpbase + NKT)
                avden(t, NKT - 1)

            @blk.scalar
            def _(sc, h=h):
                for kt in range(NKT):
                    xpbase = NKT * h
                    ctx.waitv(sc, "scmm", xpbase + kt + 1)
                    if not (h == 0 and kt < 2):
                        ctx.waitv(sc, "avd", xpbase + kt - 1)
                    ctx.inc(sc.activation(out=pp[kt % 2][:, :], in_=psA[kt % 2][:, :], func=AF.Exp), "xp", 1)

            @blk.vector
            def _(v, h=h):
                ctx.wait(v, "avd")
                for ch in range(2):
                    ins = v.reciprocal(out=denr[0:1, 512*ch:512*(ch+1)], in_=pden[ch][0:1, :])
                ctx.inc(ins, "dnr", 1)

            @blk.tensor
            def _(t, h=h):
                ctx.wait(t, "dnr")
                for ch in range(2):
                    ins = t.matmul(psA[0][:, 512*ch:512*(ch+1)], onesrow[0:1, :],
                                   denr[0:1, 512*ch:512*(ch+1)], start=True, stop=True)
                ctx.inc(ins, "bcmm", 1)

            @blk.vector
            def _(v, h=h):
                ctx.wait(v, "bcmm")
                ctx.inc(v.tensor_copy(out=dbc[:, :], in_=psA[0][:, :]), "dbc_ev", 1)
                ctx.wait(v, "dbc_ev")
                ctx.inc(v.tensor_mul(out=oT[:, QL*h:QL*(h+1)], in0=pav[:, :], in1=dbc[:, :]), "hdone", 1)

        es2.close()
        woslab2 = [ctx.sb(f"woslab{i}", [128, 256], BF16) for i in range(2)]
        xTd = ctx.sb("xTd", [128, 16 * QL], F32)
        xn2 = ctx.sb("xn2", [128, 16 * QL], BF16)
        sqbuf = [ctx.sb(f"sqb{i}", [128, QL], BF16) for i in range(2)]
        nsc = [ctx.sb(f"nsc{i}", [128, QL], F32) for i in range(2)]
        mbS = ctx.sb("mbS", [128, QL], F32)
        ibS = ctx.sb("ibS", [128, QL], F32)
        m_sb = ctx.sb("m_sb", [1, QL], F32)
        msq_sb = ctx.sb("msq_sb", [1, QL], F32)
        t_sb = ctx.sb("t_sb", [1, QL], F32)

        @blk.gpsimd
        def _(g):
            ctx.wait(g, "hdone")
            for dt in range(16):
                ctx.inc(g.dma_start(out=xTd[:, QL*dt:QL*(dt+1)], in_=xT_i[128*dt:128*(dt+1), :]), "x_ld", 16)

        @blk.scalar
        def _(sc):
            ctx.agwait(sc, "aga", 1)
            ctx.wait(sc, "hdone")
            for hg in range(8):
                for vt in range(16):
                    i = hg * 16 + vt
                    if i >= 2:
                        ctx.waitv(sc, "proj_mm", i - 1)
                    ctx.inc(sc.dma_start(out=woslab2[i % 2][:, :],
                                         in_=wobf[256*hg:256*(hg+1), 128*vt:128*(vt+1)], transpose=True),
                            f"woslab{i % 2}", 16)

        for hg in range(8):
            @blk.tensor
            def _(t, hg=hg):
                if hg == 0:
                    ctx.wait(t, "hdone")
                for vt in range(16):
                    i = hg * 16 + vt
                    ctx.waitv(t, f"woslab{i % 2}", (i // 2 + 1) * 16)
                    if vt == 0:
                        ctx.wait(t, "proj_ev")
                    for j in range(2):
                        for ch in range(2):
                            ins = t.matmul(psA[j][:, 512*ch:512*(ch+1)], woslab2[i % 2][:, 128*j:128*(j+1)],
                                           oT[:, QL*vt + 512*ch:QL*vt + 512*(ch+1)],
                                           start=(vt == 0), stop=(vt == 15))
                    ctx.inc(ins, "proj_mm", 1)

            @blk.vector
            def _(v, hg=hg):
                ctx.wait(v, "proj_mm")
                ctx.waitv(v, "x_ld", 256)
                for j in range(2):
                    dt = 2 * hg + j
                    ins = v.tensor_add(out=xTd[:, QL*dt:QL*(dt+1)], in0=xTd[:, QL*dt:QL*(dt+1)], in1=psA[j][:, :])
                ctx.inc(ins, "proj_ev", 1)

        @blk.gpsimd
        def _(g):
            ctx.wait(g, "proj_ev")
            ctx.inc(g.dma_start(out=xT1_o.rearrange("(t p) q -> p t q", p=128),
                                in_=xTd[:].rearrange("p (t q) -> p t q", q=QL)), "x1_st", 16)

        @blk.vector
        def _(v):
            ctx.wait(v, "proj_ev")
            ctx.inc(v.memset(nsc[0][:, 0:4], 0.0), "ln2_start", 1)

        psum4 = [psA[0][:, 0:512], psA[0][:, 512:QL], psA[1][:, 0:512], psA[1][:, 512:QL]]
        emit_ln_featmajor(ctx, blk, xTd, xn2, psum4, sqbuf, nsc, mbS, ibS,
                          dict(m=m_sb, msq=msq_sb, t=t_sb),
                          dict(ones_col_f=ones_col_f, ones_col_b=onesb, ones_row_f=onesrow),
                          "ln2")

        @blk.gpsimd
        def _(g):
            ctx.wait(g, "ln2_done")
            ctx.inc(g.dma_start(out=xn2_o.rearrange("(t p) q -> p t q", p=128),
                                in_=xn2[:].rearrange("p (t q) -> p t q", q=QL)), "xn2_st", 16)
            ctx.wait(g, "xn2_st")
            ctx.wait(g, "x1_st")
    return


def lb_inmaps(la_results, prep):
    kTfull = np.ascontiguousarray(np.concatenate([r["kT_o"].view(BF) for r in la_results], 0))
    vfull = np.ascontiguousarray(np.concatenate([r["vnat_o"].view(BF) for r in la_results], 0))
    maps = []
    for c in range(NC):
        maps.append({
            "qT_i": la_results[c]["qT_o"].view(BF),
            "kTfull": kTfull, "vfull": vfull,
            "xT_i": la_results[c]["xT_o"],
            "wobf": prep["L0_attn_Wo_bf"],
        })
    return maps

EPS32 = float(np.finfo(np.float32).eps)

def emit_lc(ctx, io):
    xn2_i = io["xn2_i"]
    xT1_i = io["xT1_i"]
    wgu = io["wgu"]
    wproj = io["wproj"]

    xT2_o = io["xT2_o"]
    xn3_o = io["xn3_o"]

    NHT = HID0 // 128
    GS = 4
    NG = NHT // GS

    if True:
        acc = ctx.sb("acc", [128, 16 * QL], F32)
        hgrp = ctx.sb("hgrp", [128, GS * QL], BF16)
        onesb = ctx.sb("onesb", [128, 1], BF16)
        onesrow = ctx.sb("onesrow", [1, 128], F32)
        ones_col_f = ctx.sb("ones_col_f", [128, 1], F32)
        sg = ctx.sb("sg", [128, QL], BF16)
        sqt = ctx.sb("sqt", [128, QL], BF16)
        ibS = ctx.sb("ibS", [128, QL], F32)
        mbS = ctx.sb("mbS", [128, QL], F32)
        m_sb = ctx.sb("m_sb", [1, QL], F32)
        msq_sb = ctx.sb("msq_sb", [1, QL], F32)
        t_sb = ctx.sb("t_sb", [1, QL], F32)
        es2 = contextlib.ExitStack()
        sb2 = lambda n, s, d: es2.enter_context(ctx.nc.sbuf_tensor(ctx.prefix.replace(':', '_') + n, list(s), d))
        xn2 = sb2("xn2", [128, 16 * QL], BF16)
        gslab = sb2("gslab", [128, 16 * 512], BF16)
        uslab = sb2("uslab", [128, 16 * 512], BF16)
        pslab = sb2("pslab", [128, GS * D], BF16)

        pG = ctx.ps("pG", [128, QL], F32)
        pU = ctx.ps("pU", [128, QL], F32)
        pP = ctx.ps("pP", [128, QL], F32)
        pS = [ctx.ps(f"pS{i}", [1, 512], F32) for i in range(2)]
        blk = ctx.block if ctx.block is not None else ctx.start_block()

        @blk.vector
        def _(v):
            v.memset(onesb[:, :], 1.0)
            v.memset(onesrow[:, :], 1.0)
            ctx.inc(v.memset(ones_col_f[:, :], 1.0), "ones", 1)

        @blk.gpsimd
        def _(g):
            ctx.inc(g.dma_start(out=xn2[:].rearrange("p (t q) -> p t q", q=QL),
                                in_=xn2_i.rearrange("(t p) q -> p t q", p=128)), "xn2_ld", 16)

        @blk.sync
        def _(sy):
            ctx.agwait(sy, "agw", 2)
            for g in range(NG):
                if g >= 1:
                    ctx.waitv(sy, "up_mm", g * GS)
                for d in range(16):
                    ctx.inc(sy.dma_start(out=gslab[:, 512*d:512*(d+1)],
                                         in_=wgu[512*g:512*(g+1), 128*d:128*(d+1)], transpose=True), "gsl", 16)
                    ctx.inc(sy.dma_start(out=uslab[:, 512*d:512*(d+1)],
                                         in_=wgu[HID0 + 512*g:HID0 + 512*(g+1), 128*d:128*(d+1)], transpose=True), "usl", 16)
                if g >= 1:
                    ctx.waitv(sy, "proj_mm", g * 32)
                for htj in range(GS):
                    ctx.inc(sy.dma_start(out=pslab[:, D*htj:D*(htj+1)],
                                         in_=wproj[:, 512*g + 128*htj:512*g + 128*(htj+1)], transpose=True), "psl", 16)

        for g in range(NG):
            for htj in range(GS):
                ht = GS * g + htj

                @blk.tensor
                def _(t, g=g, htj=htj, ht=ht):
                    if ht == 0:
                        ctx.wait(t, "xn2_ld")
                        ctx.wait(t, "ones")
                    if htj == 0:
                        ctx.waitv(t, "gsl", (g + 1) * 16 * 16)
                        ctx.waitv(t, "usl", (g + 1) * 16 * 16)
                    ctx.wait(t, "silu")
                    for d in range(16):
                        for ch in range(2):
                            ins = t.matmul(pG[:, 512*ch:512*(ch+1)],
                                           gslab[:, 512*d + 128*htj:512*d + 128*(htj+1)],
                                           xn2[:, 1024*d + 512*ch:1024*d + 512*(ch+1)],
                                           start=(d == 0), stop=(d == 15))
                    ctx.inc(ins, "gate_mm", 1)
                    ctx.wait(t, "hmul")
                    for d in range(16):
                        for ch in range(2):
                            ins = t.matmul(pU[:, 512*ch:512*(ch+1)],
                                           uslab[:, 512*d + 128*htj:512*d + 128*(htj+1)],
                                           xn2[:, 1024*d + 512*ch:1024*d + 512*(ch+1)],
                                           start=(d == 0), stop=(d == 15))
                    ctx.inc(ins, "up_mm", 1)

                @blk.scalar
                def _(sc):
                    ctx.wait(sc, "gate_mm")
                    ctx.wait(sc, "hmul")
                    ctx.inc(sc.activation(out=sg[:, :], in_=pG[:, :], func=AF.Silu), "silu", 1)

                @blk.vector
                def _(v, htj=htj, ht=ht):
                    ctx.wait(v, "up_mm")
                    ctx.wait(v, "silu")
                    ins = v.tensor_mul(out=hgrp[:, QL*htj:QL*(htj+1)], in0=pU[:, :], in1=sg[:, :])
                    ctx.inc(ins, "hmul", 1)
                    ctx.wait(v, "hmul")
                    ctx.inc(v.tensor_mul(out=sqt[:, :], in0=hgrp[:, QL*htj:QL*(htj+1)],
                                         in1=hgrp[:, QL*htj:QL*(htj+1)]), "hsq", 1)

                @blk.tensor
                def _(t, ht=ht):
                    ctx.wait(t, "hsq")
                    st, sp = (ht == 0), (ht == NHT - 1)
                    for ch in range(2):
                        ins = t.matmul(pS[ch][0:1, :], onesb[:, 0:1], sqt[:, 512*ch:512*(ch+1)],
                                       start=st, stop=sp)
                    ctx.inc(ins, "stat_mm", 1)

            @blk.tensor
            def _(t, g=g):
                ctx.waitv(t, "psl", (g + 1) * GS * 16)
                ctx.wait(t, "hmul")
                for dt in range(16):
                    for ch in range(2):
                        i = g * 32 + dt * 2 + ch
                        if i >= 2:
                            ctx.waitv(t, "acc_add", i - 1)
                        for htj in range(GS):
                            ins = t.matmul(pP[:, 512*(i % 2):512*(i % 2 + 1)],
                                           pslab[:, D*htj + 128*dt:D*htj + 128*(dt+1)],
                                           hgrp[:, QL*htj + 512*ch:QL*htj + 512*(ch+1)],
                                           start=(htj == 0), stop=(htj == GS - 1))
                        ctx.inc(ins, "proj_mm", 1)

            @blk.vector
            def _(v, g=g):
                for dt in range(16):
                    for ch in range(2):
                        i = g * 32 + dt * 2 + ch
                        ctx.waitv(v, "proj_mm", i + 1)
                        if g == 0:
                            ins = v.tensor_copy(out=acc[:, QL*dt + 512*ch:QL*dt + 512*(ch+1)],
                                                in_=pP[:, 512*(i % 2):512*(i % 2 + 1)])
                        else:
                            ins = v.tensor_add(out=acc[:, QL*dt + 512*ch:QL*dt + 512*(ch+1)],
                                               in0=acc[:, QL*dt + 512*ch:QL*dt + 512*(ch+1)],
                                               in1=pP[:, 512*(i % 2):512*(i % 2 + 1)])
                        ctx.inc(ins, "acc_add", 1)

        es2.close()
        xn3 = ctx.sb("xn3", [128, 16 * QL], BF16)
        sqbuf = [ctx.sb(f"sqb{i}", [128, QL], BF16) for i in range(2)]
        nsc = [ctx.sb(f"nsc{i}", [128, QL], F32) for i in range(2)]
        x1d = [ctx.sb(f"x1d{i}", [128, QL], F32) for i in range(2)]

        @blk.vector
        def _(v):
            ctx.wait(v, "stat_mm")
            for ch in range(2):
                ins = v.tensor_scalar(out=m_sb[0:1, 512*ch:512*(ch+1)], in0=pS[ch][0:1, :],
                                      scalar1=1.0 / HID0, scalar2=EPS32, op0=ALU.mult, op1=ALU.add)
            ctx.inc(ins, "rms_m", 1)

        @blk.scalar
        def _(sc):
            ctx.wait(sc, "rms_m")
            ctx.inc(sc.activation(out=t_sb[0:1, :], in_=m_sb[0:1, :], func=AF.Sqrt), "rms_sd", 1)

        @blk.vector
        def _(v):
            ctx.wait(v, "rms_sd")
            ctx.inc(v.reciprocal(out=msq_sb[0:1, :], in_=t_sb[0:1, :]), "rms_inv", 1)

        @blk.tensor
        def _(t):
            ctx.wait(t, "rms_inv")
            ctx.wait(t, "acc_add")
            for ch in range(2):
                ins = t.matmul(pG[:, 512*ch:512*(ch+1)], onesrow[0:1, :],
                               msq_sb[0:1, 512*ch:512*(ch+1)], start=True, stop=True)
            ctx.inc(ins, "rms_bc", 1)

        @blk.vector
        def _(v):
            ctx.wait(v, "rms_bc")
            ctx.inc(v.tensor_copy(out=ibS[:, :], in_=pG[:, :]), "rms_bcev", 1)

        @blk.gpsimd
        def _(g):
            ctx.wait(g, "acc_add")
            for dt in range(16):
                if dt >= 2:
                    ctx.waitv(g, "resid", dt - 1)
                ctx.inc(g.dma_start(out=x1d[dt % 2][:, :], in_=xT1_i[128*dt:128*(dt+1), :]), f"x1l{dt % 2}", 16)

        for dt in range(16):
            @blk.vector
            def _(v, dt=dt):
                if dt == 0:
                    ctx.wait(v, "rms_bcev")
                ctx.waitv(v, f"x1l{dt % 2}", (dt // 2 + 1) * 16)
                ctx.inc(v.tensor_mul(out=nsc[dt % 2][:, :], in0=acc[:, QL*dt:QL*(dt+1)], in1=ibS[:, :]), "vv", 1)
                ctx.wait(v, "vv")
                ctx.inc(v.tensor_add(out=acc[:, QL*dt:QL*(dt+1)], in0=nsc[dt % 2][:, :], in1=x1d[dt % 2][:, :]),
                        "resid", 1)

        @blk.scalar
        def _(sc):
            ctx.wait(sc, "resid")
            ctx.inc(sc.dma_start(out=xT2_o.rearrange("(t p) q -> p t q", p=128),
                                 in_=acc[:].rearrange("p (t q) -> p t q", q=QL)), "x2_st", 16)

        @blk.vector
        def _(v):
            ctx.wait(v, "resid")
            ctx.inc(v.memset(nsc[0][:, 0:4], 0.0), "ln3_start", 1)

        psum4 = [pG[:, 0:512], pG[:, 512:QL], pU[:, 0:512], pU[:, 512:QL]]
        emit_ln_featmajor(ctx, blk, acc, xn3, psum4, sqbuf, nsc, mbS, ibS,
                          dict(m=m_sb, msq=msq_sb, t=t_sb),
                          dict(ones_col_f=ones_col_f, ones_col_b=onesb, ones_row_f=onesrow),
                          "ln3")

        @blk.gpsimd
        def _(g):
            ctx.wait(g, "ln3_done")
            ctx.inc(g.dma_start(out=xn3_o.rearrange("(t p) q -> p t q", p=128),
                                in_=xn3[:].rearrange("p (t q) -> p t q", q=QL)), "xn3_st", 16)
            ctx.wait(g, "xn3_st")
            ctx.wait(g, "x2_st")
    return


def lc_inmaps(lb_results, prep):
    return [{
        "xn2_i": lb_results[c]["xn2_o"].view(BF),
        "xT1_i": lb_results[c]["xT1_o"],
        "wgu": prep["wgu_hat"],
        "wproj": prep["wproj_hat"],
    } for c in range(NC)]

def emit_ld(ctx, io):
    xn3_i = io["xn3_i"]
    wq1 = io["wq1"]
    wk1 = io["wk1"]
    wv1 = io["wv1"]
    selc = io["selc"]

    pqext_o = io["pqext_o"]
    pkext_o = io["pkext_o"]
    pvext_o = io["pvext_o"]

    if True:
        xn3 = ctx.sb("xn3", [128, 16 * QL], BF16)
        wq1s = ctx.sb("wq1s", [128, 16 * 256], BF16)
        wk1s = ctx.sb("wk1s", [128, 16 * 256], BF16)
        wv1s = ctx.sb("wv1s", [128, 16 * 256], BF16)
        selb = ctx.sb("selb", [128, 16], BF16)
        pq = ctx.sb("pq", [128, 2 * QL], BF16)
        pk2 = ctx.sb("pk2", [128, 2 * QL], BF16)
        rk = ctx.sb("rk", [16, 2 * QL], BF16)
        pvext = ctx.sb("pvext", [128, 8 * 288], BF16)
        zb = ctx.sb("zb", [128, 8 * 512], BF16)
        m1 = ctx.sb("m1", [32, QL], BF16)
        pQ = ctx.ps("pQ", [128, QL], F32)
        pRK = [ctx.ps(f"pRK{i}", [16, 512], F32) for i in range(2)]
        pV = ctx.ps("pV", [128, 256], F32)
        blk = ctx.block if ctx.block is not None else ctx.start_block()

        @blk.sync
        def _(sy):
            ctx.agwait(sy, "aga", 1)
            ctx.inc(sy.dma_start(out=xn3[:].rearrange("p (t q) -> p t q", q=QL),
                                 in_=xn3_i.rearrange("(t p) q -> p t q", p=128)), "ld0", 16)
            ctx.inc(sy.dma_start(out=wq1s[:].rearrange("p (t f) -> p t f", f=256),
                                 in_=wq1.rearrange("(t p) f -> p t f", p=128)), "ld0", 16)
            ctx.inc(sy.dma_start(out=wk1s[:].rearrange("p (t f) -> p t f", f=256),
                                 in_=wk1.rearrange("(t p) f -> p t f", p=128)), "ld0", 16)
            ctx.inc(sy.dma_start(out=wv1s[:].rearrange("p (t f) -> p t f", f=256),
                                 in_=wv1.rearrange("(t p) f -> p t f", p=128)), "ld0", 16)
            ctx.inc(sy.dma_start(out=selb[:, :], in_=selc[:, :]), "ld0", 16)

        @blk.vector
        def _(v):
            ctx.inc(v.memset(zb[:, :], 0.0), "csts", 1)
            ctx.inc(v.memset(m1[:, :], -1.0), "csts", 1)
            ctx.inc(v.memset(pvext[:].rearrange("p (s h n) -> p s h n", h=32, n=9)[:, :, :, 8:9], 1.0), "csts", 1)

        for mat, wst, name in [("q", None, "pq"), ("k", None, "pk")]:
            ws = wq1s if mat == "q" else wk1s
            for f in range(2):
                @blk.tensor
                def _(t, ws=ws, f=f, mat=mat):
                    if mat == "q" and f == 0:
                        ctx.wait(t, "ld0")
                        ctx.wait(t, "csts")
                    ctx.wait(t, "sig")
                    for d in range(16):
                        for ch in range(2):
                            ins = t.matmul(pQ[:, 512*ch:512*(ch+1)], ws[:, 256*d + 128*f:256*d + 128*(f+1)],
                                           xn3[:, 1024*d + 512*ch:1024*d + 512*(ch+1)],
                                           start=(d == 0), stop=(d == 15))
                    ctx.inc(ins, "qk_mm", 1)

                @blk.scalar
                def _(sc, f=f, mat=mat):
                    ctx.wait(sc, "qk_mm")
                    if mat == "q":
                        ins = sc.activation(out=pq[:, QL*f:QL*(f+1)], in_=pQ[:, :], func=AF.Sigmoid)
                        ctx.inc(ins, "sig", 1)
                    else:
                        ins = sc.activation(out=pk2[:, QL*f:QL*(f+1)], in_=pQ[:, :], func=AF.Sigmoid)
                        ctx.inc(ins, "sig", 1)

                if mat == "k":
                    @blk.tensor
                    def _(t, f=f):
                        ctx.wait(t, "sig")
                        if f == 1:
                            ctx.wait(t, "rk_ev")
                        for ch in range(2):
                            ins = t.matmul(pRK[ch][0:16, :], selb[:, :],
                                           pk2[:, QL*f + 512*ch:QL*f + 512*(ch+1)], start=True, stop=True)
                        ctx.inc(ins, "rk_mm", 1)

                    @blk.vector
                    def _(v, f=f):
                        ctx.wait(v, "rk_mm")
                        for ch in range(2):
                            ins = v.tensor_copy(out=rk[0:16, QL*f + 512*ch:QL*f + 512*(ch+1)], in_=pRK[ch][0:16, :])
                            ctx.inc(ins, "rk_ev", 1)
                        ctx.wait(v, "rk_ev")
                        ins = v.tensor_scalar(out=pk2[:, QL*f:QL*(f+1)], in0=pk2[:, QL*f:QL*(f+1)],
                                              scalar1=2.0, scalar2=None, op0=ALU.mult)
                        ctx.inc(ins, "pkscaled", 1)

        for s in range(8):
            @blk.tensor
            def _(t, s=s):
                if s == 0:
                    ctx.wait(t, "ld0")
                ctx.wait(t, "pv_ev")
                for d in range(16):
                    ins = t.matmul(pV[:, :], xn3[:, 1024*d + 128*s:1024*d + 128*(s+1)],
                                   wv1s[:, 256*d:256*(d+1)], start=(d == 0), stop=(d == 15))
                ctx.inc(ins, "pv_mm", 1)

            @blk.scalar
            def _(sc, s=s):
                ctx.wait(sc, "pv_mm")
                ctx.wait(sc, "csts")
                ins = sc.activation(
                    out=pvext[:].rearrange("p (s2 h n) -> p s2 h n", h=32, n=9)[:, s, :, 0:8],
                    in_=pV[:].rearrange("p (h n) -> p h n", n=8), func=AF.Sigmoid)
                ctx.inc(ins, "pv_ev", 1)

        @blk.gpsimd
        def _(g):
            ctx.wait(g, "csts")
            for b in range(16):
                ctx.inc(g.dma_start(out=pqext_o[128*b:128*(b+1), :], in_=zb[:, 0:512]), "zst", 16)
            for hf in range(2):
                for b in range(8):
                    ctx.inc(g.dma_start(out=pkext_o[QL*hf + 128*b:QL*hf + 128*(b+1), :], in_=zb[:, 0:512]), "zst", 16)
            ctx.wait(g, "zst")
            ctx.wait(g, "sig")
            for qc in range(2):
                for h in range(32):
                    f, hh = h // 16, h % 16
                    ctx.inc(g.dma_start(out=pqext_o[QL*qc + 32*h:QL*qc + 32*h+8, :],
                                        in_=pq[8*hh:8*hh+8, QL*f + 512*qc:QL*f + 512*(qc+1)]), "ast", 16)
                ctx.inc(g.dma_start(out=pqext_o[QL*qc:QL*(qc+1), :].rearrange("(h r) k -> h r k", r=32)[:, 8:9, :],
                                    in_=m1[0:32, 0:512].unsqueeze(1)), "ast", 16)
            ctx.wait(g, "pkscaled")
            for hf in range(2):
                for h in range(32):
                    f, hh = h // 16, h % 16
                    ctx.inc(g.dma_start(out=pkext_o[QL*hf + 32*h:QL*hf + 32*h+8, :],
                                        in_=pk2[8*hh:8*hh+8, QL*f + 512*hf:QL*f + 512*(hf+1)]), "ast", 16)
                for f in range(2):
                    ctx.inc(g.dma_start(
                        out=pkext_o[QL*hf:QL*(hf+1), :].rearrange("(h r) k -> h r k", r=32)[16*f:16*(f+1), 8:9, :],
                        in_=rk[0:16, QL*f + 512*hf:QL*f + 512*(hf+1)].unsqueeze(1)), "ast", 16)
            ctx.wait(g, "pv_ev")
            ctx.inc(g.dma_start(out=pvext_o.rearrange("(s p) e -> p s e", p=128),
                                in_=pvext[:].rearrange("p (s e) -> p s e", e=288)), "ast", 16)
            ctx.wait(g, "ast")
    return


def sel_const():
    s = np.zeros((128, 16), dtype=BF)
    for hh in range(16):
        s[8*hh:8*(hh+1), hh] = 1.0
    return s

def ld_inmaps(lc_results, prep):
    sc = sel_const()
    return [{
        "xn3_i": lc_results[c]["xn3_o"].view(BF),
        "wq1": prep["L1_attn_Wq_bf"], "wk1": prep["L1_attn_Wk_bf"], "wv1": prep["L1_attn_Wv_bf"],
        "selc": sc,
    } for c in range(NC)]

def emit_le(ctx, io):
    pqext_i = io["pqext_i"]
    pkext_own = io["pkext_own"]
    pvext_own = io["pvext_own"]
    pkext_all = io["pkext_all"]
    pvext_all = io["pvext_all"]
    idxs_i = io["idxs_i"]
    mtab_i = io["mtab_i"]
    trirep_i = io["trirep_i"]
    econst_i = io["econst_i"]
    wo1s_i = io["wo1s_i"]
    wo1p_i = io["wo1p_i"]
    ve0_i = io["ve0_i"]
    wg_i = io["wg_i"]
    xn3_i = io["xn3_i"]
    xT2_i = io["xT2_i"]

    xT3_o = io["xT3_o"]
    xn4_o = io["xn4_o"]

    numdram = ctx.dint("numdram", [2 * 256, 512], F32)
    dendram = ctx.dint("dendram", [2 * 32, 512], F32)

    if True:
        x3 = ctx.sb("x3", [128, 16 * QL], F32)
        oT = ctx.sb("oT", [128, 4 * 512], BF16)
        wo1s = ctx.sb("wo1s", [128, 2 * D], BF16)
        biasS = ctx.sb("biasS", [128, 16], F32)
        xn3 = ctx.sb("xn3", [128, 16 * QL], BF16)
        onesb = ctx.sb("onesb", [128, 1], BF16)
        onesrow = ctx.sb("onesrow", [1, 128], F32)
        ones_col_f = ctx.sb("ones_col_f", [128, 1], F32)
        esA = contextlib.ExitStack()
        sbA = lambda n, s, d: esA.enter_context(ctx.nc.sbuf_tensor(ctx.prefix.replace(':', '_') + n, list(s), d))
        wo1p = sbA("wo1p", [128, 2 * D], BF16)
        ve0s = sbA("ve0s", [128, 2], BF16)
        acc = [sbA(f"acc{i}", [128, 8 * 512], F32) for i in range(2)]
        pkslot = [sbA(f"pkslot{i}", [128, 8 * 512], BF16) for i in range(2)]
        pqslot = [sbA(f"pqslot{i}", [128, 8 * 512], BF16) for i in range(2)]
        pvslot = [sbA(f"pvslot{i}", [128, 4 * 288 + 32], BF16) for i in range(2)]
        ppt = sbA("ppt", [128, 4 * 2048], BF16)
        sidx = sbA("sidx", [128, 15 * 20], I32)
        mtab = sbA("mtab", [128, 15 * 2], F32)
        trirep = sbA("trirep", [128, 4 * 512], BF16)

        psch = [ctx.ps(f"psch{i}", [128, 1024], F32) for i in range(2)]
        pAV = [ctx.ps(f"pAV{i}", [128, 512], F32) for i in range(2)]
        pW = ctx.ps("pW", [128, QL], F32)
        blk = ctx.block if ctx.block is not None else ctx.start_block()

        @blk.sync
        def _(sy):
            ctx.agwait(sy, "aga", 1)
            ctx.agwait(sy, "agw", 3)
            ctx.inc(sy.dma_start(out=sidx[:, :], in_=idxs_i[:, :]), "c_ld", 16)
            ctx.inc(sy.dma_start(out=mtab[:, :], in_=mtab_i[:, :]), "c_ld", 16)
            ctx.inc(sy.dma_start(out=trirep[:].rearrange("p (s q) -> p s q", q=512),
                                 in_=trirep_i[:, 0:512].rearrange("(s p) q -> p s q", p=128)), "c_ld", 16)
            ctx.inc(sy.dma_start(out=xn3[:].rearrange("p (t q) -> p t q", q=QL),
                                 in_=xn3_i.rearrange("(t p) q -> p t q", p=128)), "c_ld", 16)
            ctx.inc(sy.dma_start(out=wo1s[:].rearrange("p (t f) -> p t f", f=D),
                                 in_=wo1s_i.rearrange("(t p) f -> p t f", p=128)), "c_ld", 16)
            ctx.inc(sy.dma_start(out=wo1p[:].rearrange("p (t f) -> p t f", f=D),
                                 in_=wo1p_i.rearrange("(t p) f -> p t f", p=128)), "c_ld", 16)
            ctx.inc(sy.dma_start(out=ve0s[:, :], in_=ve0_i[:, :]), "c_ld", 16)

        @blk.vector
        def _(v):
            v.memset(onesb[:, :], 1.0)
            v.memset(onesrow[:, :], 1.0)
            v.memset(ones_col_f[:, :], 1.0)
            v.memset(acc[0][:, :], 0.0)
            v.memset(pvslot[0][:, 1152:1184], 0.0)
            v.memset(pvslot[1][:, 1152:1184], 0.0)
            ctx.inc(v.memset(acc[1][:, :], 0.0), "acc0", 1)

        @blk.tensor
        def _(t):
            ctx.wait(t, "c_ld")
            for ht in range(16):
                for vt in range(2):
                    ins = t.matmul(pW[:, ht:ht+1], wo1p[:, D*vt + 128*ht:D*vt + 128*(ht+1)],
                                   ve0s[:, vt:vt+1], start=(vt == 0), stop=(vt == 1))
            ctx.inc(ins, "bias_mm", 1)

        @blk.vector
        def _(v):
            ctx.wait(v, "bias_mm")
            ctx.inc(v.tensor_copy(out=biasS[:, :], in_=pW[:, 0:16]), "bias_ev", 1)

        from concourse.bass_core_rust import IndirectOffsetOnAxis
        eidx = [0]
        aidx = [0]

        def slot_loads(t):
            @blk.gpsimd
            def _(g, t=t):
                if t == 0:
                    ctx.waitv(g, "c_ld", 112)
                if t == 2:
                    ctx.agwait(g, "agk", 4)
                if t >= 2:
                    ctx.waitv(g, "avadd", max(0, (t - 1) * 8))
                if t < 2:
                    hf = t
                    ctx.inc(g.dma_start(out=pkslot[t % 2][:].rearrange("p (a k) -> p a k", k=512),
                                        in_=pkext_own[QL*hf:QL*(hf+1), :].rearrange("(a p) k -> p a k", p=128)),
                            f"sl{t % 2}", 16)
                    ctx.inc(g.dma_start(out=pqslot[t % 2][:].rearrange("p (a k) -> p a k", k=512),
                                        in_=pqext_i[QL*hf:QL*(hf+1), :].rearrange("(a p) k -> p a k", p=128)),
                            f"sl{t % 2}", 16)
                    ctx.inc(g.dma_start(out=pvslot[t % 2][:, 0:1152].rearrange("p (a e) -> p a e", e=288),
                                        in_=pvext_own[512*hf:512*(hf+1), :].rearrange("(a p) e -> p a e", p=128)),
                            f"sl{t % 2}", 16)
                else:
                    ts = t - 2
                    for j in range(8):
                        ctx.inc(g.indirect_dma_start(
                            out=pkslot[t % 2][:, 512*j:512*(j+1)], out_offset=None, in_=pkext_all[:, :],
                            in_offset=IndirectOffsetOnAxis(sidx[:, 20*ts + j:20*ts + j + 1], 0)), f"sl{t % 2}", 16)
                    for j in range(8):
                        ctx.inc(g.indirect_dma_start(
                            out=pqslot[t % 2][:, 512*j:512*(j+1)], out_offset=None, in_=pqext_i[:, :],
                            in_offset=IndirectOffsetOnAxis(sidx[:, 20*ts + 8 + j:20*ts + 9 + j], 0)), f"sl{t % 2}", 16)
                    for j in range(4):
                        ctx.inc(g.indirect_dma_start(
                            out=pvslot[t % 2][:, 288*j:288*(j+1)], out_offset=None, in_=pvext_all[:, :],
                            in_offset=IndirectOffsetOnAxis(sidx[:, 20*ts + 16 + j:20*ts + 17 + j], 0)), f"sl{t % 2}", 16)

        def slot_compute(t):
            ldtotal = ctx.val(f"sl{t % 2}")
            for g in range(8):
                gidx = aidx[0] // 8 * 8 + g
                gi = t * 8 + g

                @blk.tensor
                def _(te, t=t, g=g, gi=gi, ldtotal=ldtotal):
                    if g == 0:
                        ctx.waitv(te, f"sl{t % 2}", ldtotal)
                        if t <= 1:
                            ctx.wait(te, "acc0")
                        if t == 0:
                            ctx.wait(te, "bias_ev")
                    for ks in range(4):
                        for hh in range(2):
                            ctx.waitv(te, "exp", max(0, eidx[0] + 8*g + 2*ks + hh - 1))
                            for j in (2*hh, 2*hh + 1):
                                ins = te.matmul(psch[hh][:, 512*(j - 2*hh):512*(j - 2*hh) + 512],
                                                pkslot[t % 2][32*j:32*(j+1), 512*g + 128*ks:512*g + 128*(ks+1)],
                                                pqslot[t % 2][32*j:32*(j+1), 512*g:512*(g+1)],
                                                start=True, stop=True, tile_position=(32*j, 0))
                            ctx.inc(ins, "sc_mm", 1)
                    ctx.waitv(te, "exp", eidx[0] + 8 * g + 8)
                    if gi >= 2:
                        ctx.waitv(te, "avadd", gi - 1)
                    for j in range(4):
                        for ks in range(4):
                            off = 288*ks + 36*g + 9*j
                            ins = te.matmul(pAV[gi % 2][32*j:32*(j+1), :],
                                            pvslot[t % 2][:, off:off+32],
                                            ppt[:, 2048*ks + 512*j:2048*ks + 512*(j+1)],
                                            start=(ks == 0), stop=(ks == 3),
                                            tile_position=(0, 32*j))
                    ctx.inc(ins, "av_grp", 1)

                @blk.scalar
                def _(sc, t=t, g=g, gi=gi):
                    for ks in range(4):
                        for hh in range(2):
                            e = eidx[0] + 8 * g + 2 * ks + hh
                            ctx.waitv(sc, "sc_mm", e + 1)
                            if gi >= 1 and ks == 0 and hh == 0:
                                ctx.waitv(sc, "av_grp", gi)
                            dst = ppt[:, 2048*ks + 1024*hh:2048*ks + 1024*(hh+1)]
                            ctx.inc(sc.activation(out=dst, in_=psch[hh][:, :], func=AF.Exp),
                                    "exp_pre" if t < 2 else "exp", 1)

                if t < 2:
                    @blk.vector
                    def _(v, t=t, g=g):
                        for ks in range(4):
                            for hh in range(2):
                                e = eidx[0] + 8 * g + 2 * ks + hh
                                ctx.waitv(v, "exp_pre", e + 1)
                                for j in (2*hh, 2*hh + 1):
                                    ins = v.tensor_mul(out=ppt[:, 2048*ks + 512*j:2048*ks + 512*(j+1)],
                                                       in0=ppt[:, 2048*ks + 512*j:2048*ks + 512*(j+1)],
                                                       in1=trirep[:, 512*ks:512*(ks+1)])
                                ctx.inc(ins, "exp", 1)

                @blk.vector
                def _(v, t=t, g=g, gi=gi):
                    ctx.waitv(v, "av_grp", gi + 1)
                    if t == 0:
                        ins = v.tensor_add(out=acc[0][:, 512*g:512*(g+1)], in0=acc[0][:, 512*g:512*(g+1)],
                                           in1=pAV[gi % 2][:, :])
                    elif t == 1:
                        ins = v.tensor_add(out=acc[1][:, 512*g:512*(g+1)], in0=acc[1][:, 512*g:512*(g+1)],
                                           in1=pAV[gi % 2][:, :])
                    else:
                        ts = t - 2
                        v.scalar_tensor_tensor(out=acc[0][:, 512*g:512*(g+1)], in0=pAV[gi % 2][:, :],
                                               scalar=mtab[:, 2*ts:2*ts+1], in1=acc[0][:, 512*g:512*(g+1)],
                                               op0=ALU.mult, op1=ALU.add)
                        ins = v.scalar_tensor_tensor(out=acc[1][:, 512*g:512*(g+1)], in0=pAV[gi % 2][:, :],
                                                     scalar=mtab[:, 2*ts+1:2*ts+2], in1=acc[1][:, 512*g:512*(g+1)],
                                                     op0=ALU.mult, op1=ALU.add)
                    ctx.inc(ins, "avadd", 1)
            eidx[0] += 64
            aidx[0] += 8

        for t in range(17):
            slot_loads(t)
            slot_compute(t)

        @blk.scalar
        def _(sc):
            ctx.wait(sc, "avadd")
            for qc in range(2):
                for j in range(4):
                    ctx.inc(sc.dma_start(
                        out=numdram.rearrange("(qc g j2 d2) k -> qc j2 d2 g k", g=8, j2=4, d2=8)[qc, j, :, :, :],
                        in_=acc[qc][32*j:32*j+8, :].rearrange("p (g k) -> p g k", k=512)), "bnc", 16)
                    ctx.inc(sc.dma_start(
                        out=dendram.rearrange("(qc j2 g) k -> qc j2 g k", j2=4, g=8)[qc, j, :, :].unsqueeze(0),
                        in_=acc[qc][32*j+8:32*j+9, :].rearrange("p (g k) -> p g k", k=512)), "bnc", 16)

        esA.close()
        gateT = ctx.sb("gateT", [128, 16 * QL], BF16)
        esB = contextlib.ExitStack()
        sbB = lambda n, s, d: esB.enter_context(ctx.nc.sbuf_tensor(ctx.prefix.replace(':', '_') + n, list(s), d))
        numT = sbB("numT", [128, 4 * 512], F32)
        rex = sbB("rex", [128, 4 * 512], F32)
        den32 = sbB("den32", [32, 2 * 512], F32)
        rec32 = sbB("rec32", [32, 2 * 512], F32)
        econ = sbB("econ", [32, 256], F32)

        @blk.scalar
        def _(sc):
            ctx.wait(sc, "bnc")
            ctx.inc(sc.dma_start(out=econ[:, :], in_=econst_i[:, :]), "rl", 16)
            for pt in range(2):
                for qc in range(2):
                    ctx.inc(sc.dma_start(out=numT[:, 512*(2*pt+qc):512*(2*pt+qc+1)],
                                         in_=numdram[256*qc + 128*pt:256*qc + 128*(pt+1), :]), "rl", 16)
            for qc in range(2):
                ctx.inc(sc.dma_start(out=den32[0:32, 512*qc:512*(qc+1)],
                                     in_=dendram[32*qc:32*(qc+1), :]), "rl", 16)

        @blk.vector
        def _(v):
            ctx.wait(v, "rl")
            ctx.inc(v.reciprocal(out=rec32[:, :], in_=den32[:, :]), "rcp", 1)

        @blk.tensor
        def _(t):
            ctx.wait(t, "rcp")
            for pt in range(2):
                if pt == 1:
                    ctx.waitv(t, "rex_ev", 1)
                for qc in range(2):
                    ins = t.matmul(pW[:, 512*qc:512*(qc+1)],
                                   econ[0:32, 128*pt:128*(pt+1)], rec32[0:32, 512*qc:512*(qc+1)],
                                   start=True, stop=True)
                    ctx.inc(ins, "rex_mm", 1)

        @blk.vector
        def _(v):
            for pt in range(2):
                ctx.waitv(v, "rex_mm", 2 * pt + 2)
                ins = v.tensor_copy(out=rex[:, 1024*pt:1024*(pt+1)], in_=pW[:, :])
                ctx.inc(ins, "rex_ev", 1)

        @blk.vector
        def _(v):
            ctx.wait(v, "rex_ev")
            for i in range(4):
                ins = v.tensor_mul(out=oT[:, 512*i:512*(i+1)], in0=numT[:, 512*i:512*(i+1)],
                                   in1=rex[:, 512*i:512*(i+1)])
            ctx.inc(ins, "oT_done", 1)

        esB.close()
        wgslab = ctx.sb("wgslab", [128, 256], BF16)
        x2d = [ctx.sb(f"x2d{i}", [128, QL], F32) for i in range(2)]
        xn4 = ctx.sb("xn4", [128, 16 * QL], BF16)
        sqbuf = [ctx.sb(f"sqb{i}", [128, QL], BF16) for i in range(2)]
        nsc = [ctx.sb(f"nsc{i}", [128, QL], F32) for i in range(2)]
        mbS = ctx.sb("mbS", [128, QL], F32)
        ibS = ctx.sb("ibS", [128, QL], F32)
        m_sb = nsc[0][0:1, :]
        msq_sb = nsc[1][0:1, :]
        t_sb = mbS[0:1, :]

        @blk.sync
        def _(sy):
            for hg in range(8):
                for d in range(16):
                    i = hg * 16 + d
                    if i == 0:
                        ctx.waitv(sy, "oT_done", 1)
                    if i >= 1:
                        ctx.waitv(sy, "gate_mm", i)
                    ctx.inc(sy.dma_start(out=wgslab[:, :], in_=wg_i[128*d:128*(d+1), 256*hg:256*(hg+1)]), "wgl", 16)

        for hg in range(8):
            @blk.tensor
            def _(t, hg=hg):
                for d in range(16):
                    i = hg * 16 + d
                    ctx.waitv(t, "wgl", (i + 1) * 16)
                    if d == 0:
                        ctx.wait(t, "gate_ev")
                    for j in range(2):
                        for ch in range(2):
                            ins = t.matmul(psch[j][:, 512*ch:512*ch+512],
                                           wgslab[:, 128*j:128*(j+1)],
                                           xn3[:, 1024*d + 512*ch:1024*d + 512*(ch+1)],
                                           start=(d == 0), stop=(d == 15))
                    ctx.inc(ins, "gate_mm", 1)

            @blk.scalar
            def _(sc, hg=hg):
                ctx.wait(sc, "gate_mm")
                for j in range(2):
                    for ch in range(2):
                        col = (2*hg + j) * QL + 512 * ch
                        ins = sc.activation(out=gateT[:, col:col+512], in_=psch[j][:, 512*ch:512*ch+512],
                                            func=AF.Sigmoid)
                ctx.inc(ins, "gate_ev", 1)

        @blk.gpsimd
        def _(g):
            ctx.waitv(g, "oT_done", 1)
            for dt in range(16):
                if dt >= 2:
                    ctx.waitv(g, "resid", dt - 1)
                ctx.inc(g.dma_start(out=x2d[dt % 2][:, :], in_=xT2_i[128*dt:128*(dt+1), :]), f"x2l{dt % 2}", 16)

        for ht in range(16):
            @blk.tensor
            def _(t, ht=ht):
                if ht == 0:
                    ctx.wait(t, "oT_done")
                    ctx.wait(t, "gate_ev")
                ctx.wait(t, "wo_ev")
                for qc in range(2):
                    for vt in range(2):
                        ins = t.matmul(pW[:, 512*qc:512*(qc+1)], wo1s[:, D*vt + 128*ht:D*vt + 128*(ht+1)],
                                       oT[:, 512*(2*vt+qc):512*(2*vt+qc)+512],
                                       start=(vt == 0), stop=(vt == 1))
                ctx.inc(ins, "wo_mm", 1)

            @blk.vector
            def _(v, ht=ht):
                ctx.wait(v, "wo_mm")
                ctx.waitv(v, f"x2l{ht % 2}", (ht // 2 + 1) * 16)
                ctx.inc(v.tensor_scalar(out=nsc[ht % 2][:, :], in0=pW[:, :], scalar1=biasS[:, ht:ht+1],
                                        scalar2=None, op0=ALU.add), "wo_ev", 1)
                ctx.wait(v, "wo_ev")
                ctx.inc(v.tensor_mul(out=nsc[ht % 2][:, :], in0=nsc[ht % 2][:, :],
                                     in1=gateT[:, QL*ht:QL*(ht+1)]), "vv", 1)
                ctx.wait(v, "vv")
                ins = v.tensor_add(out=x3[:, QL*ht:QL*(ht+1)], in0=nsc[ht % 2][:, :], in1=x2d[ht % 2][:, :])
                ctx.inc(ins, "resid", 1)

        @blk.scalar
        def _(sc):
            ctx.wait(sc, "resid")
            ctx.inc(sc.dma_start(out=xT3_o.rearrange("(t p) q -> p t q", p=128),
                                 in_=x3[:].rearrange("p (t q) -> p t q", q=QL)), "x3_st", 16)

        @blk.vector
        def _(v):
            ctx.wait(v, "resid")
            ctx.inc(v.memset(nsc[0][:, 0:4], 0.0), "ln4_start", 1)

        psum4 = [psch[0][:, 0:512], psch[0][:, 512:1024], psch[1][:, 0:512], psch[1][:, 512:1024]]
        emit_ln_featmajor(ctx, blk, x3, xn4, psum4, sqbuf, nsc, mbS, ibS,
                          dict(m=m_sb, msq=msq_sb, t=t_sb),
                          dict(ones_col_f=ones_col_f, ones_col_b=onesb, ones_row_f=onesrow),
                          "ln4")

        @blk.gpsimd
        def _(g):
            ctx.wait(g, "ln4_done")
            ctx.inc(g.dma_start(out=xn4_o.rearrange("(t p) q -> p t q", p=128),
                                in_=xn4[:].rearrange("p (t q) -> p t q", q=QL)), "xn4_st", 16)
            ctx.wait(g, "xn4_st")
            ctx.wait(g, "x3_st")
    return


def le_slot_schedule(c):
    slots = []
    for b in range(15 - c):
        slots.append((b, 1))
    for b in range(c):
        slots.append((b, 0))
    assert len(slots) == 15
    return slots

def le_tables(c):
    slots = le_slot_schedule(c)
    idxs = np.zeros((128, 15 * 20), np.int32)
    mtab = np.zeros((128, 15 * 2), np.float32)
    for t, (b, qc) in enumerate(slots):
        r = b if b <= 7 else 15 - b
        hf = 0 if b <= 7 else 1
        for j in range(8):
            idxs[:, 20*t + j] = 2048*r + 1024*hf + 128*j + np.arange(128)
        for j in range(8):
            idxs[:, 20*t + 8 + j] = 1024*qc + 128*j + np.arange(128)
        for j in range(4):
            idxs[:, 20*t + 16 + j] = 1024*r + 512*hf + 128*j + np.arange(128)
        mtab[:, 2*t + 0] = 1.0 if qc == 0 else 0.0
        mtab[:, 2*t + 1] = 1.0 if qc == 1 else 0.0
    return idxs, mtab

def tri_rep():
    tri = (np.arange(512)[None, :] >= np.arange(512)[:, None]).astype(BF)
    return np.ascontiguousarray(np.concatenate([tri, tri, tri, tri], 1))

def e_const():
    E = np.zeros((32, 256), np.float32)
    for p in range(32):
        j, g = p // 8, p % 8
        h = 4 * g + j
        E[p, 8*h:8*h+8] = 1.0
    return E

def le_inmaps(ld_results, lc_results, prep, inputs):
    pkall = np.ascontiguousarray(np.concatenate([r["pkext_o"].view(BF) for r in ld_results], 0))
    pvall = np.ascontiguousarray(np.concatenate([r["pvext_o"].view(BF) for r in ld_results], 0))
    tri = tri_rep()
    ec = e_const()
    ve0 = np.ascontiguousarray(inputs["L1_attn_v_emb0"].reshape(2, 128).T.astype(BF))
    maps = []
    for c in range(NC):
        idxs, mtab = le_tables(c)
        maps.append({
            "pqext_i": ld_results[c]["pqext_o"].view(BF),
            "pkext_own": ld_results[c]["pkext_o"].view(BF),
            "pvext_own": ld_results[c]["pvext_o"].view(BF),
            "pkext_all": pkall, "pvext_all": pvall,
            "idxs_i": idxs, "mtab_i": mtab, "trirep_i": tri, "econst_i": ec,
            "wo1s_i": prep["wo1_scaled"], "wo1p_i": prep["L1_attn_Wo_bf"],
            "ve0_i": ve0, "wg_i": prep["L1_attn_Wg_bf"],
            "xn3_i": lc_results[c]["xn3_o"].view(BF),
            "xT2_i": lc_results[c]["xT2_o"],
        })
    return maps

def emit_lf(ctx, io):
    xn4_i = io["xn4_i"]
    xT3_i = io["xT3_i"]
    wqg = io["wqg"]
    wqu = io["wqu"]
    wqd = io["wqd"]
    identf = io["identf"]

    xrows_o = io["xrows_o"]

    NHT = HID1 // 128
    GS = 4
    NG = NHT // GS

    if True:
        acc = ctx.sb("acc", [128, 16 * QL], F32)
        hgrp = ctx.sb("hgrp", [128, GS * QL], BF16)
        sg = ctx.sb("sg", [128, QL], BF16)
        sid = ctx.sb("sid", [128, 128], F32)
        xn4 = ctx.sb("xn4", [128, 16 * QL], BF16)
        gslab = ctx.sb("gslab", [128, 16 * 512], BF16)
        uslab = ctx.sb("uslab", [128, 16 * 512], BF16)
        pslab = ctx.sb("pslab", [128, GS * D], BF16)
        x3d = [ctx.sb(f"x3d{i}", [128, QL], F32) for i in range(2)]
        xstage = ctx.sb("xstage", [128, D], F32)

        pG = ctx.ps("pG", [128, QL], F32)
        pU = ctx.ps("pU", [128, QL], F32)
        pP = ctx.ps("pP", [128, QL], F32)
        blk = ctx.block if ctx.block is not None else ctx.start_block()

        @blk.gpsimd
        def _(g):
            ctx.inc(g.dma_start(out=xn4[:].rearrange("p (t q) -> p t q", q=QL),
                                in_=xn4_i.rearrange("(t p) q -> p t q", p=128)), "xn4_ld", 16)
            ctx.inc(g.dma_start(out=sid[:, :], in_=identf[:, :]), "xn4_ld", 16)

        @blk.sync
        def _(sy):
            ctx.agwait(sy, "agw", 6)
            for g in range(NG):
                if g >= 1:
                    ctx.waitv(sy, "up_mm", g * GS)
                for d in range(16):
                    ctx.inc(sy.dma_start(out=gslab[:, 512*d:512*(d+1)],
                                         in_=wqg[512*g:512*(g+1), 128*d:128*(d+1)], transpose=True), "gsl", 16)
                    ctx.inc(sy.dma_start(out=uslab[:, 512*d:512*(d+1)],
                                         in_=wqu[512*g:512*(g+1), 128*d:128*(d+1)], transpose=True), "usl", 16)
                if g >= 1:
                    ctx.waitv(sy, "proj_mm", g * 32)
                for htj in range(GS):
                    ctx.inc(sy.dma_start(out=pslab[:, D*htj:D*(htj+1)],
                                         in_=wqd[:, 512*g + 128*htj:512*g + 128*(htj+1)], transpose=True), "psl", 16)

        for g in range(NG):
            for htj in range(GS):
                ht = GS * g + htj

                @blk.tensor
                def _(t, g=g, htj=htj, ht=ht):
                    if ht == 0:
                        ctx.wait(t, "xn4_ld")
                    if htj == 0:
                        ctx.waitv(t, "gsl", (g + 1) * 16 * 16)
                        ctx.waitv(t, "usl", (g + 1) * 16 * 16)
                    ctx.wait(t, "silu")
                    for d in range(16):
                        for ch in range(2):
                            ins = t.matmul(pG[:, 512*ch:512*(ch+1)],
                                           gslab[:, 512*d + 128*htj:512*d + 128*(htj+1)],
                                           xn4[:, 1024*d + 512*ch:1024*d + 512*(ch+1)],
                                           start=(d == 0), stop=(d == 15))
                    ctx.inc(ins, "gate_mm", 1)
                    ctx.wait(t, "hmul")
                    for d in range(16):
                        for ch in range(2):
                            ins = t.matmul(pU[:, 512*ch:512*(ch+1)],
                                           uslab[:, 512*d + 128*htj:512*d + 128*(htj+1)],
                                           xn4[:, 1024*d + 512*ch:1024*d + 512*(ch+1)],
                                           start=(d == 0), stop=(d == 15))
                    ctx.inc(ins, "up_mm", 1)

                @blk.scalar
                def _(sc):
                    ctx.wait(sc, "gate_mm")
                    ctx.wait(sc, "hmul")
                    ctx.inc(sc.activation(out=sg[:, :], in_=pG[:, :], func=AF.Silu), "silu", 1)

                @blk.vector
                def _(v, htj=htj):
                    ctx.wait(v, "up_mm")
                    ctx.wait(v, "silu")
                    ctx.inc(v.tensor_mul(out=hgrp[:, QL*htj:QL*(htj+1)], in0=pU[:, :], in1=sg[:, :]), "hmul", 1)

            @blk.tensor
            def _(t, g=g):
                ctx.waitv(t, "psl", (g + 1) * GS * 16)
                ctx.wait(t, "hmul")
                for dt in range(16):
                    for ch in range(2):
                        i = g * 32 + dt * 2 + ch
                        if i >= 2:
                            ctx.waitv(t, "acc_add", i - 1)
                        for htj in range(GS):
                            ins = t.matmul(pP[:, 512*(i % 2):512*(i % 2 + 1)],
                                           pslab[:, D*htj + 128*dt:D*htj + 128*(dt+1)],
                                           hgrp[:, QL*htj + 512*ch:QL*htj + 512*(ch+1)],
                                           start=(htj == 0), stop=(htj == GS - 1))
                        ctx.inc(ins, "proj_mm", 1)

            @blk.vector
            def _(v, g=g):
                for dt in range(16):
                    for ch in range(2):
                        i = g * 32 + dt * 2 + ch
                        ctx.waitv(v, "proj_mm", i + 1)
                        if g == 0:
                            ins = v.tensor_copy(out=acc[:, QL*dt + 512*ch:QL*dt + 512*(ch+1)],
                                                in_=pP[:, 512*(i % 2):512*(i % 2 + 1)])
                        else:
                            ins = v.tensor_add(out=acc[:, QL*dt + 512*ch:QL*dt + 512*(ch+1)],
                                               in0=acc[:, QL*dt + 512*ch:QL*dt + 512*(ch+1)],
                                               in1=pP[:, 512*(i % 2):512*(i % 2 + 1)])
                        ctx.inc(ins, "acc_add", 1)

        @blk.gpsimd
        def _(g):
            ctx.wait(g, "xn4_ld")
            for dt in range(16):
                if dt >= 2:
                    ctx.waitv(g, "resid", dt - 1)
                ctx.inc(g.dma_start(out=x3d[dt % 2][:, :], in_=xT3_i[128*dt:128*(dt+1), :]), f"x3l{dt % 2}", 16)

        for dt in range(16):
            @blk.vector
            def _(v, dt=dt):
                if dt == 0:
                    ctx.wait(v, "acc_add")
                ctx.waitv(v, f"x3l{dt % 2}", (dt // 2 + 1) * 16)
                ctx.inc(v.tensor_add(out=acc[:, QL*dt:QL*(dt+1)], in0=acc[:, QL*dt:QL*(dt+1)],
                                     in1=x3d[dt % 2][:, :]), "resid", 1)

        for s in range(8):
            for b in range(2):
                @blk.tensor
                def _(t, s=s, b=b):
                    if s == 0 and b == 0:
                        ctx.wait(t, "resid")
                    ctx.wait(t, "xtr_ev")
                    for j in range(8):
                        dtl = b * 8 + j
                        ins = t.transpose(pG[:, 128*j:128*(j+1)], acc[:, QL*dtl + 128*s:QL*dtl + 128*(s+1)], sid[:, :])
                    ctx.inc(ins, "xtr_mm", 1)

                @blk.vector
                def _(v, s=s, b=b):
                    ctx.wait(v, "xtr_mm")
                    if b == 0:
                        ctx.waitv(v, "xr_st", max(0, s) * 16)
                    ins = v.tensor_copy(out=xstage[:, 1024*b:1024*(b+1)], in_=pG[:, :])
                    ctx.inc(ins, "xtr_ev", 1)

            @blk.scalar
            def _(sc, s=s):
                ctx.wait(sc, "xtr_ev")
                ctx.inc(sc.dma_start(out=xrows_o[128*s:128*(s+1), :], in_=xstage[:, :]), "xr_st", 16)

        @blk.gpsimd
        def _(g):
            ctx.wait(g, "xr_st")
    return


def lf_inmaps(ler, prep):
    ident = np.eye(128, dtype=np.float32)
    return [{
        "xn4_i": ler[c]["xn4_o"].view(BF),
        "xT3_i": ler[c]["xT3_o"],
        "wqg": prep["wqg_bf"], "wqu": prep["wqu_bf"], "wqd": prep["wqd_bf"],
        "identf": ident,
    } for c in range(NC)]


def _spec(CM=26112):
    return {
      "pre": dict(
        ins=[("castflat", [128, CM], F32), ("wgu_sl", [2048, 2048], F32), ("wproj_sl", [256, 8192], F32),
             ("wo1_sl", [32, 2048], F32), ("ve0_sl", [32, 1], F32), ("ve1_sl", [32, 1], F32),
             ("wqg_sl", [704, 2048], I32), ("sg_sl", [704, 1], F32), ("wqu_sl", [704, 2048], I32),
             ("su_sl", [704, 1], F32), ("wqd_sl", [256, 5632], I32), ("sd_sl", [256, 1], F32)],
        outs=[("castflat_o", [128, CM], BF16), ("wgu_o", [2048, 2048], BF16), ("wproj_o", [256, 8192], BF16),
              ("wo1_o", [32, 2048], BF16), ("wqg_o", [704, 2048], BF16), ("wqu_o", [704, 2048], BF16),
              ("wqd_o", [256, 5632], BF16)]),
      "la": dict(
        ins=[("xrows", [QL, D], F32), ("wqbf", [QSZ, D], BF16), ("wkbf", [QSZ, D], BF16), ("wvbf", [D, D], BF16),
             ("cosrep", [128, QL], F32), ("sinrep", [128, QL], F32), ("identf", [128, 128], F32),
             ("identb", [128, 128], BF16), ("permf", [128, 128], F32)],
        outs=[("qT_o", [QSZ, QL], BF16), ("kT_o", [QSZ, QL], BF16), ("vnat_o", [QL, D], BF16), ("xT_o", [D, QL], F32)]),
      "lb": dict(
        ins=[("qT_i", [QSZ, QL], BF16), ("kTfull", [NC * QSZ, QL], BF16), ("vfull", [S, D], BF16),
             ("xT_i", [D, QL], F32), ("wobf", [D, D], BF16)],
        outs=[("xT1_o", [D, QL], F32), ("xn2_o", [D, QL], BF16)]),
      "lc": dict(
        ins=[("xn2_i", [D, QL], BF16), ("xT1_i", [D, QL], F32), ("wgu", [2 * HID0, D], BF16), ("wproj", [D, HID0], BF16)],
        outs=[("xT2_o", [D, QL], F32), ("xn3_o", [D, QL], BF16)]),
      "ld": dict(
        ins=[("xn3_i", [D, QL], BF16), ("wq1", [D, 256], BF16), ("wk1", [D, 256], BF16), ("wv1", [D, 256], BF16),
             ("selc", [128, 16], BF16)],
        outs=[("pqext_o", [2 * QL, 512], BF16), ("pkext_o", [2 * QL, 512], BF16), ("pvext_o", [QL, 288], BF16)]),
      "le": dict(
        ins=[("pqext_i", [2 * QL, 512], BF16), ("pkext_own", [2 * QL, 512], BF16), ("pvext_own", [QL, 288], BF16),
             ("pkext_all", [NC * 2 * QL, 512], BF16), ("pvext_all", [NC * QL, 288], BF16),
             ("idxs_i", [128, 300], I32), ("mtab_i", [128, 30], F32), ("trirep_i", [512, 2048], BF16),
             ("econst_i", [32, 256], F32), ("wo1s_i", [256, D], BF16), ("wo1p_i", [256, D], BF16),
             ("ve0_i", [128, 2], BF16), ("wg_i", [D, D], BF16), ("xn3_i", [D, QL], BF16), ("xT2_i", [D, QL], F32)],
        outs=[("xT3_o", [D, QL], F32), ("xn4_o", [D, QL], BF16)]),
      "lf": dict(
        ins=[("xn4_i", [D, QL], BF16), ("xT3_i", [D, QL], F32), ("wqg", [HID1, D], BF16), ("wqu", [HID1, D], BF16),
             ("wqd", [D, HID1], BF16), ("identf", [128, 128], F32)],
        outs=[("xrows_o", [QL, D], F32)]),
    }



def _build_standalone(name, CM=None):
    ctx = Ctx(name)
    spec = _spec(CM or 26112)[name]
    io = {}
    for nm, shape, dt in spec["ins"]:
        io[nm] = ctx.nc.dram_tensor(nm, list(shape), dt, kind="ExternalInput").ap()
    for nm, shape, dt in spec["outs"]:
        io[nm] = ctx.nc.dram_tensor(nm, list(shape), dt, kind="ExternalOutput").ap()
    if name == "pre":
        emit_pre(ctx, io, CM)
    else:
        {"la": emit_la, "lb": emit_lb, "lc": emit_lc, "ld": emit_ld,
         "le": emit_le, "lf": emit_lf}[name](ctx, io)
    return ctx

def build_pre(CM):
    return _build_standalone("pre", CM)
def build_la():
    return _build_standalone("la")
def build_lb():
    return _build_standalone("lb")
def build_lc():
    return _build_standalone("lc")
def build_ld():
    return _build_standalone("ld")
def build_le():
    return _build_standalone("le")
def build_lf():
    return _build_standalone("lf")


CAST_SHAPES = {"L0_attn_Wq": (3072, 2048), "L0_attn_Wk": (3072, 2048),
               "L0_attn_Wv": (2048, 2048), "L0_attn_Wo": (2048, 2048),
               "L1_attn_Wq": (2048, 256), "L1_attn_Wk": (2048, 256),
               "L1_attn_Wv": (2048, 256), "L1_attn_Wg": (2048, 2048),
               "L1_attn_Wo": (256, 2048)}

def build_fused(CM):
    ctx = Ctx("fused")
    nc = ctx.nc
    din = lambda n, s, d: nc.dram_tensor(n, list(s), d, kind="ExternalInput").ap()
    dint = lambda n, s, d: nc.dram_tensor(n, list(s), d, kind="Internal").ap()

    I = {}
    for nm, shape, dt in _spec(CM)["pre"]["ins"]:
        I[nm] = din(nm, shape, dt)
    for nm, shape, dt in [("xrows", [QL, D], F32), ("cosrep", [128, QL], F32), ("sinrep", [128, QL], F32),
                          ("identf", [128, 128], F32), ("identb", [128, 128], BF16), ("permf", [128, 128], F32),
                          ("selc", [128, 16], BF16), ("idxs_i", [128, 300], I32), ("mtab_i", [128, 30], F32),
                          ("trirep_i", [512, 2048], BF16), ("econst_i", [32, 256], F32), ("ve0_i", [128, 2], BF16)]:
        I[nm] = din(nm, shape, dt)
    xrows_o = nc.dram_tensor("xrows_o", [QL, D], F32, kind="ExternalOutput").ap()

    ag = {}
    for nm, lsh, fsh, dt in [
        ("cast", [128, CM], [NC * 128, CM], BF16),
        ("wguw", [2048, 2048], [2 * HID0, 2048], BF16),
        ("wprojw", [256, 8192], [2048, 8192], BF16),
        ("wo1w", [32, 2048], [256, 2048], BF16),
        ("wqgw", [704, 2048], [HID1, 2048], BF16),
        ("wquw", [704, 2048], [HID1, 2048], BF16),
        ("wqdw", [256, 5632], [2048, 5632], BF16),
        ("kT", [QSZ, QL], [NC * QSZ, QL], BF16),
        ("vn", [QL, D], [S, D], BF16),
        ("pk", [2 * QL, 512], [NC * 2 * QL, 512], BF16),
        ("pv", [QL, 288], [NC * QL, 288], BF16),
    ]:
        ag[nm] = (dint(nm + "_l", lsh, dt),
                  nc.dram_tensor(nm + "_f", list(fsh), dt, kind="Internal",
                                 addr_space="Shared" if SHARED_AG else "Local").ap())

    loc = {}
    for nm, shape, dt in [("qT", [QSZ, QL], BF16), ("xT", [D, QL], F32), ("xn2", [D, QL], BF16),
                          ("xT1", [D, QL], F32), ("xT2", [D, QL], F32), ("xn3", [D, QL], BF16),
                          ("xT3", [D, QL], F32), ("xn4", [D, QL], BF16), ("pq", [2 * QL, 512], BF16)]:
        loc[nm] = dint(nm + "_h", shape, dt)

    flat = ag["cast"][1].rearrange("a b -> (a b)")
    wview = {}
    off = 0
    for k in CAST_KEYS:
        r, ccol = CAST_SHAPES[k]
        n = r * ccol
        wview[k] = flat[off:off + n].rearrange("(r c) -> r c", c=ccol)
        off += n

    blk = ctx.start_block()
    ctx.ags_on = True

    def _issue(g, pair, semname):
        ain, aout = pair
        ctx.inc(g.collective_compute(
            "AllGather", mybir.AluOpType.bypass,
            replica_groups=[list(range(NC))],
            ins=[ain.opt()], outs=[aout.opt()]), semname, 1)

    ctx.ag_cast_issue = lambda g: _issue(g, ag["cast"], "aga")

    def prologue():
        pass

    def issue_ags(names, semname):
        @blk.gpsimd
        def _(g):
            for nm in names:
                _issue(g, ag[nm], semname)

    ctx.set_phase("pre")
    prologue()
    io = dict(I)
    io.update({"castflat_o": ag["cast"][0], "wgu_o": ag["wguw"][0], "wproj_o": ag["wprojw"][0],
               "wo1_o": ag["wo1w"][0], "wqg_o": ag["wqgw"][0], "wqu_o": ag["wquw"][0], "wqd_o": ag["wqdw"][0]})
    emit_pre(ctx, io, CM)
    issue_ags(["wguw", "wprojw", "wo1w", "wqgw", "wquw", "wqdw"], "agw")

    ctx.set_phase("la")
    prologue()
    io = {"xrows": I["xrows"], "wqbf": wview["L0_attn_Wq"], "wkbf": wview["L0_attn_Wk"],
          "wvbf": wview["L0_attn_Wv"], "cosrep": I["cosrep"], "sinrep": I["sinrep"],
          "identf": I["identf"], "identb": I["identb"], "permf": I["permf"],
          "qT_o": loc["qT"], "kT_o": ag["kT"][0], "vnat_o": ag["vn"][0], "xT_o": loc["xT"]}
    emit_la(ctx, io)
    issue_ags(["kT", "vn"], "agk")

    ctx.set_phase("lb")
    prologue()
    io = {"qT_i": loc["qT"], "kTfull": ag["kT"][1], "vfull": ag["vn"][1], "xT_i": loc["xT"],
          "wobf": wview["L0_attn_Wo"], "xT1_o": loc["xT1"], "xn2_o": loc["xn2"]}
    emit_lb(ctx, io)

    ctx.set_phase("lc")
    prologue()
    io = {"xn2_i": loc["xn2"], "xT1_i": loc["xT1"], "wgu": ag["wguw"][1], "wproj": ag["wprojw"][1],
          "xT2_o": loc["xT2"], "xn3_o": loc["xn3"]}
    emit_lc(ctx, io)

    ctx.set_phase("ld")
    prologue()
    io = {"xn3_i": loc["xn3"], "wq1": wview["L1_attn_Wq"], "wk1": wview["L1_attn_Wk"],
          "wv1": wview["L1_attn_Wv"], "selc": I["selc"],
          "pqext_o": loc["pq"], "pkext_o": ag["pk"][0], "pvext_o": ag["pv"][0]}
    emit_ld(ctx, io)
    issue_ags(["pk", "pv"], "agk")

    ctx.set_phase("le")
    prologue()
    io = {"pqext_i": loc["pq"], "pkext_own": ag["pk"][0], "pvext_own": ag["pv"][0],
          "pkext_all": ag["pk"][1], "pvext_all": ag["pv"][1],
          "idxs_i": I["idxs_i"], "mtab_i": I["mtab_i"], "trirep_i": I["trirep_i"], "econst_i": I["econst_i"],
          "wo1s_i": ag["wo1w"][1], "wo1p_i": wview["L1_attn_Wo"], "ve0_i": I["ve0_i"],
          "wg_i": wview["L1_attn_Wg"], "xn3_i": loc["xn3"], "xT2_i": loc["xT2"],
          "xT3_o": loc["xT3"], "xn4_o": loc["xn4"]}
    emit_le(ctx, io)

    ctx.set_phase("lf")
    prologue()
    io = {"xn4_i": loc["xn4"], "xT3_i": loc["xT3"], "wqg": ag["wqgw"][1], "wqu": ag["wquw"][1],
          "wqd": ag["wqdw"][1], "identf": I["identf"], "xrows_o": xrows_o}
    emit_lf(ctx, io)
    ctx.end_phases()
    return ctx


def fused_inmaps(inputs):
    premaps, CM = pre_inmaps(inputs)
    x = np.asarray(inputs["L0_attn_hidden_states"]).reshape(S, D)
    ident = np.eye(128, dtype=np.float32)
    identb = np.eye(128, dtype=np.float32).astype(BF)
    perm = perm_matrix()
    sc = sel_const()
    tri = tri_rep()
    ec = e_const()
    ve0 = np.ascontiguousarray(np.asarray(inputs["L1_attn_v_emb0"]).reshape(2, 128).T.astype(BF))
    maps = []
    for c in range(NC):
        lo, hi = blocks_of(c)
        xr = np.ascontiguousarray(np.concatenate([x[lo*BLK:(lo+1)*BLK], x[hi*BLK:(hi+1)*BLK]], 0))
        crep, srep = core_cos_sin(c)
        idxs, mtab = le_tables(c)
        m = dict(premaps[c])
        m.update({"xrows": xr, "cosrep": crep, "sinrep": srep, "identf": ident, "identb": identb,
                  "permf": perm, "selc": sc, "idxs_i": idxs, "mtab_i": mtab, "trirep_i": tri,
                  "econst_i": ec, "ve0_i": ve0})
        maps.append(m)
    return maps, CM


def kernel(**inputs):
    inputs = {k: np.asarray(v) for k, v in inputs.items()}
    maps, CM = fused_inmaps(inputs)
    ctx = build_fused(CM)
    res = run_bass_kernel_spmd(ctx.nc, maps, core_ids=list(range(NC)))
    out = np.zeros((S, D), np.float32)
    for c in range(NC):
        lo, hi = blocks_of(c)
        xr = res.results[c]["xrows_o"]
        out[lo*BLK:(lo+1)*BLK] = xr[0:BLK]
        out[hi*BLK:(hi+1)*BLK] = xr[BLK:QL]
    return out.reshape(1, S, D)
